# version 04
# baseline (speedup 1.0000x reference)
import contextlib
import numpy as np
import concourse.bass as bass
import concourse.mybir as mybir
from concourse.bass_utils import run_bass_kernel_spmd

F32 = mybir.dt.float32
BF16 = mybir.dt.bfloat16
I16 = mybir.dt.int16
I32 = mybir.dt.int32
AF = mybir.ActivationFunctionType
OP = mybir.AluOpType
AX = mybir.AxisListType

DEBUG_SCRATCH = False
BUILD_PHASES = "all"
NCORES = 8
T = 2048
D = 2048
FF = 8192
NT = T // 128
KC = D // 128
FC = FF // 128
C2 = 196608.0
MASK_EXP = 0xFF800000 - (1 << 32)
K_RS = float(0x7F000000)
CLIP_HI = 1.984375
EPS = 1e-5
RSQD = float(1.0 / np.float32(np.sqrt(np.float32(2048.0))))
f32 = np.float32


class Sem:
    def __init__(self, nc, es, name):
        self.h = es.enter_context(nc.semaphore(name))
        self.n = 0

    def inc(self, instr, k=1):
        instr.then_inc(self.h, k)
        self.n += k
        return self.n

    def dinc(self, eng, k=1):
        eng.drain().then_inc(self.h, k)
        self.n += k
        return self.n


def quant_chain(v, X_ap, n_el, AM, PB, RS, PBB, RSB, U_ap, V_ap, Q_ap, fp32_input, npart=128):
    nb = n_el // 32
    v.drain()
    v.tensor_reduce(AM[:, 0:nb], X_ap.rearrange("p (b j) -> p b j", j=32),
                    axis=AX.X, op=OP.max, apply_absolute_value=True)
    v.drain()
    v.tensor_single_scalar(PB[:, 0:nb], AM[:, 0:nb].bitcast(I32), MASK_EXP, OP.bitwise_and)
    v.drain()
    v.tensor_scalar(RS[:, 0:nb], PB[:, 0:nb], -1.0, K_RS, OP.mult, OP.add)
    v.tensor_copy(PBB[:, 0:nb], PB[:, 0:nb].bitcast(F32))
    v.drain()
    if not fp32_input:
        v.tensor_copy(RSB[:, 0:nb], RS[:, 0:nb].bitcast(F32))
        v.drain()
    rs_b = (RS[:, 0:nb].bitcast(F32) if fp32_input else RSB[:, 0:nb])
    v.tensor_tensor(U_ap.rearrange("p (b j) -> p b j", j=32),
                    X_ap.rearrange("p (b j) -> p b j", j=32),
                    rs_b[:, :, None].broadcast_to([npart, nb, 32]), OP.mult)
    v.drain()
    v.tensor_scalar(V_ap, U_ap, C2, -C2, OP.add, OP.add)
    v.drain()
    v.tensor_scalar_min(V_ap, V_ap, CLIP_HI)
    v.drain()
    return v.tensor_tensor(Q_ap.rearrange("p (b j) -> p b j", j=32),
                           V_ap.rearrange("p (b j) -> p b j", j=32),
                           PBB[:, 0:nb][:, :, None].broadcast_to([npart, nb, 32]), OP.mult)


def _build_init(nc, S, E, P):
    ph = contextlib.ExitStack()
    sb = lambda n, sh, d: ph.enter_context(nc.sbuf_tensor(n, sh, d))
    Pidx = sb("Pidx", [128, 1], I32)
    Cidx = sb("Cidx", [128, 128], I32)
    CBl = sb("CBl", [128, 128], I32)
    TDm = sb("TDm", [128, 128], F32)
    GEm = sb("GEm", [128, 128], F32)
    with ph:
        with nc.Block() as blk:
            @blk.gpsimd
            def _(g):
                g.iota(Pidx[:, :], [[1, 1]], channel_multiplier=1)
                g.iota(Cidx[:, :], [[1, 128]], channel_multiplier=0)
                g.iota(CBl[:, :].rearrange("p (a b) -> p a b", b=16),
                       [[1, 8], [0, 16]], channel_multiplier=0)

            @blk.sync
            def _(sync):
                ldi = S("ldi")
                for dst, src in ((P.BQ, E.bq), (P.BK, E.bk), (P.BV, E.bv),
                                 (P.G2C, E.g2), (P.B2C, E.be2)):
                    ldi.inc(sync.dma_start(
                        dst[:, :], src[0:1, :].rearrange("o (c p) -> p (o c)", p=128),
                        allow_slow_non_contiguous=True), 16)
                sync.wait_ge(ldi.h, ldi.n)

        with nc.Block() as blk:
            @blk.vector
            def _(v):
                v.tensor_tensor(P.IdF[:, :], Cidx[:, :],
                                Pidx[:, :].broadcast_to([128, 128]), OP.is_equal)
                v.drain()
                v.tensor_copy(P.IdB[:, :], P.IdF[:, :])
                v.tensor_scalar(TDm[:, :], CBl[:, :], -16.0, None, OP.mult)
                v.drain()
                v.tensor_tensor(TDm[:, :], Pidx[:, :].broadcast_to([128, 128]), TDm[:, :], OP.add)
                v.drain()
                v.tensor_single_scalar(GEm[:, :], TDm[:, :], 0.0, OP.is_ge)
                v.drain()
                v.tensor_single_scalar(TDm[:, :], TDm[:, :], 15.0, OP.is_le)
                v.drain()
                v.tensor_tensor(GEm[:, :], GEm[:, :], TDm[:, :], OP.logical_and)
                v.drain()
                v.tensor_scalar(P.MASK[:, :], GEm[:, :], 1e30, -1e30, OP.mult, OP.add)


def _build_p0(nc, S, E, P, QHT):
    ph = contextlib.ExitStack()
    sb = lambda n, sh, d: ph.enter_context(nc.sbuf_tensor(n, sh, d))
    ps = lambda n, sh, d: ph.enter_context(nc.psum_tensor(n, sh, d))
    XIN = sb("XIN", [128, 2, D], F32)
    XN = sb("XN", [128, D], F32)
    T2 = sb("T2", [128, D], F32)
    T3 = sb("T3", [128, D], F32)
    H3T = sb("H3T", [128, D], F32)
    XFN = sb("XFN", [128, D], I16)
    XFB = sb("XFB", [128, 2, D], BF16)
    LNW = sb("LNW", [128, D], F32)
    LNB = sb("LNB", [128, D], F32)
    HQ = sb("HQ", [128, D], I16)
    UQ = sb("UQ", [128, D], I16)
    VQ = sb("VQ", [128, D], I16)
    QHI = sb("QHI", [128, D], I16)
    AM0 = sb("AM0", [128, 64], F32)
    PB0 = sb("PB0", [128, 64], I32)
    RS0 = sb("RS0", [128, 64], I32)
    PBB0 = sb("PBB0", [128, 64], BF16)
    RSB0 = sb("RSB0", [128, 64], BF16)
    STt = sb("STt", [128, 24], F32)
    MVt = sb("MVt", [128, 2], F32)
    VARt = sb("VARt", [128, 1], F32)
    SQt = sb("SQt", [128, 1], F32)
    R0t = sb("R0t", [128, 1], F32)
    A1t = sb("A1t", [128, 1], F32)
    RRt = sb("RRt", [128, 3], F32)
    NBt = sb("NBt", [128, 3], F32)
    XTB = sb("XTB", [128, 2, KC, 128], BF16)
    TRP = ps("TRP", [128, KC, 128], F32)
    TRQ = ps("TRQ", [128, KC, 128], BF16)

    def stats(v, src_ap):
        v.drain()
        for c in range(4):
            v.bn_stats(STt[:, 6 * c:6 * c + 6], src_ap[:, 512 * c:512 * (c + 1)])
        v.drain()
        v.bn_aggr(MVt[:, :].rearrange("p (a t) -> p a t", t=2),
                  STt[:, :].rearrange("p (g t) -> p g t", t=3))
        v.drain()
        return v.tensor_single_scalar(VARt[:, :], MVt[:, 1:2], EPS, OP.add)

    def rstd(v, k):
        v.reciprocal(RRt[:, k:k + 1], SQt[:, :])
        v.drain()
        v.tensor_tensor(NBt[:, k:k + 1], MVt[:, 0:1], RRt[:, k:k + 1], OP.mult)
        v.drain()
        return v.tensor_single_scalar(NBt[:, k:k + 1], NBt[:, k:k + 1], -1.0, OP.mult)

    with ph, nc.Block() as blk:
        @blk.sync
        def _(sync):
            ldw = S("p0_lnwb")
            ldw.inc(sync.dma_start(LNW[:, :], E.lnw[0:1, :].partition_broadcast(128)), 16)
            ldw.inc(sync.dma_start(LNB[:, :], E.lnb[0:1, :].partition_broadcast(128)), 16)
            for tt in range(NT):
                if tt >= 2:
                    sync.wait_ge(S("p0_xn").h, tt - 1)
                S("p0_ldx").inc(sync.dma_start(
                    XIN[:, tt % 2, :], E.x[tt * 128:(tt + 1) * 128, :]), 16)

        @blk.vector
        def _(v):
            for tt in range(NT):
                b = tt % 2
                v.wait_ge(S("p0_ldx").h, 16 * (tt + 1))
                if tt == 0:
                    v.wait_ge(S("p0_lnwb").h, 32)
                stats(v, XIN[:, b, :])
                S("p0_var").dinc(v)
                v.wait_ge(S("p0_sqd").h, 3 * tt + 1)
                rstd(v, 0)
                S("p0_r").dinc(v)
                v.wait_ge(S("p0_xn").h, tt + 1)
                stats(v, XN[:, :])
                S("p0_var").dinc(v)
                v.wait_ge(S("p0_sqd").h, 3 * tt + 2)
                rstd(v, 1)
                if tt >= 2:
                    v.wait_ge(S("p0_t3").h, tt - 1)
                    v.wait_ge(S("p0_xfs").h, 16 * (tt - 1))
                v.drain()
                v.tensor_copy(XFN[:, :].bitcast(BF16), XN[:, :])
                S("p0_r").dinc(v)
                v.tensor_copy(XFB[:, b, :], XFN[:, :].bitcast(BF16))
                S("p0_xfb").dinc(v)
                stats(v, XFB[:, b, :])
                S("p0_var").dinc(v)
                v.wait_ge(S("p0_sqd").h, 3 * tt + 3)
                rstd(v, 2)
                S("p0_r").dinc(v)
                v.wait_ge(S("p0_tr1").h, tt + 1)
                if tt >= 2:
                    v.wait_ge(S("p0_xts").h, 16 * (tt - 1))
                for c in range(KC):
                    v.tensor_scalar(XTB[:, b, c, :], TRP[:, c, :],
                                    P.G2C[:, c:c + 1], P.B2C[:, c:c + 1], OP.mult, OP.add)
                S("p0_xtb").dinc(v)
                v.wait_ge(S("p0_t3").h, tt + 1)
                v.scalar_tensor_tensor(H3T[:, :], T3[:, :], 1.0, LNW[:, :], OP.mult, OP.mult)
                v.drain()
                v.scalar_tensor_tensor(HQ[:, :].bitcast(BF16), H3T[:, :], 1.0, LNB[:, :],
                                       OP.mult, OP.add)
                if tt >= 1:
                    v.wait_ge(S("p0_trq").h, tt)
                if DEBUG_SCRATCH and tt == 1:
                    v.wait_ge(S("p0_dbg").h, 16)
                quant_chain(v, HQ[:, :].bitcast(BF16), D, AM0, PB0, RS0, PBB0, RSB0,
                            UQ[:, :].bitcast(BF16), VQ[:, :].bitcast(BF16),
                            QHI[:, :].bitcast(BF16), fp32_input=False)
                S("p0_qh").dinc(v)

        @blk.scalar
        def _(s):
            for tt in range(NT):
                b = tt % 2
                for k in range(3):
                    s.wait_ge(S("p0_var").h, 3 * tt + k + 1)
                    s.activation(SQt[:, :], VARt[:, :], AF.Sqrt,
                                 bias=f32(0.0), scale=f32(1.0))
                    S("p0_sqd").dinc(s)
                    if k == 0:
                        s.wait_ge(S("p0_r").h, 3 * tt + 1)
                        s.activation(XN[:, :], XIN[:, b, :], AF.Identity,
                                     bias=NBt[:, 0:1], scale=RRt[:, 0:1])
                        S("p0_xn").dinc(s)
                    elif k == 1:
                        s.wait_ge(S("p0_r").h, 3 * tt + 2)
                        s.activation(T2[:, :], XN[:, :], AF.Identity,
                                     bias=NBt[:, 1:2], scale=RRt[:, 1:2])
                        S("p0_t2").dinc(s)
                    else:
                        s.wait_ge(S("p0_r").h, 3 * tt + 3)
                        s.activation(T3[:, :], XFB[:, b, :], AF.Identity,
                                     bias=NBt[:, 2:3], scale=RRt[:, 2:3])
                        S("p0_t3").dinc(s)
                s.wait_ge(S("p0_trq").h, tt + 1)
                s.activation(
                    QHT[:, :, tt * 128:(tt + 1) * 128].bitcast(BF16),
                    TRQ[:, :, :], AF.Identity, bias=f32(0.0), scale=f32(1.0))
                S("p0_qht").dinc(s)

        @blk.tensor
        def _(t):
            for tt in range(NT):
                t.wait_ge(S("p0_t2").h, tt + 1)
                if tt >= 1:
                    t.wait_ge(S("p0_xtb").h, tt)
                for c in range(KC):
                    tr = t.transpose(TRP[:, c, :], T2[:, c * 128:(c + 1) * 128], P.IdF[:, :])
                S("p0_tr1").inc(tr)
                t.wait_ge(S("p0_qh").h, tt + 1)
                if tt >= 1:
                    t.wait_ge(S("p0_qht").h, tt)
                for c in range(KC):
                    tr = t.transpose(TRQ[:, c, :],
                                     QHI[:, c * 128:(c + 1) * 128].bitcast(BF16), P.IdB[:, :])
                S("p0_trq").inc(tr)

        @blk.gpsimd
        def _(g):
            if DEBUG_SCRATCH:
                g.wait_ge(S("p0_qh").h, 1)
                S("p0_dbg").inc(g.dma_start(E.dbg_qhi[:, :], QHI[:, :].bitcast(BF16)), 16)
            for tt in range(NT):
                b = tt % 2
                g.wait_ge(S("p0_xtb").h, tt + 1)
                S("p0_xts").inc(g.dma_start(
                    bass.AP(E.xt_s, tt * 128 * KC * 128,
                            [[KC * 128, 128], [128, KC], [1, 128]]),
                    XTB[:, b, :, :]), 16)
                g.wait_ge(S("p0_xfb").h, tt + 1)
                S("p0_xfs").inc(g.dma_start(
                    E.xffn_s[tt * 128:(tt + 1) * 128, :], XFB[:, b, :]), 16)
            if DEBUG_SCRATCH:
                g.wait_ge(S("p0_qht").h, NT)
                S("p0_dbg").inc(g.dma_start(
                    E.dbg_qht[:, :, :].rearrange("p c t -> p (c t)"),
                    QHT[:, :, :].bitcast(BF16).rearrange("p c t -> p (c t)")), 16)
                g.wait_ge(S("p0_dbg").h, 32)
            g.wait_ge(S("p0_xts").h, 16 * NT)
            g.wait_ge(S("p0_xfs").h, 16 * NT)


def _build_p1(nc, S, E, P, QHT):
    NPAN = FF // 512
    ph = contextlib.ExitStack()
    sb = lambda n, sh, d: ph.enter_context(nc.sbuf_tensor(n, sh, d))
    ps = lambda n, sh, d: ph.enter_context(nc.psum_tensor(n, sh, d))
    W1P = sb("W1P", [128, 2, 2, D], F32)
    U1 = sb("U1", [128, 2, D], F32)
    V1 = sb("V1", [128, 2, D], I16)
    Q1P = sb("Q1P", [128, 2, D], I16)
    AM1 = sb("AM1", [128, 128], F32)
    PB1 = sb("PB1", [128, 128], I32)
    RS1 = sb("RS1", [128, 128], I32)
    PBB1 = sb("PBB1", [128, 128], BF16)
    RSB1 = sb("RSB1", [128, 128], BF16)
    QW1 = sb("QW1", [128, 2, KC, 512], I16)
    B1B = sb("B1B", [128, 2, 512], F32)
    YB = sb("YB", [128, 512], I16)
    GI = sb("GI", [128, 512], I16)
    UG = sb("UG", [128, 512], I16)
    VG = sb("VG", [128, 512], I16)
    QG = sb("QG", [128, 512], I16)
    AMG = sb("AMG", [128, 16], F32)
    PBG = sb("PBG", [128, 16], I32)
    RSG = sb("RSG", [128, 16], I32)
    PBBG = sb("PBBG", [128, 16], BF16)
    RSBG = sb("RSBG", [128, 16], BF16)
    QH2B = sb("QH2B", [128, 2, 4, 128], I16)
    Y1PS = ps("Y1PS", [128, 2, 512], F32)
    W1TR = ps("W1TR", [128, KC, 128], BF16)
    H2TR = ps("H2TR", [128, 4, 128], BF16)

    with ph, nc.Block() as blk:
        @blk.sync
        def _(sync):
            for pn in range(NPAN):
                for r in range(2):
                    idx = pn * 2 + r
                    if idx >= 2:
                        sync.wait_ge(S("p1_w1q").h, idx - 1)
                    S("p1_w1l").inc(sync.dma_start(
                        W1P[:, idx % 2, :, :],
                        E.w1[pn * 512 + r * 256: pn * 512 + (r + 1) * 256, :]
                        .rearrange("(rr p) k -> p rr k", p=128)), 16)
                if pn >= 2:
                    sync.wait_ge(S("p1_yb").h, (pn - 1) * NT)
                S("p1_b1l").inc(sync.dma_start(
                    B1B[:, pn % 2, :],
                    E.b1[0:1, pn * 512:(pn + 1) * 512].partition_broadcast(128)), 16)

        @blk.vector
        def _(v):
            for pn in range(NPAN):
                for r in range(2):
                    idx = pn * 2 + r
                    v.wait_ge(S("p1_w1l").h, 16 * (idx + 1))
                    if idx >= 1:
                        v.wait_ge(S("p1_w1t").h, 2 * idx)
                    quant_chain(
                        v, W1P[:, idx % 2, :, :].rearrange("p r k -> p (r k)"), 2 * D,
                        AM1, PB1, RS1, PBB1, RSB1,
                        U1[:, :, :].rearrange("p r k -> p (r k)"),
                        V1[:, :, :].rearrange("p r k -> p (r k)").bitcast(BF16),
                        Q1P[:, :, :].rearrange("p r k -> p (r k)").bitcast(BF16),
                        fp32_input=True)
                    S("p1_w1q").dinc(v)
                for tt in range(NT):
                    i = pn * NT + tt
                    v.wait_ge(S("p1_mm").h, i + 1)
                    v.wait_ge(S("p1_b1l").h, 16 * (pn + 1))
                    if i >= 1:
                        v.wait_ge(S("p1_gi").h, i)
                    if DEBUG_SCRATCH and pn == 0 and tt == 1:
                        v.wait_ge(S("p1_dbg").h, 16)
                    v.scalar_tensor_tensor(YB[:, :].bitcast(BF16), Y1PS[:, tt % 2, :],
                                           1.0, B1B[:, pn % 2, :], OP.mult, OP.add)
                    S("p1_yb").dinc(v)
                    v.wait_ge(S("p1_gi").h, i + 1)
                    if i >= 1:
                        v.wait_ge(S("p1_h2t").h, i)
                    quant_chain(v, GI[:, :].bitcast(BF16), 512,
                                AMG, PBG, RSG, PBBG, RSBG,
                                UG[:, :].bitcast(BF16), VG[:, :].bitcast(BF16),
                                QG[:, :].bitcast(BF16), fp32_input=False)
                    S("p1_qg").dinc(v)

        @blk.tensor
        def _(t):
            for pn in range(NPAN):
                for r in range(2):
                    idx = pn * 2 + r
                    t.wait_ge(S("p1_w1q").h, idx + 1)
                    for pr in range(2):
                        gidx = idx * 2 + pr
                        if gidx >= 1:
                            t.wait_ge(S("p1_w1e").h, gidx)
                        for c in range(KC):
                            tr = t.transpose(W1TR[:, c, :],
                                             Q1P[:, pr, c * 128:(c + 1) * 128].bitcast(BF16),
                                             P.IdB[:, :])
                        S("p1_w1t").inc(tr)
                for tt in range(NT):
                    i = pn * NT + tt
                    t.wait_ge(S("p1_w1e").h, 4 * (pn + 1))
                    if i >= 2:
                        t.wait_ge(S("p1_yb").h, i - 1)
                    for kc in range(KC):
                        mm = t.matmul(Y1PS[:, tt % 2, :],
                                      QHT[:, kc, tt * 128:(tt + 1) * 128].bitcast(BF16),
                                      QW1[:, pn % 2, kc, :].bitcast(BF16),
                                      start=(kc == 0), stop=(kc == KC - 1))
                    S("p1_mm").inc(mm)
                    t.wait_ge(S("p1_qg").h, i + 1)
                    if i >= 1:
                        t.wait_ge(S("p1_h2e").h, i)
                    for c in range(4):
                        tr = t.transpose(H2TR[:, c, :],
                                         QG[:, c * 128:(c + 1) * 128].bitcast(BF16), P.IdB[:, :])
                    S("p1_h2t").inc(tr)

        @blk.scalar
        def _(s):
            for pn in range(NPAN):
                for r in range(2):
                    idx = pn * 2 + r
                    for pr in range(2):
                        gidx = idx * 2 + pr
                        s.wait_ge(S("p1_w1t").h, gidx + 1)
                        if pn >= 2 and r == 0 and pr == 0:
                            s.wait_ge(S("p1_mm").h, (pn - 1) * NT)
                        s.activation(
                            QW1[:, pn % 2, :,
                                (r * 2 + pr) * 128:(r * 2 + pr + 1) * 128].bitcast(BF16),
                            W1TR[:, :, :], AF.Identity, bias=f32(0.0), scale=f32(1.0))
                        S("p1_w1e").dinc(s)
                for tt in range(NT):
                    i = pn * NT + tt
                    s.wait_ge(S("p1_yb").h, i + 1)
                    if i >= 1:
                        s.wait_ge(S("p1_qg").h, i)
                    if DEBUG_SCRATCH and pn == 0 and tt == 1:
                        s.wait_ge(S("p1_dbg").h, 32)
                    s.activation(GI[:, :].bitcast(BF16), YB[:, :].bitcast(BF16),
                                 AF.Gelu_apprx_tanh)
                    S("p1_gi").dinc(s)
                    s.wait_ge(S("p1_h2t").h, i + 1)
                    if i >= 2:
                        s.wait_ge(S("p1_h2s").h, 16 * (i - 1))
                    s.activation(QH2B[:, i % 2, :, :].bitcast(BF16),
                                 H2TR[:, :, :], AF.Identity,
                                 bias=f32(0.0), scale=f32(1.0))
                    S("p1_h2e").dinc(s)

        @blk.gpsimd
        def _(g):
            if DEBUG_SCRATCH:
                g.wait_ge(S("p1_yb").h, 1)
                S("p1_dbg").inc(g.dma_start(E.dbg_yb[0, :, :], YB[:, :].bitcast(BF16)), 16)
                g.wait_ge(S("p1_gi").h, 1)
                S("p1_dbg").inc(g.dma_start(E.dbg_gi[0, :, :], GI[:, :].bitcast(BF16)), 16)
            for pn in range(NPAN):
                for tt in range(NT):
                    i = pn * NT + tt
                    g.wait_ge(S("p1_h2e").h, i + 1)
                    S("p1_h2s").inc(g.dma_start(
                        bass.AP(E.qh2t_s, tt * 128 * FC * 128 + pn * 4 * 128,
                                [[FC * 128, 128], [128, 4], [1, 128]]),
                        QH2B[:, i % 2, :, :].bitcast(BF16)), 16)
            g.wait_ge(S("p1_h2s").h, 16 * NPAN * NT)


def _build_p2(nc, S, E, P):
    NQ = 4
    NW2 = 4
    ph = contextlib.ExitStack()
    sb = lambda n, sh, d: ph.enter_context(nc.sbuf_tensor(n, sh, d))
    ps = lambda n, sh, d: ph.enter_context(nc.psum_tensor(n, sh, d))
    W2P = sb("W2P", [128, FF], F32)
    U2 = sb("U2", [128, FF // 2], F32)
    V2 = sb("V2", [128, FF], I16)
    Q2P = sb("Q2P", [128, FF], I16)
    AM2 = sb("AM2", [128, 128], F32)
    PB2 = sb("PB2", [128, 128], I32)
    RS2 = sb("RS2", [128, 128], I32)
    PBB2 = sb("PBB2", [128, 128], BF16)
    RSB2 = sb("RSB2", [128, 128], BF16)
    QW2 = sb("QW2", [128, FC, 512], I16)
    B2B = sb("B2B", [128, 512], F32)
    QH2L = sb("QH2L", [128, 2, FC, 128], I16)
    HB2 = sb("HB2", [128, 512], I16)
    XFQ = sb("XFQ", [128, 2, 512], BF16)
    FFB = sb("FFB", [128, 2, 512], I16)
    Y2PS = ps("Y2PS", [128, 2, 512], F32)
    W2TR = ps("W2TR", [128, 16, 128], BF16)

    with ph, nc.Block() as blk:
        @blk.sync
        def _(sync):
            for q in range(NQ):
                for pw in range(NW2):
                    idx = q * NW2 + pw
                    if idx >= 1:
                        sync.wait_ge(S("p2_w2q").h, idx)
                    S("p2_w2l").inc(sync.dma_start(
                        W2P[:, :],
                        E.w2[q * 512 + pw * 128: q * 512 + (pw + 1) * 128, :]), 16)
                if q >= 1:
                    sync.wait_ge(S("p2_hb").h, q * NT)
                S("p2_b2l").inc(sync.dma_start(
                    B2B[:, :], E.b2[0:1, q * 512:(q + 1) * 512].partition_broadcast(128)), 16)
                for tt in range(NT):
                    i = q * NT + tt
                    if i >= 2:
                        sync.wait_ge(S("p2_mm").h, i - 1)
                    S("p2_sl").inc(sync.dma_start(
                        QH2L[:, tt % 2, :, :].bitcast(BF16),
                        bass.AP(E.qh2t_s, tt * 128 * FC * 128,
                                [[FC * 128, 128], [128, FC], [1, 128]])), 16)
                    if i >= 2:
                        sync.wait_ge(S("p2_ff").h, i - 1)
                    S("p2_xfl").inc(sync.dma_start(
                        XFQ[:, tt % 2, :],
                        E.xffn_s[tt * 128:(tt + 1) * 128, q * 512:(q + 1) * 512]), 16)

        @blk.vector
        def _(v):
            for q in range(NQ):
                for pw in range(NW2):
                    idx = q * NW2 + pw
                    v.wait_ge(S("p2_w2l").h, 16 * (idx + 1))
                    if idx >= 1:
                        v.wait_ge(S("p2_w2t").h, (FC // 16) * idx)
                    hw = FF // 2
                    for hh in range(2):
                        qq = quant_chain(v, W2P[:, hh * hw:(hh + 1) * hw], hw,
                                         AM2, PB2, RS2, PBB2, RSB2,
                                         U2[:, :], V2[:, hh * hw:(hh + 1) * hw].bitcast(BF16),
                                         Q2P[:, hh * hw:(hh + 1) * hw].bitcast(BF16),
                                         fp32_input=True)
                    S("p2_w2q").dinc(v)
                for tt in range(NT):
                    i = q * NT + tt
                    v.wait_ge(S("p2_mm").h, i + 1)
                    v.wait_ge(S("p2_b2l").h, 16 * (q + 1))
                    v.scalar_tensor_tensor(HB2[:, :].bitcast(BF16), Y2PS[:, tt % 2, :],
                                           1.0, B2B[:, :], OP.mult, OP.add)
                    S("p2_hb").dinc(v)
                    v.wait_ge(S("p2_xfl").h, 16 * (i + 1))
                    if i >= 2:
                        v.wait_ge(S("p2_ffs").h, 16 * (i - 1))
                    v.tensor_tensor(FFB[:, tt % 2, :].bitcast(BF16),
                                    HB2[:, :].bitcast(BF16), XFQ[:, tt % 2, :], OP.add)
                    S("p2_ff").dinc(v)

        @blk.tensor
        def _(t):
            for q in range(NQ):
                for pw in range(NW2):
                    idx = q * NW2 + pw
                    t.wait_ge(S("p2_w2q").h, idx + 1)
                    for oc8 in range(FC // 16):
                        gg = (FC // 16) * idx + oc8
                        if gg >= 1:
                            t.wait_ge(S("p2_w2e").h, gg)
                        for c in range(16):
                            oc = oc8 * 16 + c
                            tr = t.transpose(W2TR[:, c, :],
                                             Q2P[:, oc * 128:(oc + 1) * 128].bitcast(BF16),
                                             P.IdB[:, :])
                        S("p2_w2t").inc(tr)
                for tt in range(NT):
                    i = q * NT + tt
                    t.wait_ge(S("p2_w2e").h, (FC // 16) * NW2 * (q + 1))
                    t.wait_ge(S("p2_sl").h, 16 * (i + 1))
                    if i >= 2:
                        t.wait_ge(S("p2_hb").h, i - 1)
                    for oc in range(FC):
                        mm = t.matmul(Y2PS[:, tt % 2, :],
                                      QH2L[:, tt % 2, oc, :].bitcast(BF16),
                                      QW2[:, oc, :].bitcast(BF16),
                                      start=(oc == 0), stop=(oc == FC - 1))
                    S("p2_mm").inc(mm)

        @blk.scalar
        def _(s):
            for q in range(NQ):
                for pw in range(NW2):
                    idx = q * NW2 + pw
                    for oc8 in range(FC // 16):
                        gg = (FC // 16) * idx + oc8
                        s.wait_ge(S("p2_w2t").h, gg + 1)
                        if q >= 1 and pw == 0 and oc8 == 0:
                            s.wait_ge(S("p2_mm").h, q * NT)
                        s.activation(
                            QW2[:, oc8 * 16:(oc8 + 1) * 16, pw * 128:(pw + 1) * 128].bitcast(BF16),
                            W2TR[:, :, :], AF.Identity, bias=f32(0.0), scale=f32(1.0))
                        S("p2_w2e").dinc(s)

        @blk.gpsimd
        def _(g):
            for q in range(NQ):
                for tt in range(NT):
                    i = q * NT + tt
                    g.wait_ge(S("p2_ff").h, i + 1)
                    S("p2_ffs").inc(g.dma_start(
                        E.ffn_s[tt * 128:(tt + 1) * 128, q * 512:(q + 1) * 512],
                        FFB[:, tt % 2, :].bitcast(BF16)), 16)
            g.wait_ge(S("p2_ffs").h, 16 * NQ * NT)


def _build_qkv(nc, S, E, P, hf, XTh, QTh, KTh, VTh, WQP, BOB):
    TPH = NT // 2
    tt0 = hf * TPH
    PW = max(512, T // 2)
    with nc.psum_tensor(f"QKVPS{hf}", [128, 2, PW], F32) as QKVPS, nc.Block() as blk:
        @blk.sync
        def _(sync):
            if hf == 0:
                S("p3_bo").inc(sync.dma_start(
                    BOB[:, :], E.bo[0:1, :].partition_broadcast(128)), 16)
            for ti in range(TPH):
                S("p3_xtl").inc(sync.dma_start(
                    XTh[:, ti, :, :],
                    bass.AP(E.xt_s, (tt0 + ti) * 128 * KC * 128,
                            [[KC * 128, 128], [128, KC], [1, 128]])), 16)
        @blk.gpsimd
        def _(g):
            for wi in range(3):
                wd = (E.wq, E.wk, E.wv)[wi]
                for oc in range(KC):
                    li = wi * KC + oc
                    gi = hf * 48 + li
                    if li >= 2:
                        g.wait_ge(S("p3_qkv").h, gi - 1)
                    S("p3_wl").inc(g.dma_start(
                        WQP[:, li % 2, :, :],
                        wd[:, oc * 128:(oc + 1) * 128]
                        .rearrange("(c p) o -> p c o", p=128)), 16)

        @blk.tensor
        def _(t):
            t.wait_ge(S("p3_xtl").h, S("p3_xtl").n)
            for wi in range(3):
                for oc in range(KC):
                    li = wi * KC + oc
                    gi = hf * 48 + li
                    t.wait_ge(S("p3_wl").h, 16 * (gi + 1))
                    if li >= 2:
                        t.wait_ge(S("p3_ev").h, gi - 1)
                    hw = TPH * 64
                    for half in range(2):
                        for kc in range(KC):
                            mm = t.matmul(
                                QKVPS[:, li % 2, half * hw:(half + 1) * hw],
                                WQP[:, li % 2, kc, :],
                                XTh[:, (TPH // 2) * half:(TPH // 2) * (half + 1), kc, :],
                                start=(kc == 0), stop=(kc == KC - 1))
                    S("p3_qkv").inc(mm)

        @blk.scalar
        def _(s):
            for wi in range(3):
                dst = (QTh, KTh, VTh)[wi]
                bias = (P.BQ, P.BK, P.BV)[wi]
                for oc in range(KC):
                    li = wi * KC + oc
                    gi = hf * 48 + li
                    s.wait_ge(S("p3_qkv").h, gi + 1)
                    s.activation(
                        dst[:, oc, :], QKVPS[:, li % 2, 0:T // 2],
                        AF.Identity, bias=bias[:, oc:oc + 1], scale=f32(1.0))
                    S("p3_ev").dinc(s)


def _build_mix(nc, S, P, hf, QTh, KTh, VTh, CTh, LQ, LK, LG,
               ESb, EXb, RMX, RSM, WBt, WTs, VGs):
    NBT = (T // 2) // 32
    ph = contextlib.ExitStack()
    ps = lambda n, sh, d: ph.enter_context(nc.psum_tensor(n, sh, d))
    SCPS = ps(f"SCPS{hf}", [128, 4, 128], F32)
    WTPS = ps(f"WTPS{hf}", [128, 4, 128], BF16)
    VGPS = ps(f"VGPS{hf}", [128, 4, 128], BF16)
    CXPS = ps(f"CXPS{hf}", [128, 4, 128], F32)
    with ph, nc.Block() as blk:
        @blk.vector
        def _(v):
            for bt in range(NBT):
                gb = hf * NBT + bt
                t0 = bt * 32
                if gb >= 1:
                    v.wait_ge(S("mx_sc").h, gb)
                    v.wait_ge(S("mx_vg").h, gb)
                for (Lx, Src) in ((LQ, QTh), (LK, KTh), (LG, VTh)):
                    v.tensor_copy(
                        Lx[:, :, :, :],
                        Src[:, :, t0:t0 + 32].rearrange("p h (j ti) -> p j ti h", ti=8))
                S("mx_g").dinc(v)
                v.wait_ge(S("mx_sc").h, gb + 1)
                v.scalar_tensor_tensor(
                    ESb[:, :, :], SCPS[:, :, :], 1.0,
                    P.MASK[:, None, :].broadcast_to([128, 4, 128]), OP.mult, OP.add)
                S("mx_e0").dinc(v)
                v.tensor_reduce(RMX[:, :], ESb[:, :, :], axis=AX.X, op=OP.max)
                v.drain()
                v.tensor_tensor(ESb[:, :, :], ESb[:, :, :],
                                RMX[:, :, None].broadcast_to([128, 4, 128]), OP.subtract)
                S("mx_e0").dinc(v)
                v.wait_ge(S("mx_ex").h, gb + 1)
                v.tensor_reduce(RSM[:, :], EXb[:, :, :], axis=AX.X, op=OP.add)
                v.drain()
                v.reciprocal(RSM[:, :], RSM[:, :])
                if gb >= 1:
                    v.wait_ge(S("mx_wt").h, gb)
                v.drain()
                v.tensor_tensor(WBt[:, :, :], EXb[:, :, :],
                                RSM[:, :, None].broadcast_to([128, 4, 128]), OP.mult)
                S("mx_wb").dinc(v)

        @blk.tensor
        def _(t):
            for bt in range(NBT):
                gb = hf * NBT + bt
                t.wait_ge(S("mx_g").h, gb + 1)
                if gb >= 1:
                    t.wait_ge(S("mx_e0").h, 2 * gb)
                for j in range(4):
                    mm = t.matmul(SCPS[:, j, :],
                                  LQ[:, j, :, :].rearrange("p a b -> p (a b)"),
                                  LK[:, j, :, :].rearrange("p a b -> p (a b)"),
                                  start=True, stop=True)
                S("mx_sc").inc(mm)
                t.wait_ge(S("mx_wb").h, gb + 1)
                if gb >= 1:
                    t.wait_ge(S("mx_wte").h, gb)
                for j in range(4):
                    tr = t.transpose(WTPS[:, j, :], WBt[:, j, :], P.IdB[:, :])
                S("mx_wt").inc(tr)
                if gb >= 1:
                    t.wait_ge(S("mx_vge").h, gb)
                for j in range(4):
                    tr = t.transpose(VGPS[:, j, :],
                                     LG[:, j, :, :].rearrange("p a b -> p (a b)"), P.IdB[:, :])
                S("mx_vg").inc(tr)
                t.wait_ge(S("mx_wte").h, gb + 1)
                t.wait_ge(S("mx_vge").h, gb + 1)
                if gb >= 1:
                    t.wait_ge(S("mx_ct").h, gb)
                for j in range(4):
                    mm = t.matmul(CXPS[:, j, :], VGs[:, j, :], WTs[:, j, :],
                                  start=True, stop=True)
                S("mx_cx").inc(mm)

        @blk.scalar
        def _(s):
            for bt in range(NBT):
                gb = hf * NBT + bt
                t0 = bt * 32
                s.wait_ge(S("mx_e0").h, 2 * gb + 2)
                s.activation(EXb[:, :, :], ESb[:, :, :], AF.Exp, scale=f32(RSQD))
                S("mx_ex").dinc(s)
                s.wait_ge(S("mx_wt").h, gb + 1)
                s.activation(WTs[:, :, :], WTPS[:, :, :], AF.Identity,
                             bias=f32(0.0), scale=f32(1.0))
                S("mx_wte").dinc(s)
                s.wait_ge(S("mx_vg").h, gb + 1)
                s.activation(VGs[:, :, :], VGPS[:, :, :], AF.Identity,
                             bias=f32(0.0), scale=f32(1.0))
                S("mx_vge").dinc(s)
                s.wait_ge(S("mx_cx").h, gb + 1)
                s.activation(
                    CTh[:, :, t0:t0 + 32].rearrange("p h (j ti) -> p j ti h", ti=8),
                    CXPS[:, :, :].rearrange("p j (ti h) -> p j ti h", h=16),
                    AF.Identity, bias=f32(0.0), scale=f32(1.0))
                S("mx_ct").dinc(s)


_WO_CALL = [0]


def _build_wo(nc, S, E, P, hf, CTh, BOB):
    TPH = NT // 2
    tt0 = hf * TPH
    call = _WO_CALL[0]
    _WO_CALL[0] += 1
    ph = contextlib.ExitStack()
    sb = lambda n, sh, d: ph.enter_context(nc.sbuf_tensor(n, sh, d))
    WOB = sb(f"WOB{call}", [128, KC, D], BF16)
    XR = sb(f"XR{call}", [128, 2, D], F32)
    FR = sb(f"FR{call}", [128, 2, D], BF16)
    OB = sb(f"OB{call}", [128, 2, D], F32)
    OPS = ph.enter_context(nc.psum_tensor(f"OPS{call}", [128, 2, D], F32))
    with ph, nc.Block() as blk:
        @blk.gpsimd
        def _(g):
            for kc in range(KC):
                S("p4_wol").inc(g.dma_start(
                    WOB[:, kc, :], E.wo[kc * 128:(kc + 1) * 128, :]), 16)

        @blk.sync
        def _(sync):
            if call == 0:
                sync.wait_ge(S("p3_bo").h, 16)
            for ti in range(TPH):
                i = call * TPH + ti
                tt = tt0 + ti
                if ti >= 2:
                    sync.wait_ge(S("p4_oev").h, 3 * (i - 1))
                S("p4_xrl").inc(sync.dma_start(
                    XR[:, ti % 2, :], E.x[tt * 128:(tt + 1) * 128, :]), 16)
                S("p4_frl").inc(sync.dma_start(
                    FR[:, ti % 2, :], E.ffn_s[tt * 128:(tt + 1) * 128, :]), 16)

        @blk.tensor
        def _(t):
            t.wait_ge(S("p4_wol").h, 16 * KC * (call + 1))
            for ti in range(TPH):
                i = call * TPH + ti
                if ti >= 2:
                    t.wait_ge(S("p4_oev").h, 3 * (i - 1))
                for kc in range(KC):
                    for oc4 in range(4):
                        mm = t.matmul(OPS[:, ti % 2, oc4 * 512:(oc4 + 1) * 512],
                                      CTh[:, kc, ti * 128:(ti + 1) * 128],
                                      WOB[:, kc, oc4 * 512:(oc4 + 1) * 512],
                                      start=(kc == 0), stop=(kc == KC - 1))
                S("p4_omm").inc(mm)

        @blk.vector
        def _(v):
            for ti in range(TPH):
                i = call * TPH + ti
                v.wait_ge(S("p4_omm").h, i + 1)
                v.wait_ge(S("p4_xrl").h, 16 * (i + 1))
                v.wait_ge(S("p4_frl").h, 16 * (i + 1))
                if i >= 2:
                    v.wait_ge(S("p4_ost").h, 16 * (i - 1))
                v.scalar_tensor_tensor(
                    OB[:, ti % 2, :], OPS[:, ti % 2, :], 1.0, BOB[:, :], OP.mult, OP.add)
                S("p4_oev").dinc(v)
                v.tensor_tensor(
                    OB[:, ti % 2, :], OB[:, ti % 2, :], XR[:, ti % 2, :], OP.add)
                S("p4_oev").dinc(v)
                v.tensor_tensor(
                    OB[:, ti % 2, :], OB[:, ti % 2, :], FR[:, ti % 2, :], OP.add)
                S("p4_oev").dinc(v)

        @blk.scalar
        def _(s):
            for ti in range(TPH):
                i = call * TPH + ti
                tt = tt0 + ti
                s.wait_ge(S("p4_oev").h, 3 * (i + 1))
                S("p4_ost").inc(s.dma_start(
                    E.out[tt * 128:(tt + 1) * 128, :], OB[:, ti % 2, :]), 16)
            if call == 1:
                s.wait_ge(S("p4_ost").h, 16 * NT)


def _build_p34(nc, S, E, P):
    outer = contextlib.ExitStack()
    CTh = outer.enter_context(nc.sbuf_tensor("CTh", [128, KC, T // 2], BF16))
    BOB = outer.enter_context(nc.sbuf_tensor("BOB", [128, D], F32))
    with outer:
        ph = contextlib.ExitStack()
        sb = lambda n, sh, d: ph.enter_context(nc.sbuf_tensor(n, sh, d))
        XTh = sb("XTh", [128, NT // 2, KC, 128], BF16)
        QTh = sb("QTh", [128, KC, T // 2], BF16)
        KTh = sb("KTh", [128, KC, T // 2], BF16)
        VTh = sb("VTh", [128, KC, T // 2], BF16)
        WQP = sb("WQP", [128, 2, KC, 128], BF16)
        LQ = sb("LQ", [128, 4, 8, 16], BF16)
        LK = sb("LK", [128, 4, 8, 16], BF16)
        LG = sb("LG", [128, 4, 8, 16], BF16)
        ESb = sb("ESb", [128, 4, 128], F32)
        EXb = sb("EXb", [128, 4, 128], F32)
        RMX = sb("RMX", [128, 4], F32)
        RSM = sb("RSM", [128, 4], F32)
        WBt = sb("WBt", [128, 4, 128], BF16)
        WTs = sb("WTs", [128, 4, 128], BF16)
        VGs = sb("VGs", [128, 4, 128], BF16)
        with ph:
            for hf in range(2):
                _build_qkv(nc, S, E, P, hf, XTh, QTh, KTh, VTh, WQP, BOB)
                _build_mix(nc, S, P, hf, QTh, KTh, VTh, CTh, LQ, LK, LG,
                           ESb, EXb, RMX, RSM, WBt, WTs, VGs)
                if hf == 0:
                    with nc.Block() as blk:
                        @blk.gpsimd
                        def _(g):
                            S("ct_sp").inc(g.dma_start(
                                bass.AP(E.ct_s, 0, [[KC * (T // 2), 128], [T // 2, KC], [1, T // 2]]),
                                CTh[:, :, :]), 16)
                            g.wait_ge(S("ct_sp").h, 16)
        _build_wo(nc, S, E, P, 1, CTh, BOB)
        with nc.Block() as blk:
            @blk.gpsimd
            def _(g):
                S("ct_sp").inc(g.dma_start(
                    CTh[:, :, :],
                    bass.AP(E.ct_s, 0, [[KC * (T // 2), 128], [T // 2, KC], [1, T // 2]])), 16)
                g.wait_ge(S("ct_sp").h, 32)
        _build_wo(nc, S, E, P, 0, CTh, BOB)


class _NS:
    pass


def build_program():
    nc = bass.Bass("TRN2", target_bir_lowering=False, detect_race_conditions=False)
    E = _NS()
    E.x = nc.dram_tensor("x", [T, D], F32, kind="ExternalInput")
    E.wq = nc.dram_tensor("wq", [D, D], F32, kind="ExternalInput")
    E.wk = nc.dram_tensor("wk", [D, D], F32, kind="ExternalInput")
    E.wv = nc.dram_tensor("wv", [D, D], F32, kind="ExternalInput")
    E.wo = nc.dram_tensor("wo", [D, D], F32, kind="ExternalInput")
    E.bq = nc.dram_tensor("bq", [1, D], F32, kind="ExternalInput")
    E.bk = nc.dram_tensor("bk", [1, D], F32, kind="ExternalInput")
    E.bv = nc.dram_tensor("bv", [1, D], F32, kind="ExternalInput")
    E.bo = nc.dram_tensor("bo", [1, D], F32, kind="ExternalInput")
    E.g2 = nc.dram_tensor("g2", [1, D], F32, kind="ExternalInput")
    E.be2 = nc.dram_tensor("be2", [1, D], F32, kind="ExternalInput")
    E.lnw = nc.dram_tensor("lnw", [1, D], F32, kind="ExternalInput")
    E.lnb = nc.dram_tensor("lnb", [1, D], F32, kind="ExternalInput")
    E.w1 = nc.dram_tensor("w1", [FF, D], F32, kind="ExternalInput")
    E.b1 = nc.dram_tensor("b1", [1, FF], F32, kind="ExternalInput")
    E.w2 = nc.dram_tensor("w2", [D, FF], F32, kind="ExternalInput")
    E.b2 = nc.dram_tensor("b2", [1, D], F32, kind="ExternalInput")
    E.out = nc.dram_tensor("out", [T, D], F32, kind="ExternalOutput")
    kw = {"kind": "ExternalOutput"} if DEBUG_SCRATCH else {}
    E.xt_s = nc.dram_tensor("xt_s", [NT, 128, KC, 128], BF16, **kw)
    E.xffn_s = nc.dram_tensor("xffn_s", [T, D], BF16, **kw)
    E.qh2t_s = nc.dram_tensor("qh2t_s", [NT, 128, FC, 128], BF16, **kw)
    E.ffn_s = nc.dram_tensor("ffn_s", [T, D], BF16, **kw)
    E.ct_s = nc.dram_tensor("ct_s", [128, KC, T // 2], BF16, **kw)
    if DEBUG_SCRATCH:
        E.dbg_qht = nc.dram_tensor("dbg_qht", [128, KC, T], BF16, kind="ExternalOutput")
        E.dbg_qhi = nc.dram_tensor("dbg_qhi", [128, D], BF16, kind="ExternalOutput")
        E.dbg_yb = nc.dram_tensor("dbg_yb", [NT, 128, 512], BF16, kind="ExternalOutput")
        E.dbg_gi = nc.dram_tensor("dbg_gi", [NT, 128, 512], BF16, kind="ExternalOutput")
    else:
        E.dbg_qht = E.dbg_qhi = E.dbg_yb = E.dbg_gi = None

    persist = contextlib.ExitStack()
    sbp = lambda n, sh, d: persist.enter_context(nc.sbuf_tensor(n, sh, d))
    P = _NS()
    P.IdF = sbp("IdF", [128, 128], F32)
    P.IdB = sbp("IdB", [128, 128], BF16)
    P.MASK = sbp("MASK", [128, 128], F32)
    P.BQ = sbp("BQ", [128, KC], F32)
    P.BK = sbp("BK", [128, KC], F32)
    P.BV = sbp("BV", [128, KC], F32)
    P.G2C = sbp("G2C", [128, KC], F32)
    P.B2C = sbp("B2C", [128, KC], F32)

    sems = {}

    def S(name):
        if name not in sems:
            sems[name] = Sem(nc, persist, name)
        return sems[name]

    _build_init(nc, S, E, P)
    with contextlib.ExitStack() as s01:
        QHT = s01.enter_context(nc.sbuf_tensor("QHT", [128, KC, T], I16))
        _build_p0(nc, S, E, P, QHT)
        if BUILD_PHASES == "all":
            _build_p1(nc, S, E, P, QHT)
    if BUILD_PHASES == "all":
        _build_p2(nc, S, E, P)
        _build_p34(nc, S, E, P)
    return nc, persist


_CACHE = {}


def prepare_maps(inputs):
    x = np.ascontiguousarray(np.asarray(inputs["L0_attn_x"], dtype=np.float32).reshape(16384, D))
    w = {
        "wq": inputs["L0_attn_Wq"], "wk": inputs["L0_attn_Wk"],
        "wv": inputs["L0_attn_Wv"], "wo": inputs["L0_attn_Wo"],
        "bq": np.asarray(inputs["L0_attn_bq"]).reshape(1, D),
        "bk": np.asarray(inputs["L0_attn_bk"]).reshape(1, D),
        "bv": np.asarray(inputs["L0_attn_bv"]).reshape(1, D),
        "bo": np.asarray(inputs["L0_attn_bo"]).reshape(1, D),
        "g2": np.asarray(inputs["L0_attn_ln_g"]).reshape(1, D),
        "be2": np.asarray(inputs["L0_attn_ln_b"]).reshape(1, D),
        "lnw": np.asarray(inputs["L0_ffn_ln_w"]).reshape(1, D),
        "lnb": np.asarray(inputs["L0_ffn_ln_b"]).reshape(1, D),
        "w1": inputs["L0_ffn_W1"], "b1": np.asarray(inputs["L0_ffn_b1"]).reshape(1, FF),
        "w2": inputs["L0_ffn_W2"], "b2": np.asarray(inputs["L0_ffn_b2"]).reshape(1, D),
    }
    w = {k: np.ascontiguousarray(np.asarray(v, dtype=np.float32)) for k, v in w.items()}
    in_maps = []
    for c in range(NCORES):
        m = dict(w)
        m["x"] = np.ascontiguousarray(x[c * T:(c + 1) * T, :])
        in_maps.append(m)
    return in_maps


def kernel(**inputs):
    in_maps = prepare_maps(inputs)
    if "nc" not in _CACHE:
        _CACHE["nc"], _CACHE["persist"] = build_program()
    nc = _CACHE["nc"]
    res = run_bass_kernel_spmd(nc, in_maps, core_ids=list(range(NCORES)))
    out = np.concatenate([np.asarray(r["out"]) for r in res.results], axis=0)
    return out.reshape(4, 4096, 2048)


# version 05
# speedup vs baseline: 1.0317x; 1.0317x over previous
import contextlib
import numpy as np
import concourse.bass as bass
import concourse.mybir as mybir
from concourse.bass_utils import run_bass_kernel_spmd

F32 = mybir.dt.float32
BF16 = mybir.dt.bfloat16
I16 = mybir.dt.int16
I32 = mybir.dt.int32
AF = mybir.ActivationFunctionType
OP = mybir.AluOpType
AX = mybir.AxisListType

DEBUG_SCRATCH = False
BUILD_PHASES = "all"
NCORES = 8
T = 2048
D = 2048
FF = 8192
NT = T // 128
KC = D // 128
FC = FF // 128
C2 = 196608.0
MASK_EXP = 0xFF800000 - (1 << 32)
K_RS = float(0x7F000000)
CLIP_HI = 1.984375
EPS = 1e-5
RSQD = float(1.0 / np.float32(np.sqrt(np.float32(2048.0))))
f32 = np.float32


class Sem:
    def __init__(self, nc, es, name):
        self.h = es.enter_context(nc.semaphore(name))
        self.n = 0

    def inc(self, instr, k=1):
        instr.then_inc(self.h, k)
        self.n += k
        return self.n

    def dinc(self, eng, k=1):
        eng.drain().then_inc(self.h, k)
        self.n += k
        return self.n


def quant_chain(v, X_ap, n_el, AM, PB, RS, PBB, RSB, U_ap, V_ap, Q_ap, fp32_input, npart=128):
    nb = n_el // 32
    nb2 = nb // 2
    c2 = n_el // 2
    hs = [(slice(0, nb2), slice(0, c2)), (slice(nb2, nb), slice(c2, n_el))]
    v.drain()
    for bs, cs in hs:
        v.tensor_reduce(AM[:, bs], X_ap[:, cs].rearrange("p (b j) -> p b j", j=32),
                        axis=AX.X, op=OP.max, apply_absolute_value=True)
    for bs, cs in hs:
        v.tensor_single_scalar(PB[:, bs], AM[:, bs].bitcast(I32), MASK_EXP, OP.bitwise_and)
    for bs, cs in hs:
        v.tensor_scalar(RS[:, bs], PB[:, bs], -1.0, K_RS, OP.mult, OP.add)
    for bs, cs in hs:
        v.tensor_copy(PBB[:, bs], PB[:, bs].bitcast(F32))
    if not fp32_input:
        for bs, cs in hs:
            v.tensor_copy(RSB[:, bs], RS[:, bs].bitcast(F32))
    for bs, cs in hs:
        rs_b = (RS[:, bs].bitcast(F32) if fp32_input else RSB[:, bs])
        v.tensor_tensor(U_ap[:, cs].rearrange("p (b j) -> p b j", j=32),
                        X_ap[:, cs].rearrange("p (b j) -> p b j", j=32),
                        rs_b[:, :, None].broadcast_to([npart, nb2, 32]), OP.mult)
    for bs, cs in hs:
        v.tensor_scalar(V_ap[:, cs], U_ap[:, cs], C2, -C2, OP.add, OP.add)
    for bs, cs in hs:
        v.tensor_scalar_min(V_ap[:, cs], V_ap[:, cs], CLIP_HI)
    for bs, cs in hs:
        q = v.tensor_tensor(Q_ap[:, cs].rearrange("p (b j) -> p b j", j=32),
                            V_ap[:, cs].rearrange("p (b j) -> p b j", j=32),
                            PBB[:, bs][:, :, None].broadcast_to([npart, nb2, 32]), OP.mult)
    return q


def _build_init(nc, S, E, P):
    ph = contextlib.ExitStack()
    sb = lambda n, sh, d: ph.enter_context(nc.sbuf_tensor(n, sh, d))
    Pidx = sb("Pidx", [128, 1], I32)
    Cidx = sb("Cidx", [128, 128], I32)
    CBl = sb("CBl", [128, 128], I32)
    TDm = sb("TDm", [128, 128], F32)
    GEm = sb("GEm", [128, 128], F32)
    with ph:
        with nc.Block() as blk:
            @blk.gpsimd
            def _(g):
                g.iota(Pidx[:, :], [[1, 1]], channel_multiplier=1)
                g.iota(Cidx[:, :], [[1, 128]], channel_multiplier=0)
                g.iota(CBl[:, :].rearrange("p (a b) -> p a b", b=16),
                       [[1, 8], [0, 16]], channel_multiplier=0)

            @blk.sync
            def _(sync):
                ldi = S("ldi")
                for dst, src in ((P.BQ, E.bq), (P.BK, E.bk), (P.BV, E.bv),
                                 (P.G2C, E.g2), (P.B2C, E.be2)):
                    ldi.inc(sync.dma_start(
                        dst[:, :], src[0:1, :].rearrange("o (c p) -> p (o c)", p=128),
                        allow_slow_non_contiguous=True), 16)
                sync.wait_ge(ldi.h, ldi.n)

        with nc.Block() as blk:
            @blk.vector
            def _(v):
                v.tensor_tensor(P.IdF[:, :], Cidx[:, :],
                                Pidx[:, :].broadcast_to([128, 128]), OP.is_equal)
                v.drain()
                v.tensor_copy(P.IdB[:, :], P.IdF[:, :])
                v.tensor_scalar(TDm[:, :], CBl[:, :], -16.0, None, OP.mult)
                v.drain()
                v.tensor_tensor(TDm[:, :], Pidx[:, :].broadcast_to([128, 128]), TDm[:, :], OP.add)
                v.drain()
                v.tensor_single_scalar(GEm[:, :], TDm[:, :], 0.0, OP.is_ge)
                v.drain()
                v.tensor_single_scalar(TDm[:, :], TDm[:, :], 15.0, OP.is_le)
                v.drain()
                v.tensor_tensor(GEm[:, :], GEm[:, :], TDm[:, :], OP.logical_and)
                v.drain()
                v.tensor_scalar(P.MASK[:, :], GEm[:, :], 1e30, -1e30, OP.mult, OP.add)


def _build_p0(nc, S, E, P, QHT):
    ph = contextlib.ExitStack()
    sb = lambda n, sh, d: ph.enter_context(nc.sbuf_tensor(n, sh, d))
    ps = lambda n, sh, d: ph.enter_context(nc.psum_tensor(n, sh, d))
    XIN = sb("XIN", [128, 2, D], F32)
    XN = sb("XN", [128, D], F32)
    T2 = sb("T2", [128, D], F32)
    T3 = sb("T3", [128, D], F32)
    H3T = sb("H3T", [128, D], F32)
    XFN = sb("XFN", [128, D], I16)
    XFB = sb("XFB", [128, 2, D], BF16)
    LNW = sb("LNW", [128, D], F32)
    LNB = sb("LNB", [128, D], F32)
    HQ = sb("HQ", [128, D], I16)
    UQ = sb("UQ", [128, D], I16)
    VQ = sb("VQ", [128, D], I16)
    QHI = sb("QHI", [128, D], I16)
    AM0 = sb("AM0", [128, 64], F32)
    PB0 = sb("PB0", [128, 64], I32)
    RS0 = sb("RS0", [128, 64], I32)
    PBB0 = sb("PBB0", [128, 64], BF16)
    RSB0 = sb("RSB0", [128, 64], BF16)
    STt = sb("STt", [128, 24], F32)
    MVt = sb("MVt", [128, 2], F32)
    VARt = sb("VARt", [128, 1], F32)
    SQt = sb("SQt", [128, 1], F32)
    R0t = sb("R0t", [128, 1], F32)
    A1t = sb("A1t", [128, 1], F32)
    RRt = sb("RRt", [128, 3], F32)
    NBt = sb("NBt", [128, 3], F32)
    XTB = sb("XTB", [128, 2, KC, 128], BF16)
    TRP = ps("TRP", [128, KC, 128], F32)
    TRQ = ps("TRQ", [128, KC, 128], BF16)

    def stats(v, src_ap):
        v.drain()
        for c in range(4):
            v.bn_stats(STt[:, 6 * c:6 * c + 6], src_ap[:, 512 * c:512 * (c + 1)])
        v.drain()
        v.bn_aggr(MVt[:, :].rearrange("p (a t) -> p a t", t=2),
                  STt[:, :].rearrange("p (g t) -> p g t", t=3))
        v.drain()
        return v.tensor_single_scalar(VARt[:, :], MVt[:, 1:2], EPS, OP.add)

    def rstd(v, k):
        v.reciprocal(RRt[:, k:k + 1], SQt[:, :])
        v.drain()
        v.tensor_tensor(NBt[:, k:k + 1], MVt[:, 0:1], RRt[:, k:k + 1], OP.mult)
        v.drain()
        return v.tensor_single_scalar(NBt[:, k:k + 1], NBt[:, k:k + 1], -1.0, OP.mult)

    with ph, nc.Block() as blk:
        @blk.sync
        def _(sync):
            ldw = S("p0_lnwb")
            ldw.inc(sync.dma_start(LNW[:, :], E.lnw[0:1, :].partition_broadcast(128)), 16)
            ldw.inc(sync.dma_start(LNB[:, :], E.lnb[0:1, :].partition_broadcast(128)), 16)
            for tt in range(NT):
                if tt >= 2:
                    sync.wait_ge(S("p0_xn").h, tt - 1)
                S("p0_ldx").inc(sync.dma_start(
                    XIN[:, tt % 2, :], E.x[tt * 128:(tt + 1) * 128, :]), 16)

        @blk.vector
        def _(v):
            for tt in range(NT):
                b = tt % 2
                v.wait_ge(S("p0_ldx").h, 16 * (tt + 1))
                if tt == 0:
                    v.wait_ge(S("p0_lnwb").h, 32)
                stats(v, XIN[:, b, :])
                S("p0_var").dinc(v)
                v.wait_ge(S("p0_sqd").h, 3 * tt + 1)
                rstd(v, 0)
                S("p0_r").dinc(v)
                v.wait_ge(S("p0_xn").h, tt + 1)
                stats(v, XN[:, :])
                S("p0_var").dinc(v)
                v.wait_ge(S("p0_sqd").h, 3 * tt + 2)
                rstd(v, 1)
                if tt >= 2:
                    v.wait_ge(S("p0_t3").h, tt - 1)
                    v.wait_ge(S("p0_xfs").h, 16 * (tt - 1))
                v.drain()
                v.tensor_copy(XFN[:, :].bitcast(BF16), XN[:, :])
                S("p0_r").dinc(v)
                v.tensor_copy(XFB[:, b, :], XFN[:, :].bitcast(BF16))
                S("p0_xfb").dinc(v)
                stats(v, XFB[:, b, :])
                S("p0_var").dinc(v)
                v.wait_ge(S("p0_sqd").h, 3 * tt + 3)
                rstd(v, 2)
                S("p0_r").dinc(v)
                v.wait_ge(S("p0_tr1").h, tt + 1)
                if tt >= 2:
                    v.wait_ge(S("p0_xts").h, 16 * (tt - 1))
                for c in range(KC):
                    v.tensor_scalar(XTB[:, b, c, :], TRP[:, c, :],
                                    P.G2C[:, c:c + 1], P.B2C[:, c:c + 1], OP.mult, OP.add)
                S("p0_xtb").dinc(v)
                v.wait_ge(S("p0_t3").h, tt + 1)
                v.scalar_tensor_tensor(H3T[:, :], T3[:, :], 1.0, LNW[:, :], OP.mult, OP.mult)
                v.drain()
                v.scalar_tensor_tensor(HQ[:, :].bitcast(BF16), H3T[:, :], 1.0, LNB[:, :],
                                       OP.mult, OP.add)
                if tt >= 1:
                    v.wait_ge(S("p0_trq").h, tt)
                if DEBUG_SCRATCH and tt == 1:
                    v.wait_ge(S("p0_dbg").h, 16)
                quant_chain(v, HQ[:, :].bitcast(BF16), D, AM0, PB0, RS0, PBB0, RSB0,
                            UQ[:, :].bitcast(BF16), VQ[:, :].bitcast(BF16),
                            QHI[:, :].bitcast(BF16), fp32_input=False)
                S("p0_qh").dinc(v)

        @blk.scalar
        def _(s):
            for tt in range(NT):
                b = tt % 2
                for k in range(3):
                    s.wait_ge(S("p0_var").h, 3 * tt + k + 1)
                    s.activation(SQt[:, :], VARt[:, :], AF.Sqrt,
                                 bias=f32(0.0), scale=f32(1.0))
                    S("p0_sqd").dinc(s)
                    if k == 0:
                        s.wait_ge(S("p0_r").h, 3 * tt + 1)
                        s.activation(XN[:, :], XIN[:, b, :], AF.Identity,
                                     bias=NBt[:, 0:1], scale=RRt[:, 0:1])
                        S("p0_xn").dinc(s)
                    elif k == 1:
                        s.wait_ge(S("p0_r").h, 3 * tt + 2)
                        s.activation(T2[:, :], XN[:, :], AF.Identity,
                                     bias=NBt[:, 1:2], scale=RRt[:, 1:2])
                        S("p0_t2").dinc(s)
                    else:
                        s.wait_ge(S("p0_r").h, 3 * tt + 3)
                        s.activation(T3[:, :], XFB[:, b, :], AF.Identity,
                                     bias=NBt[:, 2:3], scale=RRt[:, 2:3])
                        S("p0_t3").dinc(s)
                s.wait_ge(S("p0_trq").h, tt + 1)
                s.activation(
                    QHT[:, :, tt * 128:(tt + 1) * 128].bitcast(BF16),
                    TRQ[:, :, :], AF.Identity, bias=f32(0.0), scale=f32(1.0))
                S("p0_qht").dinc(s)

        @blk.tensor
        def _(t):
            for tt in range(NT):
                t.wait_ge(S("p0_t2").h, tt + 1)
                if tt >= 1:
                    t.wait_ge(S("p0_xtb").h, tt)
                for c in range(KC):
                    tr = t.transpose(TRP[:, c, :], T2[:, c * 128:(c + 1) * 128], P.IdF[:, :])
                S("p0_tr1").inc(tr)
                t.wait_ge(S("p0_qh").h, tt + 1)
                if tt >= 1:
                    t.wait_ge(S("p0_qht").h, tt)
                for c in range(KC):
                    tr = t.transpose(TRQ[:, c, :],
                                     QHI[:, c * 128:(c + 1) * 128].bitcast(BF16), P.IdB[:, :])
                S("p0_trq").inc(tr)

        @blk.gpsimd
        def _(g):
            if DEBUG_SCRATCH:
                g.wait_ge(S("p0_qh").h, 1)
                S("p0_dbg").inc(g.dma_start(E.dbg_qhi[:, :], QHI[:, :].bitcast(BF16)), 16)
            for tt in range(NT):
                b = tt % 2
                g.wait_ge(S("p0_xtb").h, tt + 1)
                S("p0_xts").inc(g.dma_start(
                    bass.AP(E.xt_s, tt * 128 * KC * 128,
                            [[KC * 128, 128], [128, KC], [1, 128]]),
                    XTB[:, b, :, :]), 16)
                g.wait_ge(S("p0_xfb").h, tt + 1)
                S("p0_xfs").inc(g.dma_start(
                    E.xffn_s[tt * 128:(tt + 1) * 128, :], XFB[:, b, :]), 16)
            if DEBUG_SCRATCH:
                g.wait_ge(S("p0_qht").h, NT)
                S("p0_dbg").inc(g.dma_start(
                    E.dbg_qht[:, :, :].rearrange("p c t -> p (c t)"),
                    QHT[:, :, :].bitcast(BF16).rearrange("p c t -> p (c t)")), 16)
                g.wait_ge(S("p0_dbg").h, 32)
            g.wait_ge(S("p0_xts").h, 16 * NT)
            g.wait_ge(S("p0_xfs").h, 16 * NT)


def _build_p1(nc, S, E, P, QHT):
    NPAN = FF // 512
    ph = contextlib.ExitStack()
    sb = lambda n, sh, d: ph.enter_context(nc.sbuf_tensor(n, sh, d))
    ps = lambda n, sh, d: ph.enter_context(nc.psum_tensor(n, sh, d))
    W1P = sb("W1P", [128, 2, 2, D], F32)
    U1 = sb("U1", [128, 2, D], F32)
    V1 = sb("V1", [128, 2, D], I16)
    Q1P = sb("Q1P", [128, 2, D], I16)
    AM1 = sb("AM1", [128, 128], F32)
    PB1 = sb("PB1", [128, 128], I32)
    RS1 = sb("RS1", [128, 128], I32)
    PBB1 = sb("PBB1", [128, 128], BF16)
    RSB1 = sb("RSB1", [128, 128], BF16)
    QW1 = sb("QW1", [128, 2, KC, 512], I16)
    B1B = sb("B1B", [128, 2, 512], F32)
    YB = sb("YB", [128, 2, 512], I16)
    GI = sb("GI", [128, 2, 512], I16)
    UG = sb("UG", [128, 512], I16)
    VG = sb("VG", [128, 512], I16)
    QG = sb("QG", [128, 2, 512], I16)
    AMG = sb("AMG", [128, 16], F32)
    PBG = sb("PBG", [128, 16], I32)
    RSG = sb("RSG", [128, 16], I32)
    PBBG = sb("PBBG", [128, 16], BF16)
    RSBG = sb("RSBG", [128, 16], BF16)
    QH2B = sb("QH2B", [128, 2, 4, 128], I16)
    Y1PS = ps("Y1PS", [128, 2, 512], F32)
    W1TR = ps("W1TR", [128, KC, 128], BF16)
    H2TR = ps("H2TR", [128, 4, 128], BF16)

    with ph, nc.Block() as blk:
        @blk.sync
        def _(sync):
            for pn in range(NPAN):
                for r in range(2):
                    idx = pn * 2 + r
                    if idx >= 2:
                        sync.wait_ge(S("p1_w1q").h, idx - 1)
                    S("p1_w1l").inc(sync.dma_start(
                        W1P[:, idx % 2, :, :],
                        E.w1[pn * 512 + r * 256: pn * 512 + (r + 1) * 256, :]
                        .rearrange("(rr p) k -> p rr k", p=128)), 16)
                if pn >= 2:
                    sync.wait_ge(S("p1_yb").h, (pn - 1) * NT)
                S("p1_b1l").inc(sync.dma_start(
                    B1B[:, pn % 2, :],
                    E.b1[0:1, pn * 512:(pn + 1) * 512].partition_broadcast(128)), 16)

        @blk.vector
        def _(v):
            for pn in range(NPAN):
                for r in range(2):
                    idx = pn * 2 + r
                    v.wait_ge(S("p1_w1l").h, 16 * (idx + 1))
                    if idx >= 1:
                        v.wait_ge(S("p1_w1t").h, 2 * idx)
                    quant_chain(
                        v, W1P[:, idx % 2, :, :].rearrange("p r k -> p (r k)"), 2 * D,
                        AM1, PB1, RS1, PBB1, RSB1,
                        U1[:, :, :].rearrange("p r k -> p (r k)"),
                        V1[:, :, :].rearrange("p r k -> p (r k)").bitcast(BF16),
                        Q1P[:, :, :].rearrange("p r k -> p (r k)").bitcast(BF16),
                        fp32_input=True)
                    S("p1_w1q").dinc(v)
                for tt in range(NT):
                    i = pn * NT + tt
                    v.wait_ge(S("p1_mm").h, i + 1)
                    v.wait_ge(S("p1_b1l").h, 16 * (pn + 1))
                    if i >= 2:
                        v.wait_ge(S("p1_gi").h, i - 1)
                    if DEBUG_SCRATCH and pn == 0 and tt == 1:
                        v.wait_ge(S("p1_dbg").h, 16)
                    v.scalar_tensor_tensor(YB[:, i % 2, :].bitcast(BF16), Y1PS[:, tt % 2, :],
                                           1.0, B1B[:, pn % 2, :], OP.mult, OP.add)
                    S("p1_yb").dinc(v)
                    v.wait_ge(S("p1_gi").h, i + 1)
                    if i >= 2:
                        v.wait_ge(S("p1_h2t").h, i - 1)
                    quant_chain(v, GI[:, i % 2, :].bitcast(BF16), 512,
                                AMG, PBG, RSG, PBBG, RSBG,
                                UG[:, :].bitcast(BF16), VG[:, :].bitcast(BF16),
                                QG[:, i % 2, :].bitcast(BF16), fp32_input=False)
                    S("p1_qg").dinc(v)

        @blk.tensor
        def _(t):
            for pn in range(NPAN):
                for r in range(2):
                    idx = pn * 2 + r
                    t.wait_ge(S("p1_w1q").h, idx + 1)
                    for pr in range(2):
                        gidx = idx * 2 + pr
                        if gidx >= 1:
                            t.wait_ge(S("p1_w1e").h, gidx)
                        for c in range(KC):
                            tr = t.transpose(W1TR[:, c, :],
                                             Q1P[:, pr, c * 128:(c + 1) * 128].bitcast(BF16),
                                             P.IdB[:, :])
                        S("p1_w1t").inc(tr)
                for tt in range(NT):
                    i = pn * NT + tt
                    t.wait_ge(S("p1_w1e").h, 4 * (pn + 1))
                    if i >= 2:
                        t.wait_ge(S("p1_yb").h, i - 1)
                    for kc in range(KC):
                        mm = t.matmul(Y1PS[:, tt % 2, :],
                                      QHT[:, kc, tt * 128:(tt + 1) * 128].bitcast(BF16),
                                      QW1[:, pn % 2, kc, :].bitcast(BF16),
                                      start=(kc == 0), stop=(kc == KC - 1))
                    S("p1_mm").inc(mm)
                    t.wait_ge(S("p1_qg").h, i + 1)
                    if i >= 1:
                        t.wait_ge(S("p1_h2e").h, i)
                    for c in range(4):
                        tr = t.transpose(H2TR[:, c, :],
                                         QG[:, i % 2, c * 128:(c + 1) * 128].bitcast(BF16),
                                         P.IdB[:, :])
                    S("p1_h2t").inc(tr)

        @blk.scalar
        def _(s):
            for pn in range(NPAN):
                for r in range(2):
                    idx = pn * 2 + r
                    for pr in range(2):
                        gidx = idx * 2 + pr
                        s.wait_ge(S("p1_w1t").h, gidx + 1)
                        if pn >= 2 and r == 0 and pr == 0:
                            s.wait_ge(S("p1_mm").h, (pn - 1) * NT)
                        s.activation(
                            QW1[:, pn % 2, :,
                                (r * 2 + pr) * 128:(r * 2 + pr + 1) * 128].bitcast(BF16),
                            W1TR[:, :, :], AF.Identity, bias=f32(0.0), scale=f32(1.0))
                        S("p1_w1e").dinc(s)
                for tt in range(NT):
                    i = pn * NT + tt
                    s.wait_ge(S("p1_yb").h, i + 1)
                    if i >= 2:
                        s.wait_ge(S("p1_qg").h, i - 1)
                    if DEBUG_SCRATCH and pn == 0 and tt == 1:
                        s.wait_ge(S("p1_dbg").h, 32)
                    s.activation(GI[:, i % 2, :].bitcast(BF16), YB[:, i % 2, :].bitcast(BF16),
                                 AF.Gelu_apprx_tanh)
                    S("p1_gi").dinc(s)
                    s.wait_ge(S("p1_h2t").h, i + 1)
                    if i >= 2:
                        s.wait_ge(S("p1_h2s").h, 16 * (i - 1))
                    s.activation(QH2B[:, i % 2, :, :].bitcast(BF16),
                                 H2TR[:, :, :], AF.Identity,
                                 bias=f32(0.0), scale=f32(1.0))
                    S("p1_h2e").dinc(s)

        @blk.gpsimd
        def _(g):
            if DEBUG_SCRATCH:
                g.wait_ge(S("p1_yb").h, 1)
                S("p1_dbg").inc(g.dma_start(E.dbg_yb[0, :, :], YB[:, 0, :].bitcast(BF16)), 16)
                g.wait_ge(S("p1_gi").h, 1)
                S("p1_dbg").inc(g.dma_start(E.dbg_gi[0, :, :], GI[:, 0, :].bitcast(BF16)), 16)
            for pn in range(NPAN):
                for tt in range(NT):
                    i = pn * NT + tt
                    g.wait_ge(S("p1_h2e").h, i + 1)
                    S("p1_h2s").inc(g.dma_start(
                        bass.AP(E.qh2t_s, tt * 128 * FC * 128 + pn * 4 * 128,
                                [[FC * 128, 128], [128, 4], [1, 128]]),
                        QH2B[:, i % 2, :, :].bitcast(BF16)), 16)
            g.wait_ge(S("p1_h2s").h, 16 * NPAN * NT)


def _build_p2(nc, S, E, P):
    NQ = 4
    NW2 = 4
    ph = contextlib.ExitStack()
    sb = lambda n, sh, d: ph.enter_context(nc.sbuf_tensor(n, sh, d))
    ps = lambda n, sh, d: ph.enter_context(nc.psum_tensor(n, sh, d))
    W2P = sb("W2P", [128, FF], F32)
    U2 = sb("U2", [128, FF // 2], F32)
    V2 = sb("V2", [128, FF], I16)
    Q2P = sb("Q2P", [128, FF], I16)
    AM2 = sb("AM2", [128, 128], F32)
    PB2 = sb("PB2", [128, 128], I32)
    RS2 = sb("RS2", [128, 128], I32)
    PBB2 = sb("PBB2", [128, 128], BF16)
    RSB2 = sb("RSB2", [128, 128], BF16)
    QW2 = sb("QW2", [128, FC, 512], I16)
    B2B = sb("B2B", [128, 512], F32)
    QH2L = sb("QH2L", [128, 2, FC, 128], I16)
    HB2 = sb("HB2", [128, 512], I16)
    XFQ = sb("XFQ", [128, 2, 512], BF16)
    FFB = sb("FFB", [128, 2, 512], I16)
    Y2PS = ps("Y2PS", [128, 2, 512], F32)
    W2TR = ps("W2TR", [128, 16, 128], BF16)

    with ph, nc.Block() as blk:
        @blk.sync
        def _(sync):
            for q in range(NQ):
                for pw in range(NW2):
                    idx = q * NW2 + pw
                    if idx >= 1:
                        sync.wait_ge(S("p2_w2q").h, idx)
                    S("p2_w2l").inc(sync.dma_start(
                        W2P[:, :],
                        E.w2[q * 512 + pw * 128: q * 512 + (pw + 1) * 128, :]), 16)
                if q >= 1:
                    sync.wait_ge(S("p2_hb").h, q * NT)
                S("p2_b2l").inc(sync.dma_start(
                    B2B[:, :], E.b2[0:1, q * 512:(q + 1) * 512].partition_broadcast(128)), 16)
                for tt in range(NT):
                    i = q * NT + tt
                    if i >= 2:
                        sync.wait_ge(S("p2_mm").h, i - 1)
                    S("p2_sl").inc(sync.dma_start(
                        QH2L[:, tt % 2, :, :].bitcast(BF16),
                        bass.AP(E.qh2t_s, tt * 128 * FC * 128,
                                [[FC * 128, 128], [128, FC], [1, 128]])), 16)
                    if i >= 2:
                        sync.wait_ge(S("p2_ff").h, i - 1)
                    S("p2_xfl").inc(sync.dma_start(
                        XFQ[:, tt % 2, :],
                        E.xffn_s[tt * 128:(tt + 1) * 128, q * 512:(q + 1) * 512]), 16)

        @blk.vector
        def _(v):
            for q in range(NQ):
                for pw in range(NW2):
                    idx = q * NW2 + pw
                    v.wait_ge(S("p2_w2l").h, 16 * (idx + 1))
                    if idx >= 1:
                        v.wait_ge(S("p2_w2t").h, (FC // 16) * idx)
                    hw = FF // 2
                    for hh in range(2):
                        qq = quant_chain(v, W2P[:, hh * hw:(hh + 1) * hw], hw,
                                         AM2, PB2, RS2, PBB2, RSB2,
                                         U2[:, :], V2[:, hh * hw:(hh + 1) * hw].bitcast(BF16),
                                         Q2P[:, hh * hw:(hh + 1) * hw].bitcast(BF16),
                                         fp32_input=True)
                    S("p2_w2q").dinc(v)
                for tt in range(NT):
                    i = q * NT + tt
                    v.wait_ge(S("p2_mm").h, i + 1)
                    v.wait_ge(S("p2_b2l").h, 16 * (q + 1))
                    v.scalar_tensor_tensor(HB2[:, :].bitcast(BF16), Y2PS[:, tt % 2, :],
                                           1.0, B2B[:, :], OP.mult, OP.add)
                    S("p2_hb").dinc(v)
                    v.wait_ge(S("p2_xfl").h, 16 * (i + 1))
                    if i >= 2:
                        v.wait_ge(S("p2_ffs").h, 16 * (i - 1))
                    v.tensor_tensor(FFB[:, tt % 2, :].bitcast(BF16),
                                    HB2[:, :].bitcast(BF16), XFQ[:, tt % 2, :], OP.add)
                    S("p2_ff").dinc(v)

        @blk.tensor
        def _(t):
            for q in range(NQ):
                for pw in range(NW2):
                    idx = q * NW2 + pw
                    t.wait_ge(S("p2_w2q").h, idx + 1)
                    for oc8 in range(FC // 16):
                        gg = (FC // 16) * idx + oc8
                        if gg >= 1:
                            t.wait_ge(S("p2_w2e").h, gg)
                        for c in range(16):
                            oc = oc8 * 16 + c
                            tr = t.transpose(W2TR[:, c, :],
                                             Q2P[:, oc * 128:(oc + 1) * 128].bitcast(BF16),
                                             P.IdB[:, :])
                        S("p2_w2t").inc(tr)
                for tt in range(NT):
                    i = q * NT + tt
                    t.wait_ge(S("p2_w2e").h, (FC // 16) * NW2 * (q + 1))
                    t.wait_ge(S("p2_sl").h, 16 * (i + 1))
                    if i >= 2:
                        t.wait_ge(S("p2_hb").h, i - 1)
                    for oc in range(FC):
                        mm = t.matmul(Y2PS[:, tt % 2, :],
                                      QH2L[:, tt % 2, oc, :].bitcast(BF16),
                                      QW2[:, oc, :].bitcast(BF16),
                                      start=(oc == 0), stop=(oc == FC - 1))
                    S("p2_mm").inc(mm)

        @blk.scalar
        def _(s):
            for q in range(NQ):
                for pw in range(NW2):
                    idx = q * NW2 + pw
                    for oc8 in range(FC // 16):
                        gg = (FC // 16) * idx + oc8
                        s.wait_ge(S("p2_w2t").h, gg + 1)
                        if q >= 1 and pw == 0 and oc8 == 0:
                            s.wait_ge(S("p2_mm").h, q * NT)
                        s.activation(
                            QW2[:, oc8 * 16:(oc8 + 1) * 16, pw * 128:(pw + 1) * 128].bitcast(BF16),
                            W2TR[:, :, :], AF.Identity, bias=f32(0.0), scale=f32(1.0))
                        S("p2_w2e").dinc(s)

        @blk.gpsimd
        def _(g):
            for q in range(NQ):
                for tt in range(NT):
                    i = q * NT + tt
                    g.wait_ge(S("p2_ff").h, i + 1)
                    S("p2_ffs").inc(g.dma_start(
                        E.ffn_s[tt * 128:(tt + 1) * 128, q * 512:(q + 1) * 512],
                        FFB[:, tt % 2, :].bitcast(BF16)), 16)
            g.wait_ge(S("p2_ffs").h, 16 * NQ * NT)


def _build_qkv(nc, S, E, P, hf, XTh, QTh, KTh, VTh, WQP, BOB):
    TPH = NT // 2
    tt0 = hf * TPH
    PW = max(512, T // 2)
    with nc.psum_tensor(f"QKVPS{hf}", [128, 2, PW], F32) as QKVPS, nc.Block() as blk:
        @blk.sync
        def _(sync):
            if hf == 0:
                S("p3_bo").inc(sync.dma_start(
                    BOB[:, :], E.bo[0:1, :].partition_broadcast(128)), 16)
            for ti in range(TPH):
                S("p3_xtl").inc(sync.dma_start(
                    XTh[:, ti, :, :],
                    bass.AP(E.xt_s, (tt0 + ti) * 128 * KC * 128,
                            [[KC * 128, 128], [128, KC], [1, 128]])), 16)
        @blk.gpsimd
        def _(g):
            for wi in range(3):
                wd = (E.wq, E.wk, E.wv)[wi]
                for oc in range(KC):
                    li = wi * KC + oc
                    gi = hf * 48 + li
                    if li >= 2:
                        g.wait_ge(S("p3_qkv").h, gi - 1)
                    S("p3_wl").inc(g.dma_start(
                        WQP[:, li % 2, :, :],
                        wd[:, oc * 128:(oc + 1) * 128]
                        .rearrange("(c p) o -> p c o", p=128)), 16)

        @blk.tensor
        def _(t):
            t.wait_ge(S("p3_xtl").h, S("p3_xtl").n)
            for wi in range(3):
                for oc in range(KC):
                    li = wi * KC + oc
                    gi = hf * 48 + li
                    t.wait_ge(S("p3_wl").h, 16 * (gi + 1))
                    if li >= 2:
                        t.wait_ge(S("p3_ev").h, gi - 1)
                    hw = TPH * 64
                    for half in range(2):
                        for kc in range(KC):
                            mm = t.matmul(
                                QKVPS[:, li % 2, half * hw:(half + 1) * hw],
                                WQP[:, li % 2, kc, :],
                                XTh[:, (TPH // 2) * half:(TPH // 2) * (half + 1), kc, :],
                                start=(kc == 0), stop=(kc == KC - 1))
                    S("p3_qkv").inc(mm)

        @blk.scalar
        def _(s):
            for wi in range(3):
                dst = (QTh, KTh, VTh)[wi]
                bias = (P.BQ, P.BK, P.BV)[wi]
                for oc in range(KC):
                    li = wi * KC + oc
                    gi = hf * 48 + li
                    s.wait_ge(S("p3_qkv").h, gi + 1)
                    s.activation(
                        dst[:, oc, :], QKVPS[:, li % 2, 0:T // 2],
                        AF.Identity, bias=bias[:, oc:oc + 1], scale=f32(1.0))
                    S("p3_ev").dinc(s)


def _build_mix(nc, S, P, hf, QTh, KTh, VTh, CTh, LQ, LK, LG,
               ESb, EXb, RMX, RSM, WBt, WTs, VGs):
    NBT = (T // 2) // 32
    ph = contextlib.ExitStack()
    ps = lambda n, sh, d: ph.enter_context(nc.psum_tensor(n, sh, d))
    SCPS = ps(f"SCPS{hf}", [128, 4, 128], F32)
    WTPS = ps(f"WTPS{hf}", [128, 4, 128], BF16)
    VGPS = ps(f"VGPS{hf}", [128, 4, 128], BF16)
    CXPS = ps(f"CXPS{hf}", [128, 4, 128], F32)
    with ph, nc.Block() as blk:
        @blk.vector
        def _(v):
            for bt in range(NBT):
                gb = hf * NBT + bt
                t0 = bt * 32
                if gb >= 1:
                    v.wait_ge(S("mx_sc").h, gb)
                    v.wait_ge(S("mx_vg").h, gb)
                for (Lx, Src) in ((LQ, QTh), (LK, KTh), (LG, VTh)):
                    v.tensor_copy(
                        Lx[:, :, :, :],
                        Src[:, :, t0:t0 + 32].rearrange("p h (j ti) -> p j ti h", ti=8))
                S("mx_g").dinc(v)
                v.wait_ge(S("mx_sc").h, gb + 1)
                v.scalar_tensor_tensor(
                    ESb[:, :, :], SCPS[:, :, :], 1.0,
                    P.MASK[:, None, :].broadcast_to([128, 4, 128]), OP.mult, OP.add)
                S("mx_e0").dinc(v)
                v.tensor_reduce(RMX[:, :], ESb[:, :, :], axis=AX.X, op=OP.max)
                v.drain()
                v.tensor_tensor(ESb[:, :, :], ESb[:, :, :],
                                RMX[:, :, None].broadcast_to([128, 4, 128]), OP.subtract)
                S("mx_e0").dinc(v)
                v.wait_ge(S("mx_ex").h, gb + 1)
                v.tensor_reduce(RSM[:, :], EXb[:, :, :], axis=AX.X, op=OP.add)
                v.drain()
                v.reciprocal(RSM[:, :], RSM[:, :])
                if gb >= 1:
                    v.wait_ge(S("mx_wt").h, gb)
                v.drain()
                v.tensor_tensor(WBt[:, :, :], EXb[:, :, :],
                                RSM[:, :, None].broadcast_to([128, 4, 128]), OP.mult)
                S("mx_wb").dinc(v)

        @blk.tensor
        def _(t):
            for bt in range(NBT):
                gb = hf * NBT + bt
                t.wait_ge(S("mx_g").h, gb + 1)
                if gb >= 1:
                    t.wait_ge(S("mx_e0").h, 2 * gb)
                for j in range(4):
                    mm = t.matmul(SCPS[:, j, :],
                                  LQ[:, j, :, :].rearrange("p a b -> p (a b)"),
                                  LK[:, j, :, :].rearrange("p a b -> p (a b)"),
                                  start=True, stop=True)
                S("mx_sc").inc(mm)
                t.wait_ge(S("mx_wb").h, gb + 1)
                if gb >= 1:
                    t.wait_ge(S("mx_wte").h, gb)
                for j in range(4):
                    tr = t.transpose(WTPS[:, j, :], WBt[:, j, :], P.IdB[:, :])
                S("mx_wt").inc(tr)
                if gb >= 1:
                    t.wait_ge(S("mx_vge").h, gb)
                for j in range(4):
                    tr = t.transpose(VGPS[:, j, :],
                                     LG[:, j, :, :].rearrange("p a b -> p (a b)"), P.IdB[:, :])
                S("mx_vg").inc(tr)
                t.wait_ge(S("mx_wte").h, gb + 1)
                t.wait_ge(S("mx_vge").h, gb + 1)
                if gb >= 1:
                    t.wait_ge(S("mx_ct").h, gb)
                for j in range(4):
                    mm = t.matmul(CXPS[:, j, :], VGs[:, j, :], WTs[:, j, :],
                                  start=True, stop=True)
                S("mx_cx").inc(mm)

        @blk.scalar
        def _(s):
            for bt in range(NBT):
                gb = hf * NBT + bt
                t0 = bt * 32
                s.wait_ge(S("mx_e0").h, 2 * gb + 2)
                s.activation(EXb[:, :, :], ESb[:, :, :], AF.Exp, scale=f32(RSQD))
                S("mx_ex").dinc(s)
                s.wait_ge(S("mx_wt").h, gb + 1)
                s.activation(WTs[:, :, :], WTPS[:, :, :], AF.Identity,
                             bias=f32(0.0), scale=f32(1.0))
                S("mx_wte").dinc(s)
                s.wait_ge(S("mx_vg").h, gb + 1)
                s.activation(VGs[:, :, :], VGPS[:, :, :], AF.Identity,
                             bias=f32(0.0), scale=f32(1.0))
                S("mx_vge").dinc(s)
                s.wait_ge(S("mx_cx").h, gb + 1)
                s.activation(
                    CTh[:, :, t0:t0 + 32].rearrange("p h (j ti) -> p j ti h", ti=8),
                    CXPS[:, :, :].rearrange("p j (ti h) -> p j ti h", h=16),
                    AF.Identity, bias=f32(0.0), scale=f32(1.0))
                S("mx_ct").dinc(s)


_WO_CALL = [0]


def _build_wo(nc, S, E, P, hf, CTh, BOB):
    TPH = NT // 2
    tt0 = hf * TPH
    call = _WO_CALL[0]
    _WO_CALL[0] += 1
    ph = contextlib.ExitStack()
    sb = lambda n, sh, d: ph.enter_context(nc.sbuf_tensor(n, sh, d))
    WOB = sb(f"WOB{call}", [128, KC, D], BF16)
    XR = sb(f"XR{call}", [128, 2, D], F32)
    FR = sb(f"FR{call}", [128, 2, D], BF16)
    OB = sb(f"OB{call}", [128, 2, D], F32)
    OPS = ph.enter_context(nc.psum_tensor(f"OPS{call}", [128, 2, D], F32))
    with ph, nc.Block() as blk:
        @blk.gpsimd
        def _(g):
            for kc in range(KC):
                S("p4_wol").inc(g.dma_start(
                    WOB[:, kc, :], E.wo[kc * 128:(kc + 1) * 128, :]), 16)

        @blk.sync
        def _(sync):
            if call == 0:
                sync.wait_ge(S("p3_bo").h, 16)
            for ti in range(TPH):
                i = call * TPH + ti
                tt = tt0 + ti
                if ti >= 2:
                    sync.wait_ge(S("p4_oev").h, 3 * (i - 1))
                S("p4_xrl").inc(sync.dma_start(
                    XR[:, ti % 2, :], E.x[tt * 128:(tt + 1) * 128, :]), 16)
                S("p4_frl").inc(sync.dma_start(
                    FR[:, ti % 2, :], E.ffn_s[tt * 128:(tt + 1) * 128, :]), 16)

        @blk.tensor
        def _(t):
            t.wait_ge(S("p4_wol").h, 16 * KC * (call + 1))
            for ti in range(TPH):
                i = call * TPH + ti
                if ti >= 2:
                    t.wait_ge(S("p4_oev").h, 3 * (i - 1))
                for kc in range(KC):
                    for oc4 in range(4):
                        mm = t.matmul(OPS[:, ti % 2, oc4 * 512:(oc4 + 1) * 512],
                                      CTh[:, kc, ti * 128:(ti + 1) * 128],
                                      WOB[:, kc, oc4 * 512:(oc4 + 1) * 512],
                                      start=(kc == 0), stop=(kc == KC - 1))
                S("p4_omm").inc(mm)

        @blk.vector
        def _(v):
            for ti in range(TPH):
                i = call * TPH + ti
                v.wait_ge(S("p4_omm").h, i + 1)
                v.wait_ge(S("p4_xrl").h, 16 * (i + 1))
                v.wait_ge(S("p4_frl").h, 16 * (i + 1))
                if i >= 2:
                    v.wait_ge(S("p4_ost").h, 16 * (i - 1))
                v.scalar_tensor_tensor(
                    OB[:, ti % 2, :], OPS[:, ti % 2, :], 1.0, BOB[:, :], OP.mult, OP.add)
                S("p4_oev").dinc(v)
                v.tensor_tensor(
                    OB[:, ti % 2, :], OB[:, ti % 2, :], XR[:, ti % 2, :], OP.add)
                S("p4_oev").dinc(v)
                v.tensor_tensor(
                    OB[:, ti % 2, :], OB[:, ti % 2, :], FR[:, ti % 2, :], OP.add)
                S("p4_oev").dinc(v)

        @blk.scalar
        def _(s):
            for ti in range(TPH):
                i = call * TPH + ti
                tt = tt0 + ti
                s.wait_ge(S("p4_oev").h, 3 * (i + 1))
                S("p4_ost").inc(s.dma_start(
                    E.out[tt * 128:(tt + 1) * 128, :], OB[:, ti % 2, :]), 16)
            if call == 1:
                s.wait_ge(S("p4_ost").h, 16 * NT)


def _build_p34(nc, S, E, P):
    outer = contextlib.ExitStack()
    CTh = outer.enter_context(nc.sbuf_tensor("CTh", [128, KC, T // 2], BF16))
    BOB = outer.enter_context(nc.sbuf_tensor("BOB", [128, D], F32))
    with outer:
        ph = contextlib.ExitStack()
        sb = lambda n, sh, d: ph.enter_context(nc.sbuf_tensor(n, sh, d))
        XTh = sb("XTh", [128, NT // 2, KC, 128], BF16)
        QTh = sb("QTh", [128, KC, T // 2], BF16)
        KTh = sb("KTh", [128, KC, T // 2], BF16)
        VTh = sb("VTh", [128, KC, T // 2], BF16)
        WQP = sb("WQP", [128, 2, KC, 128], BF16)
        LQ = sb("LQ", [128, 4, 8, 16], BF16)
        LK = sb("LK", [128, 4, 8, 16], BF16)
        LG = sb("LG", [128, 4, 8, 16], BF16)
        ESb = sb("ESb", [128, 4, 128], F32)
        EXb = sb("EXb", [128, 4, 128], F32)
        RMX = sb("RMX", [128, 4], F32)
        RSM = sb("RSM", [128, 4], F32)
        WBt = sb("WBt", [128, 4, 128], BF16)
        WTs = sb("WTs", [128, 4, 128], BF16)
        VGs = sb("VGs", [128, 4, 128], BF16)
        with ph:
            for hf in range(2):
                _build_qkv(nc, S, E, P, hf, XTh, QTh, KTh, VTh, WQP, BOB)
                _build_mix(nc, S, P, hf, QTh, KTh, VTh, CTh, LQ, LK, LG,
                           ESb, EXb, RMX, RSM, WBt, WTs, VGs)
                if hf == 0:
                    with nc.Block() as blk:
                        @blk.gpsimd
                        def _(g):
                            S("ct_sp").inc(g.dma_start(
                                bass.AP(E.ct_s, 0, [[KC * (T // 2), 128], [T // 2, KC], [1, T // 2]]),
                                CTh[:, :, :]), 16)
                            g.wait_ge(S("ct_sp").h, 16)
        _build_wo(nc, S, E, P, 1, CTh, BOB)
        with nc.Block() as blk:
            @blk.gpsimd
            def _(g):
                S("ct_sp").inc(g.dma_start(
                    CTh[:, :, :],
                    bass.AP(E.ct_s, 0, [[KC * (T // 2), 128], [T // 2, KC], [1, T // 2]])), 16)
                g.wait_ge(S("ct_sp").h, 32)
        _build_wo(nc, S, E, P, 0, CTh, BOB)


class _NS:
    pass


def build_program():
    nc = bass.Bass("TRN2", target_bir_lowering=False, detect_race_conditions=False)
    E = _NS()
    E.x = nc.dram_tensor("x", [T, D], F32, kind="ExternalInput")
    E.wq = nc.dram_tensor("wq", [D, D], F32, kind="ExternalInput")
    E.wk = nc.dram_tensor("wk", [D, D], F32, kind="ExternalInput")
    E.wv = nc.dram_tensor("wv", [D, D], F32, kind="ExternalInput")
    E.wo = nc.dram_tensor("wo", [D, D], F32, kind="ExternalInput")
    E.bq = nc.dram_tensor("bq", [1, D], F32, kind="ExternalInput")
    E.bk = nc.dram_tensor("bk", [1, D], F32, kind="ExternalInput")
    E.bv = nc.dram_tensor("bv", [1, D], F32, kind="ExternalInput")
    E.bo = nc.dram_tensor("bo", [1, D], F32, kind="ExternalInput")
    E.g2 = nc.dram_tensor("g2", [1, D], F32, kind="ExternalInput")
    E.be2 = nc.dram_tensor("be2", [1, D], F32, kind="ExternalInput")
    E.lnw = nc.dram_tensor("lnw", [1, D], F32, kind="ExternalInput")
    E.lnb = nc.dram_tensor("lnb", [1, D], F32, kind="ExternalInput")
    E.w1 = nc.dram_tensor("w1", [FF, D], F32, kind="ExternalInput")
    E.b1 = nc.dram_tensor("b1", [1, FF], F32, kind="ExternalInput")
    E.w2 = nc.dram_tensor("w2", [D, FF], F32, kind="ExternalInput")
    E.b2 = nc.dram_tensor("b2", [1, D], F32, kind="ExternalInput")
    E.out = nc.dram_tensor("out", [T, D], F32, kind="ExternalOutput")
    kw = {"kind": "ExternalOutput"} if DEBUG_SCRATCH else {}
    E.xt_s = nc.dram_tensor("xt_s", [NT, 128, KC, 128], BF16, **kw)
    E.xffn_s = nc.dram_tensor("xffn_s", [T, D], BF16, **kw)
    E.qh2t_s = nc.dram_tensor("qh2t_s", [NT, 128, FC, 128], BF16, **kw)
    E.ffn_s = nc.dram_tensor("ffn_s", [T, D], BF16, **kw)
    E.ct_s = nc.dram_tensor("ct_s", [128, KC, T // 2], BF16, **kw)
    if DEBUG_SCRATCH:
        E.dbg_qht = nc.dram_tensor("dbg_qht", [128, KC, T], BF16, kind="ExternalOutput")
        E.dbg_qhi = nc.dram_tensor("dbg_qhi", [128, D], BF16, kind="ExternalOutput")
        E.dbg_yb = nc.dram_tensor("dbg_yb", [NT, 128, 512], BF16, kind="ExternalOutput")
        E.dbg_gi = nc.dram_tensor("dbg_gi", [NT, 128, 512], BF16, kind="ExternalOutput")
    else:
        E.dbg_qht = E.dbg_qhi = E.dbg_yb = E.dbg_gi = None

    persist = contextlib.ExitStack()
    sbp = lambda n, sh, d: persist.enter_context(nc.sbuf_tensor(n, sh, d))
    P = _NS()
    P.IdF = sbp("IdF", [128, 128], F32)
    P.IdB = sbp("IdB", [128, 128], BF16)
    P.MASK = sbp("MASK", [128, 128], F32)
    P.BQ = sbp("BQ", [128, KC], F32)
    P.BK = sbp("BK", [128, KC], F32)
    P.BV = sbp("BV", [128, KC], F32)
    P.G2C = sbp("G2C", [128, KC], F32)
    P.B2C = sbp("B2C", [128, KC], F32)

    sems = {}

    def S(name):
        if name not in sems:
            sems[name] = Sem(nc, persist, name)
        return sems[name]

    _build_init(nc, S, E, P)
    with contextlib.ExitStack() as s01:
        QHT = s01.enter_context(nc.sbuf_tensor("QHT", [128, KC, T], I16))
        _build_p0(nc, S, E, P, QHT)
        if BUILD_PHASES == "all":
            _build_p1(nc, S, E, P, QHT)
    if BUILD_PHASES == "all":
        _build_p2(nc, S, E, P)
        _build_p34(nc, S, E, P)
    return nc, persist


_CACHE = {}


def prepare_maps(inputs):
    x = np.ascontiguousarray(np.asarray(inputs["L0_attn_x"], dtype=np.float32).reshape(16384, D))
    w = {
        "wq": inputs["L0_attn_Wq"], "wk": inputs["L0_attn_Wk"],
        "wv": inputs["L0_attn_Wv"], "wo": inputs["L0_attn_Wo"],
        "bq": np.asarray(inputs["L0_attn_bq"]).reshape(1, D),
        "bk": np.asarray(inputs["L0_attn_bk"]).reshape(1, D),
        "bv": np.asarray(inputs["L0_attn_bv"]).reshape(1, D),
        "bo": np.asarray(inputs["L0_attn_bo"]).reshape(1, D),
        "g2": np.asarray(inputs["L0_attn_ln_g"]).reshape(1, D),
        "be2": np.asarray(inputs["L0_attn_ln_b"]).reshape(1, D),
        "lnw": np.asarray(inputs["L0_ffn_ln_w"]).reshape(1, D),
        "lnb": np.asarray(inputs["L0_ffn_ln_b"]).reshape(1, D),
        "w1": inputs["L0_ffn_W1"], "b1": np.asarray(inputs["L0_ffn_b1"]).reshape(1, FF),
        "w2": inputs["L0_ffn_W2"], "b2": np.asarray(inputs["L0_ffn_b2"]).reshape(1, D),
    }
    w = {k: np.ascontiguousarray(np.asarray(v, dtype=np.float32)) for k, v in w.items()}
    in_maps = []
    for c in range(NCORES):
        m = dict(w)
        m["x"] = np.ascontiguousarray(x[c * T:(c + 1) * T, :])
        in_maps.append(m)
    return in_maps


def kernel(**inputs):
    in_maps = prepare_maps(inputs)
    if "nc" not in _CACHE:
        _CACHE["nc"], _CACHE["persist"] = build_program()
    nc = _CACHE["nc"]
    res = run_bass_kernel_spmd(nc, in_maps, core_ids=list(range(NCORES)))
    out = np.concatenate([np.asarray(r["out"]) for r in res.results], axis=0)
    return out.reshape(4, 4096, 2048)


# version 10
# speedup vs baseline: 1.3301x; 1.2893x over previous
import contextlib
import numpy as np
import concourse.bass as bass
import concourse.mybir as mybir
from concourse.bass_utils import run_bass_kernel_spmd

F32 = mybir.dt.float32
BF16 = mybir.dt.bfloat16
I16 = mybir.dt.int16
I32 = mybir.dt.int32
AF = mybir.ActivationFunctionType
OP = mybir.AluOpType
AX = mybir.AxisListType

DEBUG_SCRATCH = False
BUILD_PHASES = "all"
NCORES = 8
T = 2048
D = 2048
FF = 8192
NT = T // 128
KC = D // 128
FC = FF // 128
C2 = 196608.0
MASK_EXP = 0xFF800000 - (1 << 32)
K_RS = float(0x7F000000)
CLIP_HI = 1.984375
EPS = 1e-5
RSQD = float(1.0 / np.float32(np.sqrt(np.float32(2048.0))))
f32 = np.float32


class Sem:
    def __init__(self, nc, es, name):
        self.h = es.enter_context(nc.semaphore(name))
        self.n = 0

    def inc(self, instr, k=1):
        instr.then_inc(self.h, k)
        self.n += k
        return self.n

    def dinc(self, eng, k=1):
        eng.drain().then_inc(self.h, k)
        self.n += k
        return self.n


def quant_chain(v, X_ap, n_el, AM, PB, RS, PBB, RSB, U_ap, V_ap, Q_ap, fp32_input, npart=128):
    nb = n_el // 32
    nb2 = nb // 2
    c2 = n_el // 2
    hs = [(slice(0, nb2), slice(0, c2)), (slice(nb2, nb), slice(c2, n_el))]
    v.drain()
    for bs, cs in hs:
        v.tensor_reduce(AM[:, bs], X_ap[:, cs].rearrange("p (b j) -> p b j", j=32),
                        axis=AX.X, op=OP.max, apply_absolute_value=True)
    for bs, cs in hs:
        v.tensor_single_scalar(PB[:, bs], AM[:, bs].bitcast(I32), MASK_EXP, OP.bitwise_and)
    v.drain()
    for bs, cs in hs:
        v.tensor_scalar(RS[:, bs], PB[:, bs], -1.0, K_RS, OP.mult, OP.add)
    for bs, cs in hs:
        v.tensor_copy(PBB[:, bs], PB[:, bs].bitcast(F32))
    v.drain()
    if not fp32_input:
        for bs, cs in hs:
            v.tensor_copy(RSB[:, bs], RS[:, bs].bitcast(F32))
        v.drain()
    for bs, cs in hs:
        rs_b = (RS[:, bs].bitcast(F32) if fp32_input else RSB[:, bs])
        v.tensor_tensor(U_ap[:, cs].rearrange("p (b j) -> p b j", j=32),
                        X_ap[:, cs].rearrange("p (b j) -> p b j", j=32),
                        rs_b[:, :, None].broadcast_to([npart, nb2, 32]), OP.mult)
    for bs, cs in hs:
        v.tensor_scalar(V_ap[:, cs], U_ap[:, cs], C2, -C2, OP.add, OP.add)
    for bs, cs in hs:
        v.tensor_scalar_min(V_ap[:, cs], V_ap[:, cs], CLIP_HI)
    for bs, cs in hs:
        q = v.tensor_tensor(Q_ap[:, cs].rearrange("p (b j) -> p b j", j=32),
                            V_ap[:, cs].rearrange("p (b j) -> p b j", j=32),
                            PBB[:, bs][:, :, None].broadcast_to([npart, nb2, 32]), OP.mult)
    return q


def _build_init(nc, S, E, P):
    ph = contextlib.ExitStack()
    sb = lambda n, sh, d: ph.enter_context(nc.sbuf_tensor(n, sh, d))
    Pidx = sb("Pidx", [128, 1], I32)
    Cidx = sb("Cidx", [128, 128], I32)
    CBl = sb("CBl", [128, 128], I32)
    TDm = sb("TDm", [128, 128], F32)
    GEm = sb("GEm", [128, 128], F32)
    with ph:
        with nc.Block() as blk:
            @blk.gpsimd
            def _(g):
                g.iota(Pidx[:, :], [[1, 1]], channel_multiplier=1)
                g.iota(Cidx[:, :], [[1, 128]], channel_multiplier=0)
                g.iota(CBl[:, :].rearrange("p (a b) -> p a b", b=16),
                       [[1, 8], [0, 16]], channel_multiplier=0)

            @blk.sync
            def _(sync):
                ldi = S("ldi")
                for dst, src in ((P.BQ, E.bq), (P.BK, E.bk), (P.BV, E.bv),
                                 (P.G2C, E.g2), (P.B2C, E.be2)):
                    ldi.inc(sync.dma_start(
                        dst[:, :], src[0:1, :].rearrange("o (c p) -> p (o c)", p=128),
                        allow_slow_non_contiguous=True), 16)
                sync.wait_ge(ldi.h, ldi.n)

        with nc.Block() as blk:
            @blk.vector
            def _(v):
                v.tensor_tensor(P.IdF[:, :], Cidx[:, :],
                                Pidx[:, :].broadcast_to([128, 128]), OP.is_equal)
                v.drain()
                v.tensor_copy(P.IdB[:, :], P.IdF[:, :])
                v.tensor_single_scalar(TDm[:, 0:1], Pidx[:, :], 0.0, OP.is_le)
                v.drain()
                v.tensor_copy(P.ONES1[:, :], TDm[:, 0:1].broadcast_to([128, 128]))
                v.drain()
                v.tensor_scalar(TDm[:, :], CBl[:, :], -16.0, None, OP.mult)
                v.drain()
                v.tensor_tensor(TDm[:, :], Pidx[:, :].broadcast_to([128, 128]), TDm[:, :], OP.add)
                v.drain()
                v.tensor_single_scalar(GEm[:, :], TDm[:, :], 0.0, OP.is_ge)
                v.drain()
                v.tensor_single_scalar(TDm[:, :], TDm[:, :], 15.0, OP.is_le)
                v.drain()
                v.tensor_tensor(GEm[:, :], GEm[:, :], TDm[:, :], OP.logical_and)
                v.drain()
                v.tensor_scalar(P.MASK[:, :], GEm[:, :], 1e30, -1e30, OP.mult, OP.add)


def _build_p0(nc, S, E, P, QHT):
    ph = contextlib.ExitStack()
    sb = lambda n, sh, d: ph.enter_context(nc.sbuf_tensor(n, sh, d))
    ps = lambda n, sh, d: ph.enter_context(nc.psum_tensor(n, sh, d))
    XIN = sb("XIN", [128, 2, D], F32)
    XN = sb("XN", [128, D], F32)
    T2 = sb("T2", [128, D], F32)
    T3 = sb("T3", [128, D], F32)
    H3T = sb("H3T", [128, D], F32)
    XFN = sb("XFN", [128, D], I16)
    XFB = sb("XFB", [128, 2, D], BF16)
    LNW = sb("LNW", [128, D], F32)
    LNB = sb("LNB", [128, D], F32)
    HQ = sb("HQ", [128, D], I16)
    UQ = sb("UQ", [128, D], I16)
    VQ = sb("VQ", [128, D], I16)
    QHI = sb("QHI", [128, D], I16)
    AM0 = sb("AM0", [128, 64], F32)
    PB0 = sb("PB0", [128, 64], I32)
    RS0 = sb("RS0", [128, 64], I32)
    PBB0 = sb("PBB0", [128, 64], BF16)
    RSB0 = sb("RSB0", [128, 64], BF16)
    STt = sb("STt", [128, 24], F32)
    MVt = sb("MVt", [128, 2], F32)
    VARt = sb("VARt", [128, 1], F32)
    SQt = sb("SQt", [128, 1], F32)
    R0t = sb("R0t", [128, 1], F32)
    A1t = sb("A1t", [128, 1], F32)
    RRt = sb("RRt", [128, 3], F32)
    NBt = sb("NBt", [128, 3], F32)
    XTB = sb("XTB", [128, 2, KC, 128], BF16)
    TRP = ps("TRP", [128, KC, 128], F32)
    TRQ = ps("TRQ", [128, KC, 128], BF16)

    def stats(v, src_ap):
        v.drain()
        for c in range(4):
            v.bn_stats(STt[:, 6 * c:6 * c + 6], src_ap[:, 512 * c:512 * (c + 1)])
        v.drain()
        v.bn_aggr(MVt[:, :].rearrange("p (a t) -> p a t", t=2),
                  STt[:, :].rearrange("p (g t) -> p g t", t=3))
        v.drain()
        return v.tensor_single_scalar(VARt[:, :], MVt[:, 1:2], EPS, OP.add)

    def rstd(v, k):
        v.reciprocal(RRt[:, k:k + 1], SQt[:, :])
        v.drain()
        v.tensor_tensor(NBt[:, k:k + 1], MVt[:, 0:1], RRt[:, k:k + 1], OP.mult)
        v.drain()
        return v.tensor_single_scalar(NBt[:, k:k + 1], NBt[:, k:k + 1], -1.0, OP.mult)

    with ph, nc.Block() as blk:
        @blk.sync
        def _(sync):
            ldw = S("p0_lnwb")
            ldw.inc(sync.dma_start(LNW[:, :], E.lnw[0:1, :].partition_broadcast(128)), 16)
            ldw.inc(sync.dma_start(LNB[:, :], E.lnb[0:1, :].partition_broadcast(128)), 16)
            for tt in range(NT):
                if tt >= 2:
                    sync.wait_ge(S("p0_xn").h, tt - 1)
                S("p0_ldx").inc(sync.dma_start(
                    XIN[:, tt % 2, :], E.x[tt * 128:(tt + 1) * 128, :]), 16)

        @blk.vector
        def _(v):
            for tt in range(NT):
                b = tt % 2
                v.wait_ge(S("p0_ldx").h, 16 * (tt + 1))
                if tt == 0:
                    v.wait_ge(S("p0_lnwb").h, 32)
                stats(v, XIN[:, b, :])
                S("p0_var").dinc(v)
                v.wait_ge(S("p0_sqd").h, 3 * tt + 1)
                rstd(v, 0)
                S("p0_r").dinc(v)
                v.wait_ge(S("p0_xn").h, tt + 1)
                stats(v, XN[:, :])
                S("p0_var").dinc(v)
                v.wait_ge(S("p0_sqd").h, 3 * tt + 2)
                rstd(v, 1)
                if tt >= 2:
                    v.wait_ge(S("p0_t3").h, tt - 1)
                    v.wait_ge(S("p0_xfs").h, 16 * (tt - 1))
                v.drain()
                v.tensor_copy(XFN[:, :].bitcast(BF16), XN[:, :])
                S("p0_r").dinc(v)
                v.tensor_copy(XFB[:, b, :], XFN[:, :].bitcast(BF16))
                S("p0_xfb").dinc(v)
                stats(v, XFB[:, b, :])
                S("p0_var").dinc(v)
                v.wait_ge(S("p0_sqd").h, 3 * tt + 3)
                rstd(v, 2)
                S("p0_r").dinc(v)
                v.wait_ge(S("p0_tr1").h, tt + 1)
                if tt >= 2:
                    v.wait_ge(S("p0_xts").h, 16 * (tt - 1))
                for c in range(KC):
                    v.tensor_scalar(XTB[:, b, c, :], TRP[:, c, :],
                                    P.G2C[:, c:c + 1], P.B2C[:, c:c + 1], OP.mult, OP.add)
                S("p0_xtb").dinc(v)
                v.wait_ge(S("p0_t3").h, tt + 1)
                v.scalar_tensor_tensor(H3T[:, :], T3[:, :], 1.0, LNW[:, :], OP.mult, OP.mult)
                v.drain()
                v.scalar_tensor_tensor(HQ[:, :].bitcast(BF16), H3T[:, :], 1.0, LNB[:, :],
                                       OP.mult, OP.add)
                if tt >= 1:
                    v.wait_ge(S("p0_trq").h, tt)
                if DEBUG_SCRATCH and tt == 1:
                    v.wait_ge(S("p0_dbg").h, 16)
                quant_chain(v, HQ[:, :].bitcast(BF16), D, AM0, PB0, RS0, PBB0, RSB0,
                            UQ[:, :].bitcast(BF16), VQ[:, :].bitcast(BF16),
                            QHI[:, :].bitcast(BF16), fp32_input=False)
                S("p0_qh").dinc(v)

        @blk.scalar
        def _(s):
            for tt in range(NT):
                b = tt % 2
                for k in range(3):
                    s.wait_ge(S("p0_var").h, 3 * tt + k + 1)
                    s.activation(SQt[:, :], VARt[:, :], AF.Sqrt,
                                 bias=f32(0.0), scale=f32(1.0))
                    S("p0_sqd").dinc(s)
                    if k == 0:
                        s.wait_ge(S("p0_r").h, 3 * tt + 1)
                        s.activation(XN[:, :], XIN[:, b, :], AF.Identity,
                                     bias=NBt[:, 0:1], scale=RRt[:, 0:1])
                        S("p0_xn").dinc(s)
                    elif k == 1:
                        s.wait_ge(S("p0_r").h, 3 * tt + 2)
                        s.activation(T2[:, :], XN[:, :], AF.Identity,
                                     bias=NBt[:, 1:2], scale=RRt[:, 1:2])
                        S("p0_t2").dinc(s)
                    else:
                        s.wait_ge(S("p0_r").h, 3 * tt + 3)
                        s.activation(T3[:, :], XFB[:, b, :], AF.Identity,
                                     bias=NBt[:, 2:3], scale=RRt[:, 2:3])
                        S("p0_t3").dinc(s)
                s.wait_ge(S("p0_trq").h, tt + 1)
                s.activation(
                    QHT[:, :, tt * 128:(tt + 1) * 128].bitcast(BF16),
                    TRQ[:, :, :], AF.Identity, bias=f32(0.0), scale=f32(1.0))
                S("p0_qht").dinc(s)

        @blk.tensor
        def _(t):
            for tt in range(NT):
                t.wait_ge(S("p0_t2").h, tt + 1)
                if tt >= 1:
                    t.wait_ge(S("p0_xtb").h, tt)
                for c in range(KC):
                    tr = t.transpose(TRP[:, c, :], T2[:, c * 128:(c + 1) * 128], P.IdF[:, :])
                S("p0_tr1").inc(tr)
                t.wait_ge(S("p0_qh").h, tt + 1)
                if tt >= 1:
                    t.wait_ge(S("p0_qht").h, tt)
                for c in range(KC):
                    tr = t.transpose(TRQ[:, c, :],
                                     QHI[:, c * 128:(c + 1) * 128].bitcast(BF16), P.IdB[:, :])
                S("p0_trq").inc(tr)

        @blk.gpsimd
        def _(g):
            if DEBUG_SCRATCH:
                g.wait_ge(S("p0_qh").h, 1)
                S("p0_dbg").inc(g.dma_start(E.dbg_qhi[:, :], QHI[:, :].bitcast(BF16)), 16)
            for tt in range(NT):
                b = tt % 2
                g.wait_ge(S("p0_xtb").h, tt + 1)
                S("p0_xts").inc(g.dma_start(
                    bass.AP(E.xt_s, tt * 128 * KC * 128,
                            [[KC * 128, 128], [128, KC], [1, 128]]),
                    XTB[:, b, :, :]), 16)
                g.wait_ge(S("p0_xfb").h, tt + 1)
                S("p0_xfs").inc(g.dma_start(
                    E.xffn_s[tt * 128:(tt + 1) * 128, :], XFB[:, b, :]), 16)
            if DEBUG_SCRATCH:
                g.wait_ge(S("p0_qht").h, NT)
                S("p0_dbg").inc(g.dma_start(
                    E.dbg_qht[:, :, :].rearrange("p c t -> p (c t)"),
                    QHT[:, :, :].bitcast(BF16).rearrange("p c t -> p (c t)")), 16)
                g.wait_ge(S("p0_dbg").h, 32)
            g.wait_ge(S("p0_xts").h, 16 * NT)
            g.wait_ge(S("p0_xfs").h, 16 * NT)


def _build_p1(nc, S, E, P, QHT):
    NPAN = FF // 512
    ph = contextlib.ExitStack()
    sb = lambda n, sh, d: ph.enter_context(nc.sbuf_tensor(n, sh, d))
    ps = lambda n, sh, d: ph.enter_context(nc.psum_tensor(n, sh, d))
    W1P = sb("W1P", [128, 2, 2, D], F32)
    U1 = sb("U1", [128, 2, D], F32)
    V1 = sb("V1", [128, 2, D], I16)
    Q1P = sb("Q1P", [128, 2, D], I16)
    AM1 = sb("AM1", [128, 128], F32)
    PB1 = sb("PB1", [128, 128], I32)
    RS1 = sb("RS1", [128, 128], I32)
    PBB1 = sb("PBB1", [128, 128], BF16)
    RSB1 = sb("RSB1", [128, 128], BF16)
    QW1 = sb("QW1", [128, 2, KC, 512], I16)
    BH1 = sb("BH1", [128, 2, 512], BF16)
    BL1 = sb("BL1", [128, 2, 512], BF16)
    B1F = sb("B1F", [1, 2, 512], F32)
    YB = sb("YB", [128, 512], I16)
    GI = sb("GI", [128, 4, 512], I16)
    UG = sb("UG", [128, 512], I16)
    VG = sb("VG", [128, 512], I16)
    QG = sb("QG", [128, NT, 512], I16)
    AMG = sb("AMG", [128, 16], F32)
    PBG = sb("PBG", [128, 16], I32)
    RSG = sb("RSG", [128, 16], I32)
    PBBG = sb("PBBG", [128, 16], BF16)
    RSBG = sb("RSBG", [128, 16], BF16)
    QH2B = sb("QH2B", [128, 4, 4, 128], I16)
    Y1PS = ps("Y1PS", [128, 4, 512], F32)
    W1TR = ps("W1TR", [128, KC, 128], BF16)
    H2TR = ps("H2TR", [128, 2, 4, 256], BF16)

    with ph, nc.Block() as blk:
        @blk.sync
        def _(sync):
            for pn in range(NPAN):
                for r in range(2):
                    idx = pn * 2 + r
                    if idx >= 2:
                        sync.wait_ge(S("p1_w1q").h, idx - 1)
                    S("p1_w1l").inc(sync.dma_start(
                        W1P[:, idx % 2, :, :],
                        E.w1[pn * 512 + r * 256: pn * 512 + (r + 1) * 256, :]
                        .rearrange("(rr p) k -> p rr k", p=128)), 16)
                if pn >= 2:
                    sync.wait_ge(S("p1_mm").h, (pn - 1) * NT)
                S("p1_b1l").inc(sync.dma_start(
                    B1F[0:1, pn % 2, :], E.b1[0:1, pn * 512:(pn + 1) * 512]), 16)

        @blk.vector
        def _(v):
            v.memset(BH1[:, :, :], 0.0)
            v.memset(BL1[:, :, :], 0.0)
            S("p1_bz").dinc(v)
            for pn in range(NPAN):
                v.wait_ge(S("p1_b1l").h, 16 * (pn + 1))
                if pn >= 2:
                    v.wait_ge(S("p1_mm").h, (pn - 1) * NT)
                v.tensor_copy(BH1[0:1, pn % 2, :], B1F[0:1, pn % 2, :])
                v.drain()
                v.tensor_tensor(BL1[0:1, pn % 2, :], B1F[0:1, pn % 2, :],
                                BH1[0:1, pn % 2, :], OP.subtract)
                S("p1_blo").dinc(v)
                for r in range(2):
                    idx = pn * 2 + r
                    v.wait_ge(S("p1_w1l").h, 16 * (idx + 1))
                    if idx >= 1:
                        v.wait_ge(S("p1_w1t").h, 2 * idx)
                    quant_chain(
                        v, W1P[:, idx % 2, :, :].rearrange("p r k -> p (r k)"), 2 * D,
                        AM1, PB1, RS1, PBB1, RSB1,
                        U1[:, :, :].rearrange("p r k -> p (r k)"),
                        V1[:, :, :].rearrange("p r k -> p (r k)").bitcast(BF16),
                        Q1P[:, :, :].rearrange("p r k -> p (r k)").bitcast(BF16),
                        fp32_input=True)
                    S("p1_w1q").dinc(v)
                for tt in range(NT):
                    i = pn * NT + tt
                    if i >= 1:
                        v.wait_ge(S("p1_gi").h, i)
                    if i >= NT + 1:
                        v.wait_ge(S("p1_h2t").h, i - NT)
                    if DEBUG_SCRATCH and pn == 0 and tt == 1:
                        v.wait_ge(S("p1_dbg").h, 16)
                    if i >= 1:
                        quant_chain(v, GI[:, (i - 1) % 4, :].bitcast(BF16), 512,
                                    AMG, PBG, RSG, PBBG, RSBG,
                                    UG[:, :].bitcast(BF16), VG[:, :].bitcast(BF16),
                                    QG[:, (i - 1) % NT, :].bitcast(BF16), fp32_input=False)
                        S("p1_qg").dinc(v)
                    if i == NPAN * NT - 1:
                        v.wait_ge(S("p1_gi").h, i + 1)
                        quant_chain(v, GI[:, i % 4, :].bitcast(BF16), 512,
                                    AMG, PBG, RSG, PBBG, RSBG,
                                    UG[:, :].bitcast(BF16), VG[:, :].bitcast(BF16),
                                    QG[:, i % NT, :].bitcast(BF16), fp32_input=False)
                        S("p1_qg").dinc(v)

        @blk.tensor
        def _(t):
            for pn in range(NPAN):
                for r in range(2):
                    idx = pn * 2 + r
                    t.wait_ge(S("p1_w1q").h, idx + 1)
                    for pr in range(2):
                        gidx = idx * 2 + pr
                        if gidx >= 1:
                            t.wait_ge(S("p1_w1e").h, gidx)
                        for c in range(KC):
                            tr = t.transpose(W1TR[:, c, :],
                                             Q1P[:, pr, c * 128:(c + 1) * 128].bitcast(BF16),
                                             P.IdB[:, :])
                        S("p1_w1t").inc(tr)
                def h2_tr(t, j):
                    t.wait_ge(S("p1_qg").h, j + 1)
                    if j >= 2:
                        t.wait_ge(S("p1_h2e").h, j - 1)
                    for c in range(4):
                        tr = t.transpose(H2TR[:, j % 2, c, 0:128],
                                         QG[:, j % NT, c * 128:(c + 1) * 128].bitcast(BF16),
                                         P.IdB[:, :])
                    S("p1_h2t").inc(tr)

                for tt in range(NT):
                    i = pn * NT + tt
                    t.wait_ge(S("p1_w1e").h, 4 * (pn + 1))
                    if tt == 0:
                        t.wait_ge(S("p1_bz").h, 1)
                        t.wait_ge(S("p1_blo").h, pn + 1)
                    if i >= 4:
                        t.wait_ge(S("p1_yb").h, i - 3)
                    for kc in range(KC):
                        mm = t.matmul(Y1PS[:, tt % 4, :],
                                      QHT[:, kc, tt * 128:(tt + 1) * 128].bitcast(BF16),
                                      QW1[:, pn % 2, kc, :].bitcast(BF16),
                                      start=(kc == 0), stop=False)
                    t.matmul(Y1PS[:, tt % 4, :], P.ONES1[:, :], BH1[:, pn % 2, :],
                             start=False, stop=False)
                    mm = t.matmul(Y1PS[:, tt % 4, :], P.ONES1[:, :], BL1[:, pn % 2, :],
                                  start=False, stop=True)
                    S("p1_mm").inc(mm)
                    if i >= 2:
                        h2_tr(t, i - 2)
                    if i == NPAN * NT - 1:
                        h2_tr(t, i - 1)
                        h2_tr(t, i)

        @blk.scalar
        def _(s):
            for pn in range(NPAN):
                for r in range(2):
                    idx = pn * 2 + r
                    for pr in range(2):
                        gidx = idx * 2 + pr
                        s.wait_ge(S("p1_w1t").h, gidx + 1)
                        if pn >= 2 and r == 0 and pr == 0:
                            s.wait_ge(S("p1_mm").h, (pn - 1) * NT)
                        s.activation(
                            QW1[:, pn % 2, :,
                                (r * 2 + pr) * 128:(r * 2 + pr + 1) * 128].bitcast(BF16),
                            W1TR[:, :, :], AF.Identity, bias=f32(0.0), scale=f32(1.0))
                        S("p1_w1e").dinc(s)
                for tt in range(NT):
                    i = pn * NT + tt
                    s.wait_ge(S("p1_mm").h, i + 1)
                    if i >= 4:
                        s.wait_ge(S("p1_qg").h, i - 3)
                    if DEBUG_SCRATCH and pn == 0 and tt == 1:
                        s.wait_ge(S("p1_dbg").h, 32)
                    s.activation(YB[:, :].bitcast(BF16), Y1PS[:, tt % 4, :],
                                 AF.Identity, bias=f32(0.0), scale=f32(1.0))
                    S("p1_yb").dinc(s)
                    s.activation(GI[:, i % 4, :].bitcast(BF16), YB[:, :].bitcast(BF16),
                                 AF.Gelu_apprx_tanh)
                    S("p1_gi").dinc(s)

                    def h2_ev(s, j):
                        s.wait_ge(S("p1_h2t").h, j + 1)
                        if j >= 4:
                            s.wait_ge(S("p1_h2s").h, 16 * (j - 3))
                        s.activation(QH2B[:, j % 4, :, :].bitcast(BF16),
                                     H2TR[:, j % 2, :, 0:128], AF.Identity,
                                     bias=f32(0.0), scale=f32(1.0))
                        S("p1_h2e").dinc(s)

                    if i >= 2:
                        h2_ev(s, i - 2)
                    if i == NPAN * NT - 1:
                        h2_ev(s, i - 1)
                        h2_ev(s, i)

        @blk.gpsimd
        def _(g):
            if DEBUG_SCRATCH:
                g.wait_ge(S("p1_yb").h, 1)
                S("p1_dbg").inc(g.dma_start(E.dbg_yb[0, :, :], YB[:, :].bitcast(BF16)), 16)
                g.wait_ge(S("p1_gi").h, 1)
                S("p1_dbg").inc(g.dma_start(E.dbg_gi[0, :, :], GI[:, 0, :].bitcast(BF16)), 16)
            for pn in range(NPAN):
                for tt in range(NT):
                    i = pn * NT + tt
                    g.wait_ge(S("p1_h2e").h, i + 1)
                    S("p1_h2s").inc(g.dma_start(
                        bass.AP(E.qh2t_s, tt * 128 * FC * 128 + pn * 4 * 128,
                                [[FC * 128, 128], [128, 4], [1, 128]]),
                        QH2B[:, i % 4, :, :].bitcast(BF16)), 16)
            g.wait_ge(S("p1_h2s").h, 16 * NPAN * NT)


def _build_p2(nc, S, E, P):
    NQ = 4
    NW2 = 4
    ph = contextlib.ExitStack()
    sb = lambda n, sh, d: ph.enter_context(nc.sbuf_tensor(n, sh, d))
    ps = lambda n, sh, d: ph.enter_context(nc.psum_tensor(n, sh, d))
    W2P = sb("W2P", [128, FF], F32)
    U2 = sb("U2", [128, FF // 2], F32)
    V2 = sb("V2", [128, FF], I16)
    Q2P = sb("Q2P", [128, FF], I16)
    AM2 = sb("AM2", [128, 128], F32)
    PB2 = sb("PB2", [128, 128], I32)
    RS2 = sb("RS2", [128, 128], I32)
    PBB2 = sb("PBB2", [128, 128], BF16)
    RSB2 = sb("RSB2", [128, 128], BF16)
    QW2 = sb("QW2", [128, FC, 512], I16)
    BH2 = sb("BH2", [128, 512], BF16)
    BL2 = sb("BL2", [128, 512], BF16)
    B2F = sb("B2F", [1, 512], F32)
    QH2L = sb("QH2L", [128, 2, FC, 128], I16)
    HB2 = sb("HB2", [128, 2, 512], I16)
    XFQ = sb("XFQ", [128, 4, 512], BF16)
    FFB = sb("FFB", [128, 4, 512], I16)
    Y2PS = ps("Y2PS", [128, 4, 512], F32)
    W2TR = ps("W2TR", [128, 16, 128], BF16)

    with ph, nc.Block() as blk:
        @blk.sync
        def _(sync):
            for q in range(NQ):
                if q >= 1:
                    sync.wait_ge(S("p2_mm").h, q * NT)
                S("p2_b2l").inc(sync.dma_start(
                    B2F[0:1, :], E.b2[0:1, q * 512:(q + 1) * 512]), 16)
                for pw in range(NW2):
                    idx = q * NW2 + pw
                    if idx >= 1:
                        sync.wait_ge(S("p2_w2q").h, idx)
                    S("p2_w2l").inc(sync.dma_start(
                        W2P[:, :],
                        E.w2[q * 512 + pw * 128: q * 512 + (pw + 1) * 128, :]), 16)
                for tt in range(NT):
                    i = q * NT + tt
                    if i >= 2:
                        sync.wait_ge(S("p2_mm").h, i - 1)
                    S("p2_sl").inc(sync.dma_start(
                        QH2L[:, tt % 2, :, :].bitcast(BF16),
                        bass.AP(E.qh2t_s, tt * 128 * FC * 128,
                                [[FC * 128, 128], [128, FC], [1, 128]])), 16)
                    if i >= 4:
                        sync.wait_ge(S("p2_ff").h, i - 3)
                    S("p2_xfl").inc(sync.dma_start(
                        XFQ[:, i % 4, :],
                        E.xffn_s[tt * 128:(tt + 1) * 128, q * 512:(q + 1) * 512]), 16)

        @blk.vector
        def _(v):
            v.memset(BH2[:, :], 0.0)
            v.memset(BL2[:, :], 0.0)
            S("p2_bz").dinc(v)
            for q in range(NQ):
                v.wait_ge(S("p2_b2l").h, 16 * (q + 1))
                if q >= 1:
                    v.wait_ge(S("p2_mm").h, q * NT)
                v.tensor_copy(BH2[0:1, :], B2F[0:1, :])
                v.drain()
                v.tensor_tensor(BL2[0:1, :], B2F[0:1, :], BH2[0:1, :], OP.subtract)
                S("p2_blo").dinc(v)
                for pw in range(NW2):
                    idx = q * NW2 + pw
                    v.wait_ge(S("p2_w2l").h, 16 * (idx + 1))
                    if idx >= 1:
                        v.wait_ge(S("p2_w2t").h, (FC // 16) * idx)
                    hw = FF // 2
                    for hh in range(2):
                        qq = quant_chain(v, W2P[:, hh * hw:(hh + 1) * hw], hw,
                                         AM2, PB2, RS2, PBB2, RSB2,
                                         U2[:, :], V2[:, hh * hw:(hh + 1) * hw].bitcast(BF16),
                                         Q2P[:, hh * hw:(hh + 1) * hw].bitcast(BF16),
                                         fp32_input=True)
                    S("p2_w2q").dinc(v)
                for tt in range(NT):
                    i = q * NT + tt
                    v.wait_ge(S("p2_hb").h, i + 1)
                    v.wait_ge(S("p2_xfl").h, 16 * (i + 1))
                    if i >= 4:
                        v.wait_ge(S("p2_ffs").h, 16 * (i - 3))
                    v.tensor_tensor(FFB[:, i % 4, :].bitcast(BF16),
                                    HB2[:, i % 2, :].bitcast(BF16), XFQ[:, i % 4, :], OP.add)
                    S("p2_ff").dinc(v)

        @blk.tensor
        def _(t):
            for q in range(NQ):
                for pw in range(NW2):
                    idx = q * NW2 + pw
                    t.wait_ge(S("p2_w2q").h, idx + 1)
                    for oc8 in range(FC // 16):
                        gg = (FC // 16) * idx + oc8
                        if gg >= 1:
                            t.wait_ge(S("p2_w2e").h, gg)
                        for c in range(16):
                            oc = oc8 * 16 + c
                            tr = t.transpose(W2TR[:, c, :],
                                             Q2P[:, oc * 128:(oc + 1) * 128].bitcast(BF16),
                                             P.IdB[:, :])
                        S("p2_w2t").inc(tr)
                for tt in range(NT):
                    i = q * NT + tt
                    t.wait_ge(S("p2_w2e").h, (FC // 16) * NW2 * (q + 1))
                    t.wait_ge(S("p2_sl").h, 16 * (i + 1))
                    if tt == 0:
                        t.wait_ge(S("p2_bz").h, 1)
                        t.wait_ge(S("p2_blo").h, q + 1)
                    if i >= 4:
                        t.wait_ge(S("p2_hb").h, i - 3)
                    for oc in range(FC):
                        mm = t.matmul(Y2PS[:, i % 4, :],
                                      QH2L[:, tt % 2, oc, :].bitcast(BF16),
                                      QW2[:, oc, :].bitcast(BF16),
                                      start=(oc == 0), stop=False)
                    t.matmul(Y2PS[:, i % 4, :], P.ONES1[:, :], BH2[:, :], start=False, stop=False)
                    mm = t.matmul(Y2PS[:, i % 4, :], P.ONES1[:, :], BL2[:, :], start=False, stop=True)
                    S("p2_mm").inc(mm)

        @blk.scalar
        def _(s):
            for q in range(NQ):
                for pw in range(NW2):
                    idx = q * NW2 + pw
                    for oc8 in range(FC // 16):
                        gg = (FC // 16) * idx + oc8
                        s.wait_ge(S("p2_w2t").h, gg + 1)
                        if q >= 1 and pw == 0 and oc8 == 0:
                            s.wait_ge(S("p2_mm").h, q * NT)
                        s.activation(
                            QW2[:, oc8 * 16:(oc8 + 1) * 16, pw * 128:(pw + 1) * 128].bitcast(BF16),
                            W2TR[:, :, :], AF.Identity, bias=f32(0.0), scale=f32(1.0))
                        S("p2_w2e").dinc(s)
                for tt in range(NT):
                    i = q * NT + tt
                    s.wait_ge(S("p2_mm").h, i + 1)
                    if i >= 2:
                        s.wait_ge(S("p2_ff").h, i - 1)
                    s.activation(HB2[:, i % 2, :].bitcast(BF16), Y2PS[:, i % 4, :],
                                 AF.Identity, bias=f32(0.0), scale=f32(1.0))
                    S("p2_hb").dinc(s)

        @blk.gpsimd
        def _(g):
            for q in range(NQ):
                for tt in range(NT):
                    i = q * NT + tt
                    g.wait_ge(S("p2_ff").h, i + 1)
                    S("p2_ffs").inc(g.dma_start(
                        E.ffn_s[tt * 128:(tt + 1) * 128, q * 512:(q + 1) * 512],
                        FFB[:, i % 4, :].bitcast(BF16)), 16)
            g.wait_ge(S("p2_ffs").h, 16 * NQ * NT)


def _build_qkv(nc, S, E, P, hf, XTh, QTh, KTh, VTh, WQP, BOF):
    TPH = NT // 2
    tt0 = hf * TPH
    PW = max(512, T // 2)
    with nc.psum_tensor(f"QKVPS{hf}", [128, 2, PW], F32) as QKVPS, nc.Block() as blk:
        @blk.sync
        def _(sync):
            if hf == 0:
                S("p3_bo").inc(sync.dma_start(BOF[0:1, :], E.bo[0:1, :]), 16)
            for ti in range(TPH):
                S("p3_xtl").inc(sync.dma_start(
                    XTh[:, ti, :, :],
                    bass.AP(E.xt_s, (tt0 + ti) * 128 * KC * 128,
                            [[KC * 128, 128], [128, KC], [1, 128]])), 16)
        @blk.gpsimd
        def _(g):
            for wi in range(3):
                wd = (E.wq, E.wk, E.wv)[wi]
                for oc in range(KC):
                    li = wi * KC + oc
                    gi = hf * 48 + li
                    if li >= 2:
                        g.wait_ge(S("p3_qkv").h, gi - 1)
                    S("p3_wl").inc(g.dma_start(
                        WQP[:, li % 2, :, :],
                        wd[:, oc * 128:(oc + 1) * 128]
                        .rearrange("(c p) o -> p c o", p=128)), 16)

        @blk.tensor
        def _(t):
            t.wait_ge(S("p3_xtl").h, S("p3_xtl").n)
            for wi in range(3):
                for oc in range(KC):
                    li = wi * KC + oc
                    gi = hf * 48 + li
                    t.wait_ge(S("p3_wl").h, 16 * (gi + 1))
                    if li >= 2:
                        t.wait_ge(S("p3_ev").h, gi - 1)
                    hw = TPH * 64
                    for half in range(2):
                        for kc in range(KC):
                            mm = t.matmul(
                                QKVPS[:, li % 2, half * hw:(half + 1) * hw],
                                WQP[:, li % 2, kc, :],
                                XTh[:, (TPH // 2) * half:(TPH // 2) * (half + 1), kc, :],
                                start=(kc == 0), stop=(kc == KC - 1))
                    S("p3_qkv").inc(mm)

        @blk.scalar
        def _(s):
            for wi in range(3):
                dst = (QTh, KTh, VTh)[wi]
                bias = (P.BQ, P.BK, P.BV)[wi]
                for oc in range(KC):
                    li = wi * KC + oc
                    gi = hf * 48 + li
                    s.wait_ge(S("p3_qkv").h, gi + 1)
                    s.activation(
                        dst[:, oc, :], QKVPS[:, li % 2, 0:T // 2],
                        AF.Identity, bias=bias[:, oc:oc + 1], scale=f32(1.0))
                    S("p3_ev").dinc(s)


def _build_mix(nc, S, P, hf, QTh, KTh, VTh, CTh, LQ, LK, LG,
               ESb, EXb, RMX, RSM, WBt, WTs, VGs):
    NBT = (T // 2) // 32
    ph = contextlib.ExitStack()
    ps = lambda n, sh, d: ph.enter_context(nc.psum_tensor(n, sh, d))
    SCPS = ps(f"SCPS{hf}", [128, 4, 128], F32)
    WTPS = ps(f"WTPS{hf}", [128, 4, 128], BF16)
    VGPS = ps(f"VGPS{hf}", [128, 4, 128], BF16)
    CXPS = ps(f"CXPS{hf}", [128, 4, 128], F32)
    with ph, nc.Block() as blk:
        @blk.vector
        def _(v):
            for bt in range(NBT):
                gb = hf * NBT + bt
                t0 = bt * 32
                if gb >= 1:
                    v.wait_ge(S("mx_sc").h, gb)
                    v.wait_ge(S("mx_vg").h, gb)
                for (Lx, Src) in ((LQ, QTh), (LK, KTh), (LG, VTh)):
                    v.tensor_copy(
                        Lx[:, :, :, :],
                        Src[:, :, t0:t0 + 32].rearrange("p h (j ti) -> p j ti h", ti=8))
                S("mx_g").dinc(v)
                v.wait_ge(S("mx_sc").h, gb + 1)
                v.scalar_tensor_tensor(
                    ESb[:, :, :], SCPS[:, :, :], 1.0,
                    P.MASK[:, None, :].broadcast_to([128, 4, 128]), OP.mult, OP.add)
                S("mx_e0").dinc(v)
                v.tensor_reduce(RMX[:, :], ESb[:, :, :], axis=AX.X, op=OP.max)
                v.drain()
                v.tensor_tensor(ESb[:, :, :], ESb[:, :, :],
                                RMX[:, :, None].broadcast_to([128, 4, 128]), OP.subtract)
                S("mx_e0").dinc(v)
                v.wait_ge(S("mx_ex").h, gb + 1)
                v.tensor_reduce(RSM[:, :], EXb[:, :, :], axis=AX.X, op=OP.add)
                v.drain()
                v.reciprocal(RSM[:, :], RSM[:, :])
                if gb >= 1:
                    v.wait_ge(S("mx_wt").h, gb)
                v.drain()
                v.tensor_tensor(WBt[:, :, :], EXb[:, :, :],
                                RSM[:, :, None].broadcast_to([128, 4, 128]), OP.mult)
                S("mx_wb").dinc(v)

        @blk.tensor
        def _(t):
            for bt in range(NBT):
                gb = hf * NBT + bt
                t.wait_ge(S("mx_g").h, gb + 1)
                if gb >= 1:
                    t.wait_ge(S("mx_e0").h, 2 * gb)
                for j in range(4):
                    mm = t.matmul(SCPS[:, j, :],
                                  LQ[:, j, :, :].rearrange("p a b -> p (a b)"),
                                  LK[:, j, :, :].rearrange("p a b -> p (a b)"),
                                  start=True, stop=True)
                S("mx_sc").inc(mm)
                t.wait_ge(S("mx_wb").h, gb + 1)
                if gb >= 1:
                    t.wait_ge(S("mx_wte").h, gb)
                for j in range(4):
                    tr = t.transpose(WTPS[:, j, :], WBt[:, j, :], P.IdB[:, :])
                S("mx_wt").inc(tr)
                if gb >= 1:
                    t.wait_ge(S("mx_vge").h, gb)
                for j in range(4):
                    tr = t.transpose(VGPS[:, j, :],
                                     LG[:, j, :, :].rearrange("p a b -> p (a b)"), P.IdB[:, :])
                S("mx_vg").inc(tr)
                t.wait_ge(S("mx_wte").h, gb + 1)
                t.wait_ge(S("mx_vge").h, gb + 1)
                if gb >= 1:
                    t.wait_ge(S("mx_ct").h, gb)
                for j in range(4):
                    mm = t.matmul(CXPS[:, j, :], VGs[:, j, :], WTs[:, j, :],
                                  start=True, stop=True)
                S("mx_cx").inc(mm)

        @blk.scalar
        def _(s):
            for bt in range(NBT):
                gb = hf * NBT + bt
                t0 = bt * 32
                s.wait_ge(S("mx_e0").h, 2 * gb + 2)
                s.activation(EXb[:, :, :], ESb[:, :, :], AF.Exp, scale=f32(RSQD))
                S("mx_ex").dinc(s)
                s.wait_ge(S("mx_wt").h, gb + 1)
                s.activation(WTs[:, :, :], WTPS[:, :, :], AF.Identity,
                             bias=f32(0.0), scale=f32(1.0))
                S("mx_wte").dinc(s)
                s.wait_ge(S("mx_vg").h, gb + 1)
                s.activation(VGs[:, :, :], VGPS[:, :, :], AF.Identity,
                             bias=f32(0.0), scale=f32(1.0))
                S("mx_vge").dinc(s)
                s.wait_ge(S("mx_cx").h, gb + 1)
                s.activation(
                    CTh[:, :, t0:t0 + 32].rearrange("p h (j ti) -> p j ti h", ti=8),
                    CXPS[:, :, :].rearrange("p j (ti h) -> p j ti h", h=16),
                    AF.Identity, bias=f32(0.0), scale=f32(1.0))
                S("mx_ct").dinc(s)


_WO_CALL = [0]


def _build_wo(nc, S, E, P, hf, CTh, BOX):
    BOF, BOH, BOL = BOX
    TPH = NT // 2
    tt0 = hf * TPH
    call = _WO_CALL[0]
    _WO_CALL[0] += 1
    ph = contextlib.ExitStack()
    sb = lambda n, sh, d: ph.enter_context(nc.sbuf_tensor(n, sh, d))
    WOB = sb(f"WOB{call}", [128, KC, D], BF16)
    XR = sb(f"XR{call}", [128, 2, D], F32)
    FR = sb(f"FR{call}", [128, 2, D], BF16)
    OB = sb(f"OB{call}", [128, 2, D], F32)
    OPS = ph.enter_context(nc.psum_tensor(f"OPS{call}", [128, 2, D], F32))
    with ph, nc.Block() as blk:
        @blk.gpsimd
        def _(g):
            for kc in range(KC):
                S("p4_wol").inc(g.dma_start(
                    WOB[:, kc, :], E.wo[kc * 128:(kc + 1) * 128, :]), 16)

        @blk.sync
        def _(sync):
            for ti in range(TPH):
                i = call * TPH + ti
                tt = tt0 + ti
                if ti >= 2:
                    sync.wait_ge(S("p4_oev").h, 3 * (i - 1))
                S("p4_xrl").inc(sync.dma_start(
                    XR[:, ti % 2, :], E.x[tt * 128:(tt + 1) * 128, :]), 16)
                S("p4_frl").inc(sync.dma_start(
                    FR[:, ti % 2, :], E.ffn_s[tt * 128:(tt + 1) * 128, :]), 16)

        @blk.tensor
        def _(t):
            t.wait_ge(S("p4_wol").h, 16 * KC * (call + 1))
            t.wait_ge(S("p4_bo2").h, 1)
            for ti in range(TPH):
                i = call * TPH + ti
                if ti >= 2:
                    t.wait_ge(S("p4_oev").h, 3 * (i - 1))
                for kc in range(KC):
                    for oc4 in range(4):
                        mm = t.matmul(OPS[:, ti % 2, oc4 * 512:(oc4 + 1) * 512],
                                      CTh[:, kc, ti * 128:(ti + 1) * 128],
                                      WOB[:, kc, oc4 * 512:(oc4 + 1) * 512],
                                      start=(kc == 0), stop=False)
                for oc4 in range(4):
                    t.matmul(OPS[:, ti % 2, oc4 * 512:(oc4 + 1) * 512], P.ONES1[:, :],
                             BOH[:, oc4 * 512:(oc4 + 1) * 512], start=False, stop=False)
                    mm = t.matmul(OPS[:, ti % 2, oc4 * 512:(oc4 + 1) * 512], P.ONES1[:, :],
                                  BOL[:, oc4 * 512:(oc4 + 1) * 512], start=False, stop=True)
                S("p4_omm").inc(mm)

        @blk.vector
        def _(v):
            if call == 0:
                v.wait_ge(S("p3_bo").h, 16)
                v.memset(BOH[:, :], 0.0)
                v.memset(BOL[:, :], 0.0)
                v.drain()
                v.tensor_copy(BOH[0:1, :], BOF[0:1, :])
                v.drain()
                v.tensor_tensor(BOL[0:1, :], BOF[0:1, :], BOH[0:1, :], OP.subtract)
                S("p4_bo2").dinc(v)
            for ti in range(TPH):
                i = call * TPH + ti
                v.wait_ge(S("p4_omm").h, i + 1)
                v.wait_ge(S("p4_xrl").h, 16 * (i + 1))
                v.wait_ge(S("p4_frl").h, 16 * (i + 1))
                if i >= 2:
                    v.wait_ge(S("p4_ost").h, 16 * (i - 1))
                v.tensor_tensor(
                    OB[:, ti % 2, :], OPS[:, ti % 2, :], XR[:, ti % 2, :], OP.add)
                S("p4_oev").dinc(v)
                S("p4_oev").dinc(v)
                v.tensor_tensor(
                    OB[:, ti % 2, :], OB[:, ti % 2, :], FR[:, ti % 2, :], OP.add)
                S("p4_oev").dinc(v)

        @blk.scalar
        def _(s):
            for ti in range(TPH):
                i = call * TPH + ti
                tt = tt0 + ti
                s.wait_ge(S("p4_oev").h, 3 * (i + 1))
                S("p4_ost").inc(s.dma_start(
                    E.out[tt * 128:(tt + 1) * 128, :], OB[:, ti % 2, :]), 16)
            if call == 1:
                s.wait_ge(S("p4_ost").h, 16 * NT)


def _build_p34(nc, S, E, P):
    outer = contextlib.ExitStack()
    CTh = outer.enter_context(nc.sbuf_tensor("CTh", [128, KC, T // 2], BF16))
    BOH = outer.enter_context(nc.sbuf_tensor("BOH", [128, D], BF16))
    BOL = outer.enter_context(nc.sbuf_tensor("BOL", [128, D], BF16))
    BOF = outer.enter_context(nc.sbuf_tensor("BOF", [1, D], F32))
    with outer:
        ph = contextlib.ExitStack()
        sb = lambda n, sh, d: ph.enter_context(nc.sbuf_tensor(n, sh, d))
        XTh = sb("XTh", [128, NT // 2, KC, 128], BF16)
        QTh = sb("QTh", [128, KC, T // 2], BF16)
        KTh = sb("KTh", [128, KC, T // 2], BF16)
        VTh = sb("VTh", [128, KC, T // 2], BF16)
        WQP = sb("WQP", [128, 2, KC, 128], BF16)
        LQ = sb("LQ", [128, 4, 8, 16], BF16)
        LK = sb("LK", [128, 4, 8, 16], BF16)
        LG = sb("LG", [128, 4, 8, 16], BF16)
        ESb = sb("ESb", [128, 4, 128], F32)
        EXb = sb("EXb", [128, 4, 128], F32)
        RMX = sb("RMX", [128, 4], F32)
        RSM = sb("RSM", [128, 4], F32)
        WBt = sb("WBt", [128, 4, 128], BF16)
        WTs = sb("WTs", [128, 4, 128], BF16)
        VGs = sb("VGs", [128, 4, 128], BF16)
        with ph:
            for hf in range(2):
                _build_qkv(nc, S, E, P, hf, XTh, QTh, KTh, VTh, WQP, BOF)
                _build_mix(nc, S, P, hf, QTh, KTh, VTh, CTh, LQ, LK, LG,
                           ESb, EXb, RMX, RSM, WBt, WTs, VGs)
                if hf == 0:
                    with nc.Block() as blk:
                        @blk.gpsimd
                        def _(g):
                            S("ct_sp").inc(g.dma_start(
                                bass.AP(E.ct_s, 0, [[KC * (T // 2), 128], [T // 2, KC], [1, T // 2]]),
                                CTh[:, :, :]), 16)
                            g.wait_ge(S("ct_sp").h, 16)
        _build_wo(nc, S, E, P, 1, CTh, (BOF, BOH, BOL))
        with nc.Block() as blk:
            @blk.gpsimd
            def _(g):
                S("ct_sp").inc(g.dma_start(
                    CTh[:, :, :],
                    bass.AP(E.ct_s, 0, [[KC * (T // 2), 128], [T // 2, KC], [1, T // 2]])), 16)
                g.wait_ge(S("ct_sp").h, 32)
        _build_wo(nc, S, E, P, 0, CTh, (BOF, BOH, BOL))


class _NS:
    pass


def build_program():
    nc = bass.Bass("TRN2", target_bir_lowering=False, detect_race_conditions=False)
    E = _NS()
    E.x = nc.dram_tensor("x", [T, D], F32, kind="ExternalInput")
    E.wq = nc.dram_tensor("wq", [D, D], F32, kind="ExternalInput")
    E.wk = nc.dram_tensor("wk", [D, D], F32, kind="ExternalInput")
    E.wv = nc.dram_tensor("wv", [D, D], F32, kind="ExternalInput")
    E.wo = nc.dram_tensor("wo", [D, D], F32, kind="ExternalInput")
    E.bq = nc.dram_tensor("bq", [1, D], F32, kind="ExternalInput")
    E.bk = nc.dram_tensor("bk", [1, D], F32, kind="ExternalInput")
    E.bv = nc.dram_tensor("bv", [1, D], F32, kind="ExternalInput")
    E.bo = nc.dram_tensor("bo", [1, D], F32, kind="ExternalInput")
    E.g2 = nc.dram_tensor("g2", [1, D], F32, kind="ExternalInput")
    E.be2 = nc.dram_tensor("be2", [1, D], F32, kind="ExternalInput")
    E.lnw = nc.dram_tensor("lnw", [1, D], F32, kind="ExternalInput")
    E.lnb = nc.dram_tensor("lnb", [1, D], F32, kind="ExternalInput")
    E.w1 = nc.dram_tensor("w1", [FF, D], F32, kind="ExternalInput")
    E.b1 = nc.dram_tensor("b1", [1, FF], F32, kind="ExternalInput")
    E.w2 = nc.dram_tensor("w2", [D, FF], F32, kind="ExternalInput")
    E.b2 = nc.dram_tensor("b2", [1, D], F32, kind="ExternalInput")
    E.out = nc.dram_tensor("out", [T, D], F32, kind="ExternalOutput")
    kw = {"kind": "ExternalOutput"} if DEBUG_SCRATCH else {}
    E.xt_s = nc.dram_tensor("xt_s", [NT, 128, KC, 128], BF16, **kw)
    E.xffn_s = nc.dram_tensor("xffn_s", [T, D], BF16, **kw)
    E.qh2t_s = nc.dram_tensor("qh2t_s", [NT, 128, FC, 128], BF16, **kw)
    E.ffn_s = nc.dram_tensor("ffn_s", [T, D], BF16, **kw)
    E.ct_s = nc.dram_tensor("ct_s", [128, KC, T // 2], BF16, **kw)
    if DEBUG_SCRATCH:
        E.dbg_qht = nc.dram_tensor("dbg_qht", [128, KC, T], BF16, kind="ExternalOutput")
        E.dbg_qhi = nc.dram_tensor("dbg_qhi", [128, D], BF16, kind="ExternalOutput")
        E.dbg_yb = nc.dram_tensor("dbg_yb", [NT, 128, 512], BF16, kind="ExternalOutput")
        E.dbg_gi = nc.dram_tensor("dbg_gi", [NT, 128, 512], BF16, kind="ExternalOutput")
    else:
        E.dbg_qht = E.dbg_qhi = E.dbg_yb = E.dbg_gi = None

    persist = contextlib.ExitStack()
    sbp = lambda n, sh, d: persist.enter_context(nc.sbuf_tensor(n, sh, d))
    P = _NS()
    P.IdF = sbp("IdF", [128, 128], F32)
    P.IdB = sbp("IdB", [128, 128], BF16)
    P.MASK = sbp("MASK", [128, 128], F32)
    P.BQ = sbp("BQ", [128, KC], F32)
    P.BK = sbp("BK", [128, KC], F32)
    P.BV = sbp("BV", [128, KC], F32)
    P.G2C = sbp("G2C", [128, KC], F32)
    P.B2C = sbp("B2C", [128, KC], F32)
    P.ONES1 = sbp("ONES1", [128, 128], BF16)

    sems = {}

    def S(name):
        if name not in sems:
            sems[name] = Sem(nc, persist, name)
        return sems[name]

    _build_init(nc, S, E, P)
    with contextlib.ExitStack() as s01:
        QHT = s01.enter_context(nc.sbuf_tensor("QHT", [128, KC, T], I16))
        _build_p0(nc, S, E, P, QHT)
        if BUILD_PHASES == "all":
            _build_p1(nc, S, E, P, QHT)
    if BUILD_PHASES == "all":
        _build_p2(nc, S, E, P)
        _build_p34(nc, S, E, P)
    return nc, persist


_CACHE = {}


def prepare_maps(inputs):
    x = np.ascontiguousarray(np.asarray(inputs["L0_attn_x"], dtype=np.float32).reshape(16384, D))
    w = {
        "wq": inputs["L0_attn_Wq"], "wk": inputs["L0_attn_Wk"],
        "wv": inputs["L0_attn_Wv"], "wo": inputs["L0_attn_Wo"],
        "bq": np.asarray(inputs["L0_attn_bq"]).reshape(1, D),
        "bk": np.asarray(inputs["L0_attn_bk"]).reshape(1, D),
        "bv": np.asarray(inputs["L0_attn_bv"]).reshape(1, D),
        "bo": np.asarray(inputs["L0_attn_bo"]).reshape(1, D),
        "g2": np.asarray(inputs["L0_attn_ln_g"]).reshape(1, D),
        "be2": np.asarray(inputs["L0_attn_ln_b"]).reshape(1, D),
        "lnw": np.asarray(inputs["L0_ffn_ln_w"]).reshape(1, D),
        "lnb": np.asarray(inputs["L0_ffn_ln_b"]).reshape(1, D),
        "w1": inputs["L0_ffn_W1"], "b1": np.asarray(inputs["L0_ffn_b1"]).reshape(1, FF),
        "w2": inputs["L0_ffn_W2"], "b2": np.asarray(inputs["L0_ffn_b2"]).reshape(1, D),
    }
    w = {k: np.ascontiguousarray(np.asarray(v, dtype=np.float32)) for k, v in w.items()}
    in_maps = []
    for c in range(NCORES):
        m = dict(w)
        m["x"] = np.ascontiguousarray(x[c * T:(c + 1) * T, :])
        in_maps.append(m)
    return in_maps


def kernel(**inputs):
    in_maps = prepare_maps(inputs)
    if "nc" not in _CACHE:
        _CACHE["nc"], _CACHE["persist"] = build_program()
    nc = _CACHE["nc"]
    res = run_bass_kernel_spmd(nc, in_maps, core_ids=list(range(NCORES)))
    out = np.concatenate([np.asarray(r["out"]) for r in res.results], axis=0)
    return out.reshape(4, 4096, 2048)


# version 11 (final)
# speedup vs baseline: 1.3876x; 1.0432x over previous
import contextlib
import numpy as np
import concourse.bass as bass
import concourse.mybir as mybir
from concourse.bass_utils import run_bass_kernel_spmd

F32 = mybir.dt.float32
BF16 = mybir.dt.bfloat16
I16 = mybir.dt.int16
I32 = mybir.dt.int32
AF = mybir.ActivationFunctionType
OP = mybir.AluOpType
AX = mybir.AxisListType

DEBUG_SCRATCH = False
BUILD_PHASES = "all"
NCORES = 8
T = 2048
D = 2048
FF = 8192
NT = T // 128
KC = D // 128
FC = FF // 128
C2 = 196608.0
MASK_EXP = 0xFF800000 - (1 << 32)
K_RS = float(0x7F000000)
CLIP_HI = 1.984375
EPS = 1e-5
RSQD = float(1.0 / np.float32(np.sqrt(np.float32(2048.0))))
f32 = np.float32


class Sem:
    def __init__(self, nc, es, name):
        self.h = es.enter_context(nc.semaphore(name))
        self.n = 0

    def inc(self, instr, k=1):
        instr.then_inc(self.h, k)
        self.n += k
        return self.n

    def dinc(self, eng, k=1):
        eng.drain().then_inc(self.h, k)
        self.n += k
        return self.n


def quant_chain(v, X_ap, n_el, AM, PB, RS, PBB, RSB, U_ap, V_ap, Q_ap, fp32_input, npart=128):
    nb = n_el // 32
    nb2 = nb // 2
    c2 = n_el // 2
    hs = [(slice(0, nb2), slice(0, c2)), (slice(nb2, nb), slice(c2, n_el))]
    v.drain()
    for bs, cs in hs:
        v.tensor_reduce(AM[:, bs], X_ap[:, cs].rearrange("p (b j) -> p b j", j=32),
                        axis=AX.X, op=OP.max, apply_absolute_value=True)
    for bs, cs in hs:
        v.tensor_single_scalar(PB[:, bs], AM[:, bs].bitcast(I32), MASK_EXP, OP.bitwise_and)
    v.drain()
    for bs, cs in hs:
        v.tensor_scalar(RS[:, bs], PB[:, bs], -1.0, K_RS, OP.mult, OP.add)
    for bs, cs in hs:
        v.tensor_copy(PBB[:, bs], PB[:, bs].bitcast(F32))
    v.drain()
    if not fp32_input:
        for bs, cs in hs:
            v.tensor_copy(RSB[:, bs], RS[:, bs].bitcast(F32))
        v.drain()
    for bs, cs in hs:
        rs_b = (RS[:, bs].bitcast(F32) if fp32_input else RSB[:, bs])
        v.tensor_tensor(U_ap[:, cs].rearrange("p (b j) -> p b j", j=32),
                        X_ap[:, cs].rearrange("p (b j) -> p b j", j=32),
                        rs_b[:, :, None].broadcast_to([npart, nb2, 32]), OP.mult)
    for bs, cs in hs:
        v.tensor_scalar(V_ap[:, cs], U_ap[:, cs], C2, -C2, OP.add, OP.add)
    for bs, cs in hs:
        q = v.scalar_tensor_tensor(Q_ap[:, cs].rearrange("p (b j) -> p b j", j=32),
                                   V_ap[:, cs].rearrange("p (b j) -> p b j", j=32),
                                   CLIP_HI,
                                   PBB[:, bs][:, :, None].broadcast_to([npart, nb2, 32]),
                                   OP.min, OP.mult)
    return q


def _build_init(nc, S, E, P):
    ph = contextlib.ExitStack()
    sb = lambda n, sh, d: ph.enter_context(nc.sbuf_tensor(n, sh, d))
    Pidx = sb("Pidx", [128, 1], I32)
    Cidx = sb("Cidx", [128, 128], I32)
    CBl = sb("CBl", [128, 128], I32)
    TDm = sb("TDm", [128, 128], F32)
    GEm = sb("GEm", [128, 128], F32)
    with ph:
        with nc.Block() as blk:
            @blk.gpsimd
            def _(g):
                g.iota(Pidx[:, :], [[1, 1]], channel_multiplier=1)
                g.iota(Cidx[:, :], [[1, 128]], channel_multiplier=0)
                g.iota(CBl[:, :].rearrange("p (a b) -> p a b", b=16),
                       [[1, 8], [0, 16]], channel_multiplier=0)

            @blk.sync
            def _(sync):
                ldi = S("ldi")
                for dst, src in ((P.BQ, E.bq), (P.BK, E.bk), (P.BV, E.bv),
                                 (P.G2C, E.g2), (P.B2C, E.be2)):
                    ldi.inc(sync.dma_start(
                        dst[:, :], src[0:1, :].rearrange("o (c p) -> p (o c)", p=128),
                        allow_slow_non_contiguous=True), 16)
                sync.wait_ge(ldi.h, ldi.n)

        with nc.Block() as blk:
            @blk.vector
            def _(v):
                v.tensor_tensor(P.IdF[:, :], Cidx[:, :],
                                Pidx[:, :].broadcast_to([128, 128]), OP.is_equal)
                v.drain()
                v.tensor_copy(P.IdB[:, :], P.IdF[:, :])
                v.tensor_single_scalar(TDm[:, 0:1], Pidx[:, :], 0.0, OP.is_le)
                v.drain()
                v.tensor_copy(P.ONES1[:, :], TDm[:, 0:1].broadcast_to([128, 128]))
                v.drain()
                v.tensor_scalar(TDm[:, :], CBl[:, :], -16.0, None, OP.mult)
                v.drain()
                v.tensor_tensor(TDm[:, :], Pidx[:, :].broadcast_to([128, 128]), TDm[:, :], OP.add)
                v.drain()
                v.tensor_single_scalar(GEm[:, :], TDm[:, :], 0.0, OP.is_ge)
                v.drain()
                v.tensor_single_scalar(TDm[:, :], TDm[:, :], 15.0, OP.is_le)
                v.drain()
                v.tensor_tensor(GEm[:, :], GEm[:, :], TDm[:, :], OP.logical_and)
                v.drain()
                v.tensor_scalar(P.MASK[:, :], GEm[:, :], 1e30, -1e30, OP.mult, OP.add)


def _build_p0(nc, S, E, P, QHT):
    ph = contextlib.ExitStack()
    sb = lambda n, sh, d: ph.enter_context(nc.sbuf_tensor(n, sh, d))
    ps = lambda n, sh, d: ph.enter_context(nc.psum_tensor(n, sh, d))
    XIN = sb("XIN", [128, 2, D], F32)
    XN = sb("XN", [128, D], F32)
    T2 = sb("T2", [128, D], F32)
    T3 = sb("T3", [128, D], F32)
    H3T = sb("H3T", [128, D], F32)
    XFN = sb("XFN", [128, D], I16)
    XFB = sb("XFB", [128, 2, D], BF16)
    LNW = sb("LNW", [128, D], F32)
    LNB = sb("LNB", [128, D], F32)
    HQ = sb("HQ", [128, D], I16)
    UQ = sb("UQ", [128, D], I16)
    VQ = sb("VQ", [128, D], I16)
    QHI = sb("QHI", [128, D], I16)
    AM0 = sb("AM0", [128, 64], F32)
    PB0 = sb("PB0", [128, 64], I32)
    RS0 = sb("RS0", [128, 64], I32)
    PBB0 = sb("PBB0", [128, 64], BF16)
    RSB0 = sb("RSB0", [128, 64], BF16)
    STt = sb("STt", [128, 24], F32)
    MVt = sb("MVt", [128, 2], F32)
    VARt = sb("VARt", [128, 1], F32)
    SQt = sb("SQt", [128, 1], F32)
    R0t = sb("R0t", [128, 1], F32)
    A1t = sb("A1t", [128, 1], F32)
    RRt = sb("RRt", [128, 3], F32)
    NBt = sb("NBt", [128, 3], F32)
    XTB = sb("XTB", [128, 2, KC, 128], BF16)
    TRP = ps("TRP", [128, KC, 128], F32)
    TRQ = ps("TRQ", [128, KC, 128], BF16)

    def stats(v, src_ap):
        v.drain()
        for c in range(4):
            v.bn_stats(STt[:, 6 * c:6 * c + 6], src_ap[:, 512 * c:512 * (c + 1)])
        v.drain()
        v.bn_aggr(MVt[:, :].rearrange("p (a t) -> p a t", t=2),
                  STt[:, :].rearrange("p (g t) -> p g t", t=3))
        v.drain()
        return v.tensor_single_scalar(VARt[:, :], MVt[:, 1:2], EPS, OP.add)

    def rstd(v, k):
        v.reciprocal(RRt[:, k:k + 1], SQt[:, :])
        v.drain()
        v.tensor_tensor(NBt[:, k:k + 1], MVt[:, 0:1], RRt[:, k:k + 1], OP.mult)
        v.drain()
        return v.tensor_single_scalar(NBt[:, k:k + 1], NBt[:, k:k + 1], -1.0, OP.mult)

    with ph, nc.Block() as blk:
        @blk.sync
        def _(sync):
            ldw = S("p0_lnwb")
            ldw.inc(sync.dma_start(LNW[:, :], E.lnw[0:1, :].partition_broadcast(128)), 16)
            ldw.inc(sync.dma_start(LNB[:, :], E.lnb[0:1, :].partition_broadcast(128)), 16)
            for tt in range(NT):
                if tt >= 2:
                    sync.wait_ge(S("p0_xn").h, tt - 1)
                S("p0_ldx").inc(sync.dma_start(
                    XIN[:, tt % 2, :], E.x[tt * 128:(tt + 1) * 128, :]), 16)

        @blk.vector
        def _(v):
            for tt in range(NT):
                b = tt % 2
                v.wait_ge(S("p0_ldx").h, 16 * (tt + 1))
                if tt == 0:
                    v.wait_ge(S("p0_lnwb").h, 32)
                stats(v, XIN[:, b, :])
                S("p0_var").dinc(v)
                v.wait_ge(S("p0_sqd").h, 3 * tt + 1)
                rstd(v, 0)
                S("p0_r").dinc(v)
                v.wait_ge(S("p0_xn").h, tt + 1)
                stats(v, XN[:, :])
                S("p0_var").dinc(v)
                v.wait_ge(S("p0_sqd").h, 3 * tt + 2)
                rstd(v, 1)
                if tt >= 2:
                    v.wait_ge(S("p0_t3").h, tt - 1)
                    v.wait_ge(S("p0_xfs").h, 16 * (tt - 1))
                v.drain()
                v.tensor_copy(XFN[:, :].bitcast(BF16), XN[:, :])
                S("p0_r").dinc(v)
                v.tensor_copy(XFB[:, b, :], XFN[:, :].bitcast(BF16))
                S("p0_xfb").dinc(v)
                stats(v, XFB[:, b, :])
                S("p0_var").dinc(v)
                v.wait_ge(S("p0_sqd").h, 3 * tt + 3)
                rstd(v, 2)
                S("p0_r").dinc(v)
                v.wait_ge(S("p0_tr1").h, tt + 1)
                if tt >= 2:
                    v.wait_ge(S("p0_xts").h, 16 * (tt - 1))
                for c in range(KC):
                    v.tensor_scalar(XTB[:, b, c, :], TRP[:, c, :],
                                    P.G2C[:, c:c + 1], P.B2C[:, c:c + 1], OP.mult, OP.add)
                S("p0_xtb").dinc(v)
                v.wait_ge(S("p0_t3").h, tt + 1)
                v.scalar_tensor_tensor(H3T[:, :], T3[:, :], 1.0, LNW[:, :], OP.mult, OP.mult)
                v.drain()
                v.scalar_tensor_tensor(HQ[:, :].bitcast(BF16), H3T[:, :], 1.0, LNB[:, :],
                                       OP.mult, OP.add)
                if tt >= 1:
                    v.wait_ge(S("p0_trq").h, tt)
                if DEBUG_SCRATCH and tt == 1:
                    v.wait_ge(S("p0_dbg").h, 16)
                quant_chain(v, HQ[:, :].bitcast(BF16), D, AM0, PB0, RS0, PBB0, RSB0,
                            UQ[:, :].bitcast(BF16), VQ[:, :].bitcast(BF16),
                            QHI[:, :].bitcast(BF16), fp32_input=False)
                S("p0_qh").dinc(v)

        @blk.scalar
        def _(s):
            for tt in range(NT):
                b = tt % 2
                for k in range(3):
                    s.wait_ge(S("p0_var").h, 3 * tt + k + 1)
                    s.activation(SQt[:, :], VARt[:, :], AF.Sqrt,
                                 bias=f32(0.0), scale=f32(1.0))
                    S("p0_sqd").dinc(s)
                    if k == 0:
                        s.wait_ge(S("p0_r").h, 3 * tt + 1)
                        s.activation(XN[:, :], XIN[:, b, :], AF.Identity,
                                     bias=NBt[:, 0:1], scale=RRt[:, 0:1])
                        S("p0_xn").dinc(s)
                    elif k == 1:
                        s.wait_ge(S("p0_r").h, 3 * tt + 2)
                        s.activation(T2[:, :], XN[:, :], AF.Identity,
                                     bias=NBt[:, 1:2], scale=RRt[:, 1:2])
                        S("p0_t2").dinc(s)
                    else:
                        s.wait_ge(S("p0_r").h, 3 * tt + 3)
                        s.activation(T3[:, :], XFB[:, b, :], AF.Identity,
                                     bias=NBt[:, 2:3], scale=RRt[:, 2:3])
                        S("p0_t3").dinc(s)
                s.wait_ge(S("p0_trq").h, tt + 1)
                s.activation(
                    QHT[:, :, tt * 128:(tt + 1) * 128].bitcast(BF16),
                    TRQ[:, :, :], AF.Identity, bias=f32(0.0), scale=f32(1.0))
                S("p0_qht").dinc(s)

        @blk.tensor
        def _(t):
            for tt in range(NT):
                t.wait_ge(S("p0_t2").h, tt + 1)
                if tt >= 1:
                    t.wait_ge(S("p0_xtb").h, tt)
                for c in range(KC):
                    tr = t.transpose(TRP[:, c, :], T2[:, c * 128:(c + 1) * 128], P.IdF[:, :])
                S("p0_tr1").inc(tr)
                t.wait_ge(S("p0_qh").h, tt + 1)
                if tt >= 1:
                    t.wait_ge(S("p0_qht").h, tt)
                for c in range(KC):
                    tr = t.transpose(TRQ[:, c, :],
                                     QHI[:, c * 128:(c + 1) * 128].bitcast(BF16), P.IdB[:, :])
                S("p0_trq").inc(tr)

        @blk.gpsimd
        def _(g):
            if DEBUG_SCRATCH:
                g.wait_ge(S("p0_qh").h, 1)
                S("p0_dbg").inc(g.dma_start(E.dbg_qhi[:, :], QHI[:, :].bitcast(BF16)), 16)
            for tt in range(NT):
                b = tt % 2
                g.wait_ge(S("p0_xtb").h, tt + 1)
                S("p0_xts").inc(g.dma_start(
                    bass.AP(E.xt_s, tt * 128 * KC * 128,
                            [[KC * 128, 128], [128, KC], [1, 128]]),
                    XTB[:, b, :, :]), 16)
                g.wait_ge(S("p0_xfb").h, tt + 1)
                S("p0_xfs").inc(g.dma_start(
                    E.xffn_s[tt * 128:(tt + 1) * 128, :], XFB[:, b, :]), 16)
            if DEBUG_SCRATCH:
                g.wait_ge(S("p0_qht").h, NT)
                S("p0_dbg").inc(g.dma_start(
                    E.dbg_qht[:, :, :].rearrange("p c t -> p (c t)"),
                    QHT[:, :, :].bitcast(BF16).rearrange("p c t -> p (c t)")), 16)
                g.wait_ge(S("p0_dbg").h, 32)
            g.wait_ge(S("p0_xts").h, 16 * NT)
            g.wait_ge(S("p0_xfs").h, 16 * NT)


def _build_p1(nc, S, E, P, QHT):
    NPAN = FF // 512
    ph = contextlib.ExitStack()
    sb = lambda n, sh, d: ph.enter_context(nc.sbuf_tensor(n, sh, d))
    ps = lambda n, sh, d: ph.enter_context(nc.psum_tensor(n, sh, d))
    W1P = sb("W1P", [128, 2, 2, D], F32)
    U1 = sb("U1", [128, 2, D], F32)
    V1 = sb("V1", [128, 2, D], I16)
    Q1P = sb("Q1P", [128, 2, D], I16)
    AM1 = sb("AM1", [128, 128], F32)
    PB1 = sb("PB1", [128, 128], I32)
    RS1 = sb("RS1", [128, 128], I32)
    PBB1 = sb("PBB1", [128, 128], BF16)
    RSB1 = sb("RSB1", [128, 128], BF16)
    QW1 = sb("QW1", [128, 2, KC, 512], I16)
    BH1 = sb("BH1", [128, 2, 512], BF16)
    BL1 = sb("BL1", [128, 2, 512], BF16)
    B1F = sb("B1F", [1, 2, 512], F32)
    YB = sb("YB", [128, 512], I16)
    GI = sb("GI", [128, 4, 512], I16)
    UG = sb("UG", [128, 512], I16)
    VG = sb("VG", [128, 512], I16)
    QG = sb("QG", [128, NT, 512], I16)
    AMG = sb("AMG", [128, 16], F32)
    PBG = sb("PBG", [128, 16], I32)
    RSG = sb("RSG", [128, 16], I32)
    PBBG = sb("PBBG", [128, 16], BF16)
    RSBG = sb("RSBG", [128, 16], BF16)
    QH2B = sb("QH2B", [128, 4, 4, 128], I16)
    Y1PS = ps("Y1PS", [128, 4, 512], F32)
    W1TR = ps("W1TR", [128, KC, 128], BF16)
    H2TR = ps("H2TR", [128, 2, 4, 256], BF16)

    with ph, nc.Block() as blk:
        @blk.sync
        def _(sync):
            for pn in range(NPAN):
                for r in range(2):
                    idx = pn * 2 + r
                    if idx >= 2:
                        sync.wait_ge(S("p1_w1q").h, idx - 1)
                    S("p1_w1l").inc(sync.dma_start(
                        W1P[:, idx % 2, :, :],
                        E.w1[pn * 512 + r * 256: pn * 512 + (r + 1) * 256, :]
                        .rearrange("(rr p) k -> p rr k", p=128)), 16)
                if pn >= 2:
                    sync.wait_ge(S("p1_mm").h, (pn - 1) * NT)
                S("p1_b1l").inc(sync.dma_start(
                    B1F[0:1, pn % 2, :], E.b1[0:1, pn * 512:(pn + 1) * 512]), 16)

        @blk.vector
        def _(v):
            v.memset(BH1[:, :, :], 0.0)
            v.memset(BL1[:, :, :], 0.0)
            S("p1_bz").dinc(v)
            for pn in range(NPAN):
                v.wait_ge(S("p1_b1l").h, 16 * (pn + 1))
                if pn >= 2:
                    v.wait_ge(S("p1_mm").h, (pn - 1) * NT)
                v.tensor_copy(BH1[0:1, pn % 2, :], B1F[0:1, pn % 2, :])
                v.drain()
                v.tensor_tensor(BL1[0:1, pn % 2, :], B1F[0:1, pn % 2, :],
                                BH1[0:1, pn % 2, :], OP.subtract)
                S("p1_blo").dinc(v)
                for r in range(2):
                    idx = pn * 2 + r
                    v.wait_ge(S("p1_w1l").h, 16 * (idx + 1))
                    if idx >= 1:
                        v.wait_ge(S("p1_w1t").h, 2 * idx)
                    quant_chain(
                        v, W1P[:, idx % 2, :, :].rearrange("p r k -> p (r k)"), 2 * D,
                        AM1, PB1, RS1, PBB1, RSB1,
                        U1[:, :, :].rearrange("p r k -> p (r k)"),
                        V1[:, :, :].rearrange("p r k -> p (r k)").bitcast(BF16),
                        Q1P[:, :, :].rearrange("p r k -> p (r k)").bitcast(BF16),
                        fp32_input=True)
                    S("p1_w1q").dinc(v)
                for tt in range(NT):
                    i = pn * NT + tt
                    if i >= 1:
                        v.wait_ge(S("p1_gi").h, i)
                    if i >= NT + 1:
                        v.wait_ge(S("p1_h2t").h, i - NT)
                    if DEBUG_SCRATCH and pn == 0 and tt == 1:
                        v.wait_ge(S("p1_dbg").h, 16)
                    if i >= 1:
                        quant_chain(v, GI[:, (i - 1) % 4, :].bitcast(BF16), 512,
                                    AMG, PBG, RSG, PBBG, RSBG,
                                    UG[:, :].bitcast(BF16), VG[:, :].bitcast(BF16),
                                    QG[:, (i - 1) % NT, :].bitcast(BF16), fp32_input=False)
                        S("p1_qg").dinc(v)
                    if i == NPAN * NT - 1:
                        v.wait_ge(S("p1_gi").h, i + 1)
                        quant_chain(v, GI[:, i % 4, :].bitcast(BF16), 512,
                                    AMG, PBG, RSG, PBBG, RSBG,
                                    UG[:, :].bitcast(BF16), VG[:, :].bitcast(BF16),
                                    QG[:, i % NT, :].bitcast(BF16), fp32_input=False)
                        S("p1_qg").dinc(v)

        @blk.tensor
        def _(t):
            for pn in range(NPAN):
                for r in range(2):
                    idx = pn * 2 + r
                    t.wait_ge(S("p1_w1q").h, idx + 1)
                    for pr in range(2):
                        gidx = idx * 2 + pr
                        if gidx >= 1:
                            t.wait_ge(S("p1_w1e").h, gidx)
                        for c in range(KC):
                            tr = t.transpose(W1TR[:, c, :],
                                             Q1P[:, pr, c * 128:(c + 1) * 128].bitcast(BF16),
                                             P.IdB[:, :])
                        S("p1_w1t").inc(tr)
                def h2_tr(t, j):
                    t.wait_ge(S("p1_qg").h, j + 1)
                    if j >= 2:
                        t.wait_ge(S("p1_h2e").h, j - 1)
                    for c in range(4):
                        tr = t.transpose(H2TR[:, j % 2, c, 0:128],
                                         QG[:, j % NT, c * 128:(c + 1) * 128].bitcast(BF16),
                                         P.IdB[:, :])
                    S("p1_h2t").inc(tr)

                for tt in range(NT):
                    i = pn * NT + tt
                    t.wait_ge(S("p1_w1e").h, 4 * (pn + 1))
                    if tt == 0:
                        t.wait_ge(S("p1_bz").h, 1)
                        t.wait_ge(S("p1_blo").h, pn + 1)
                    if i >= 4:
                        t.wait_ge(S("p1_yb").h, i - 3)
                    for kc in range(KC):
                        mm = t.matmul(Y1PS[:, tt % 4, :],
                                      QHT[:, kc, tt * 128:(tt + 1) * 128].bitcast(BF16),
                                      QW1[:, pn % 2, kc, :].bitcast(BF16),
                                      start=(kc == 0), stop=False)
                    t.matmul(Y1PS[:, tt % 4, :], P.ONES1[:, :], BH1[:, pn % 2, :],
                             start=False, stop=False)
                    mm = t.matmul(Y1PS[:, tt % 4, :], P.ONES1[:, :], BL1[:, pn % 2, :],
                                  start=False, stop=True)
                    S("p1_mm").inc(mm)
                    if i >= 2:
                        h2_tr(t, i - 2)
                    if i == NPAN * NT - 1:
                        h2_tr(t, i - 1)
                        h2_tr(t, i)

        @blk.scalar
        def _(s):
            for pn in range(NPAN):
                for r in range(2):
                    idx = pn * 2 + r
                    for pr in range(2):
                        gidx = idx * 2 + pr
                        s.wait_ge(S("p1_w1t").h, gidx + 1)
                        if pn >= 2 and r == 0 and pr == 0:
                            s.wait_ge(S("p1_mm").h, (pn - 1) * NT)
                        s.activation(
                            QW1[:, pn % 2, :,
                                (r * 2 + pr) * 128:(r * 2 + pr + 1) * 128].bitcast(BF16),
                            W1TR[:, :, :], AF.Identity, bias=f32(0.0), scale=f32(1.0))
                        S("p1_w1e").dinc(s)
                for tt in range(NT):
                    i = pn * NT + tt
                    s.wait_ge(S("p1_mm").h, i + 1)
                    if i >= 4:
                        s.wait_ge(S("p1_qg").h, i - 3)
                    if DEBUG_SCRATCH and pn == 0 and tt == 1:
                        s.wait_ge(S("p1_dbg").h, 32)
                    s.activation(YB[:, :].bitcast(BF16), Y1PS[:, tt % 4, :],
                                 AF.Identity, bias=f32(0.0), scale=f32(1.0))
                    S("p1_yb").dinc(s)
                    s.activation(GI[:, i % 4, :].bitcast(BF16), YB[:, :].bitcast(BF16),
                                 AF.Gelu_apprx_tanh)
                    S("p1_gi").dinc(s)

                    def h2_ev(s, j):
                        s.wait_ge(S("p1_h2t").h, j + 1)
                        if j >= 4:
                            s.wait_ge(S("p1_h2s").h, 16 * (j - 3))
                        s.activation(QH2B[:, j % 4, :, :].bitcast(BF16),
                                     H2TR[:, j % 2, :, 0:128], AF.Identity,
                                     bias=f32(0.0), scale=f32(1.0))
                        S("p1_h2e").dinc(s)

                    if i >= 2:
                        h2_ev(s, i - 2)
                    if i == NPAN * NT - 1:
                        h2_ev(s, i - 1)
                        h2_ev(s, i)

        @blk.gpsimd
        def _(g):
            if DEBUG_SCRATCH:
                g.wait_ge(S("p1_yb").h, 1)
                S("p1_dbg").inc(g.dma_start(E.dbg_yb[0, :, :], YB[:, :].bitcast(BF16)), 16)
                g.wait_ge(S("p1_gi").h, 1)
                S("p1_dbg").inc(g.dma_start(E.dbg_gi[0, :, :], GI[:, 0, :].bitcast(BF16)), 16)
            for pn in range(NPAN):
                for tt in range(NT):
                    i = pn * NT + tt
                    g.wait_ge(S("p1_h2e").h, i + 1)
                    S("p1_h2s").inc(g.dma_start(
                        bass.AP(E.qh2t_s, tt * 128 * FC * 128 + pn * 4 * 128,
                                [[FC * 128, 128], [128, 4], [1, 128]]),
                        QH2B[:, i % 4, :, :].bitcast(BF16)), 16)
            g.wait_ge(S("p1_h2s").h, 16 * NPAN * NT)


def _build_p2(nc, S, E, P):
    NQ = 4
    NW2 = 4
    ph = contextlib.ExitStack()
    sb = lambda n, sh, d: ph.enter_context(nc.sbuf_tensor(n, sh, d))
    ps = lambda n, sh, d: ph.enter_context(nc.psum_tensor(n, sh, d))
    W2P = sb("W2P", [128, FF], F32)
    U2 = sb("U2", [128, FF // 4], F32)
    V2 = sb("V2", [128, FF // 4], I16)
    Q2P = sb("Q2P", [128, 2, FF], I16)
    AM2 = sb("AM2", [128, 128], F32)
    PB2 = sb("PB2", [128, 128], I32)
    RS2 = sb("RS2", [128, 128], I32)
    PBB2 = sb("PBB2", [128, 128], BF16)
    RSB2 = sb("RSB2", [128, 128], BF16)
    QW2 = sb("QW2", [128, FC, 512], I16)
    BH2 = sb("BH2", [128, 2, 512], BF16)
    BL2 = sb("BL2", [128, 2, 512], BF16)
    B2F = sb("B2F", [1, 2, 512], F32)
    QH2L = sb("QH2L", [128, 2, FC, 128], I16)
    HB2 = sb("HB2", [128, 2, 512], I16)
    XFQ = sb("XFQ", [128, 4, 512], BF16)
    FFB = sb("FFB", [128, 4, 512], I16)
    Y2PS = ps("Y2PS", [128, 4, 512], F32)
    W2TR = ps("W2TR", [128, 16, 128], BF16)

    with ph, nc.Block() as blk:
        def w2_load(sync, idx):
            if idx >= 1:
                sync.wait_ge(S("p2_w2q").h, idx)
            S("p2_w2l").inc(sync.dma_start(
                W2P[:, :],
                E.w2[idx * 128:(idx + 1) * 128, :]), 16)

        chunksA = [(pw, qt) for pw in range(2) for qt in range(4)]
        cptA = (len(chunksA) + NT - 1) // NT
        loads_at = {0: 0, 4 // cptA: 1}

        @blk.sync
        def _(sync):
            for q in range(NQ):
                if q >= 2:
                    sync.wait_ge(S("p2_blo").h, q - 1)
                S("p2_b2l").inc(sync.dma_start(
                    B2F[0:1, q % 2, :], E.b2[0:1, q * 512:(q + 1) * 512]), 16)
                if q == 0:
                    for pw in range(NW2):
                        w2_load(sync, pw)
                else:
                    w2_load(sync, q * NW2 + 2)
                    w2_load(sync, q * NW2 + 3)
                for tt in range(NT):
                    i = q * NT + tt
                    if tt in loads_at and q < NQ - 1:
                        w2_load(sync, (q + 1) * NW2 + loads_at[tt])
                    if i >= 2:
                        sync.wait_ge(S("p2_mm").h, i - 1)
                    S("p2_sl").inc(sync.dma_start(
                        QH2L[:, tt % 2, :, :].bitcast(BF16),
                        bass.AP(E.qh2t_s, tt * 128 * FC * 128,
                                [[FC * 128, 128], [128, FC], [1, 128]])), 16)
                    if i >= 4:
                        sync.wait_ge(S("p2_ff").h, i - 3)
                    S("p2_xfl").inc(sync.dma_start(
                        XFQ[:, i % 4, :],
                        E.xffn_s[tt * 128:(tt + 1) * 128, q * 512:(q + 1) * 512]), 16)

        def w2_quarter(v, idx, qt):
            qw = FF // 4
            if qt == 0:
                v.wait_ge(S("p2_w2l").h, 16 * (idx + 1))
                if idx >= 2:
                    v.wait_ge(S("p2_w2t").h, (FC // 16) * (idx - 1))
            quant_chain(v, W2P[:, qt * qw:(qt + 1) * qw], qw,
                        AM2, PB2, RS2, PBB2, RSB2,
                        U2[:, 0:qw], V2[:, 0:qw].bitcast(BF16),
                        Q2P[:, idx % 2, qt * qw:(qt + 1) * qw].bitcast(BF16),
                        fp32_input=True)
            if qt == 3:
                S("p2_w2q").dinc(v)

        @blk.vector
        def _(v):
            v.memset(BH2[:, :, :], 0.0)
            v.memset(BL2[:, :, :], 0.0)
            S("p2_bz").dinc(v)
            chunksA = [(pw, qt) for pw in range(2) for qt in range(4)]
            cptA = (len(chunksA) + NT - 1) // NT
            for q in range(NQ):
                v.wait_ge(S("p2_b2l").h, 16 * (q + 1))
                if q >= 2:
                    v.wait_ge(S("p2_mm").h, (q - 1) * NT)
                v.tensor_copy(BH2[0:1, q % 2, :], B2F[0:1, q % 2, :])
                v.drain()
                v.tensor_tensor(BL2[0:1, q % 2, :], B2F[0:1, q % 2, :],
                                BH2[0:1, q % 2, :], OP.subtract)
                S("p2_blo").dinc(v)
                if q == 0:
                    for pw in range(NW2):
                        for qt in range(4):
                            w2_quarter(v, pw, qt)
                for tt in range(NT):
                    i = q * NT + tt
                    if q < NQ - 1:
                        for (pw, qt) in chunksA[tt * cptA:(tt + 1) * cptA]:
                            w2_quarter(v, (q + 1) * NW2 + pw, qt)
                    v.wait_ge(S("p2_hb").h, i + 1)
                    v.wait_ge(S("p2_xfl").h, 16 * (i + 1))
                    if i >= 4:
                        v.wait_ge(S("p2_ffs").h, 16 * (i - 3))
                    v.tensor_tensor(FFB[:, i % 4, :].bitcast(BF16),
                                    HB2[:, i % 2, :].bitcast(BF16), XFQ[:, i % 4, :], OP.add)
                    S("p2_ff").dinc(v)
                if q < NQ - 1:
                    for pw in (2, 3):
                        for qt in range(4):
                            w2_quarter(v, (q + 1) * NW2 + pw, qt)

        @blk.tensor
        def _(t):
            for q in range(NQ):
                for pw in range(NW2):
                    idx = q * NW2 + pw
                    t.wait_ge(S("p2_w2q").h, idx + 1)
                    for oc8 in range(FC // 16):
                        gg = (FC // 16) * idx + oc8
                        if gg >= 1:
                            t.wait_ge(S("p2_w2e").h, gg)
                        for c in range(16):
                            oc = oc8 * 16 + c
                            tr = t.transpose(W2TR[:, c, :],
                                             Q2P[:, idx % 2, oc * 128:(oc + 1) * 128].bitcast(BF16),
                                             P.IdB[:, :])
                        S("p2_w2t").inc(tr)
                for tt in range(NT):
                    i = q * NT + tt
                    t.wait_ge(S("p2_w2e").h, (FC // 16) * NW2 * (q + 1))
                    t.wait_ge(S("p2_sl").h, 16 * (i + 1))
                    if tt == 0:
                        t.wait_ge(S("p2_bz").h, 1)
                        t.wait_ge(S("p2_blo").h, q + 1)
                    if i >= 4:
                        t.wait_ge(S("p2_hb").h, i - 3)
                    for oc in range(FC):
                        mm = t.matmul(Y2PS[:, i % 4, :],
                                      QH2L[:, tt % 2, oc, :].bitcast(BF16),
                                      QW2[:, oc, :].bitcast(BF16),
                                      start=(oc == 0), stop=False)
                    t.matmul(Y2PS[:, i % 4, :], P.ONES1[:, :], BH2[:, q % 2, :],
                             start=False, stop=False)
                    mm = t.matmul(Y2PS[:, i % 4, :], P.ONES1[:, :], BL2[:, q % 2, :],
                                  start=False, stop=True)
                    S("p2_mm").inc(mm)

        @blk.scalar
        def _(s):
            for q in range(NQ):
                for pw in range(NW2):
                    idx = q * NW2 + pw
                    for oc8 in range(FC // 16):
                        gg = (FC // 16) * idx + oc8
                        s.wait_ge(S("p2_w2t").h, gg + 1)
                        if q >= 1 and pw == 0 and oc8 == 0:
                            s.wait_ge(S("p2_mm").h, q * NT)
                        s.activation(
                            QW2[:, oc8 * 16:(oc8 + 1) * 16, pw * 128:(pw + 1) * 128].bitcast(BF16),
                            W2TR[:, :, :], AF.Identity, bias=f32(0.0), scale=f32(1.0))
                        S("p2_w2e").dinc(s)
                for tt in range(NT):
                    i = q * NT + tt
                    s.wait_ge(S("p2_mm").h, i + 1)
                    if i >= 2:
                        s.wait_ge(S("p2_ff").h, i - 1)
                    s.activation(HB2[:, i % 2, :].bitcast(BF16), Y2PS[:, i % 4, :],
                                 AF.Identity, bias=f32(0.0), scale=f32(1.0))
                    S("p2_hb").dinc(s)

        @blk.gpsimd
        def _(g):
            for q in range(NQ):
                for tt in range(NT):
                    i = q * NT + tt
                    g.wait_ge(S("p2_ff").h, i + 1)
                    S("p2_ffs").inc(g.dma_start(
                        E.ffn_s[tt * 128:(tt + 1) * 128, q * 512:(q + 1) * 512],
                        FFB[:, i % 4, :].bitcast(BF16)), 16)
            g.wait_ge(S("p2_ffs").h, 16 * NQ * NT)


def _build_qkv(nc, S, E, P, hf, XTh, QTh, KTh, VTh, WQP, BOF):
    TPH = NT // 2
    tt0 = hf * TPH
    PW = max(512, T // 2)
    with nc.psum_tensor(f"QKVPS{hf}", [128, 2, PW], F32) as QKVPS, nc.Block() as blk:
        @blk.sync
        def _(sync):
            if hf == 0:
                S("p3_bo").inc(sync.dma_start(BOF[0:1, :], E.bo[0:1, :]), 16)
            for ti in range(TPH):
                S("p3_xtl").inc(sync.dma_start(
                    XTh[:, ti, :, :],
                    bass.AP(E.xt_s, (tt0 + ti) * 128 * KC * 128,
                            [[KC * 128, 128], [128, KC], [1, 128]])), 16)
        @blk.gpsimd
        def _(g):
            for wi in range(3):
                wd = (E.wq, E.wk, E.wv)[wi]
                for oc in range(KC):
                    li = wi * KC + oc
                    gi = hf * 48 + li
                    if li >= 2:
                        g.wait_ge(S("p3_qkv").h, gi - 1)
                    S("p3_wl").inc(g.dma_start(
                        WQP[:, li % 2, :, :],
                        wd[:, oc * 128:(oc + 1) * 128]
                        .rearrange("(c p) o -> p c o", p=128)), 16)

        @blk.tensor
        def _(t):
            t.wait_ge(S("p3_xtl").h, S("p3_xtl").n)
            for wi in range(3):
                for oc in range(KC):
                    li = wi * KC + oc
                    gi = hf * 48 + li
                    t.wait_ge(S("p3_wl").h, 16 * (gi + 1))
                    if li >= 2:
                        t.wait_ge(S("p3_ev").h, gi - 1)
                    hw = TPH * 64
                    for half in range(2):
                        for kc in range(KC):
                            mm = t.matmul(
                                QKVPS[:, li % 2, half * hw:(half + 1) * hw],
                                WQP[:, li % 2, kc, :],
                                XTh[:, (TPH // 2) * half:(TPH // 2) * (half + 1), kc, :],
                                start=(kc == 0), stop=(kc == KC - 1))
                    S("p3_qkv").inc(mm)

        @blk.scalar
        def _(s):
            for wi in range(3):
                dst = (QTh, KTh, VTh)[wi]
                bias = (P.BQ, P.BK, P.BV)[wi]
                for oc in range(KC):
                    li = wi * KC + oc
                    gi = hf * 48 + li
                    s.wait_ge(S("p3_qkv").h, gi + 1)
                    s.activation(
                        dst[:, oc, :], QKVPS[:, li % 2, 0:T // 2],
                        AF.Identity, bias=bias[:, oc:oc + 1], scale=f32(1.0))
                    S("p3_ev").dinc(s)


def _build_mix(nc, S, P, hf, QTh, KTh, VTh, CTh, LQ, LK, LG,
               ESb, EXb, RMX, RSM, WBt, WTs, VGs):
    NBT = (T // 2) // 32
    ph = contextlib.ExitStack()
    ps = lambda n, sh, d: ph.enter_context(nc.psum_tensor(n, sh, d))
    SCPS = ps(f"SCPS{hf}", [128, 4, 128], F32)
    WTPS = ps(f"WTPS{hf}", [128, 4, 128], BF16)
    VGPS = ps(f"VGPS{hf}", [128, 4, 128], BF16)
    CXPS = ps(f"CXPS{hf}", [128, 4, 128], F32)
    with ph, nc.Block() as blk:
        @blk.vector
        def _(v):
            for bt in range(NBT):
                gb = hf * NBT + bt
                t0 = bt * 32
                if gb >= 1:
                    v.wait_ge(S("mx_sc").h, gb)
                    v.wait_ge(S("mx_vg").h, gb)
                for (Lx, Src) in ((LQ, QTh), (LK, KTh), (LG, VTh)):
                    v.tensor_copy(
                        Lx[:, :, :, :],
                        Src[:, :, t0:t0 + 32].rearrange("p h (j ti) -> p j ti h", ti=8))
                S("mx_g").dinc(v)
                v.wait_ge(S("mx_sc").h, gb + 1)
                v.scalar_tensor_tensor(
                    ESb[:, :, :], SCPS[:, :, :], 1.0,
                    P.MASK[:, None, :].broadcast_to([128, 4, 128]), OP.mult, OP.add)
                S("mx_e0").dinc(v)
                v.tensor_reduce(RMX[:, :], ESb[:, :, :], axis=AX.X, op=OP.max)
                v.drain()
                v.tensor_tensor(ESb[:, :, :], ESb[:, :, :],
                                RMX[:, :, None].broadcast_to([128, 4, 128]), OP.subtract)
                S("mx_e0").dinc(v)
                v.wait_ge(S("mx_ex").h, gb + 1)
                v.tensor_reduce(RSM[:, :], EXb[:, :, :], axis=AX.X, op=OP.add)
                v.drain()
                v.reciprocal(RSM[:, :], RSM[:, :])
                if gb >= 1:
                    v.wait_ge(S("mx_wt").h, gb)
                v.drain()
                v.tensor_tensor(WBt[:, :, :], EXb[:, :, :],
                                RSM[:, :, None].broadcast_to([128, 4, 128]), OP.mult)
                S("mx_wb").dinc(v)

        @blk.tensor
        def _(t):
            for bt in range(NBT):
                gb = hf * NBT + bt
                t.wait_ge(S("mx_g").h, gb + 1)
                if gb >= 1:
                    t.wait_ge(S("mx_e0").h, 2 * gb)
                for j in range(4):
                    mm = t.matmul(SCPS[:, j, :],
                                  LQ[:, j, :, :].rearrange("p a b -> p (a b)"),
                                  LK[:, j, :, :].rearrange("p a b -> p (a b)"),
                                  start=True, stop=True)
                S("mx_sc").inc(mm)
                t.wait_ge(S("mx_wb").h, gb + 1)
                if gb >= 1:
                    t.wait_ge(S("mx_wte").h, gb)
                for j in range(4):
                    tr = t.transpose(WTPS[:, j, :], WBt[:, j, :], P.IdB[:, :])
                S("mx_wt").inc(tr)
                if gb >= 1:
                    t.wait_ge(S("mx_vge").h, gb)
                for j in range(4):
                    tr = t.transpose(VGPS[:, j, :],
                                     LG[:, j, :, :].rearrange("p a b -> p (a b)"), P.IdB[:, :])
                S("mx_vg").inc(tr)
                t.wait_ge(S("mx_wte").h, gb + 1)
                t.wait_ge(S("mx_vge").h, gb + 1)
                if gb >= 1:
                    t.wait_ge(S("mx_ct").h, gb)
                for j in range(4):
                    mm = t.matmul(CXPS[:, j, :], VGs[:, j, :], WTs[:, j, :],
                                  start=True, stop=True)
                S("mx_cx").inc(mm)

        @blk.scalar
        def _(s):
            for bt in range(NBT):
                gb = hf * NBT + bt
                t0 = bt * 32
                s.wait_ge(S("mx_e0").h, 2 * gb + 2)
                s.activation(EXb[:, :, :], ESb[:, :, :], AF.Exp, scale=f32(RSQD))
                S("mx_ex").dinc(s)
                s.wait_ge(S("mx_wt").h, gb + 1)
                s.activation(WTs[:, :, :], WTPS[:, :, :], AF.Identity,
                             bias=f32(0.0), scale=f32(1.0))
                S("mx_wte").dinc(s)
                s.wait_ge(S("mx_vg").h, gb + 1)
                s.activation(VGs[:, :, :], VGPS[:, :, :], AF.Identity,
                             bias=f32(0.0), scale=f32(1.0))
                S("mx_vge").dinc(s)
                s.wait_ge(S("mx_cx").h, gb + 1)
                s.activation(
                    CTh[:, :, t0:t0 + 32].rearrange("p h (j ti) -> p j ti h", ti=8),
                    CXPS[:, :, :].rearrange("p j (ti h) -> p j ti h", h=16),
                    AF.Identity, bias=f32(0.0), scale=f32(1.0))
                S("mx_ct").dinc(s)


_WO_CALL = [0]


def _build_wo(nc, S, E, P, hf, CTh, BOX):
    BOF, BOH, BOL = BOX
    TPH = NT // 2
    tt0 = hf * TPH
    call = _WO_CALL[0]
    _WO_CALL[0] += 1
    ph = contextlib.ExitStack()
    sb = lambda n, sh, d: ph.enter_context(nc.sbuf_tensor(n, sh, d))
    WOB = sb(f"WOB{call}", [128, KC, D], BF16)
    XR = sb(f"XR{call}", [128, 2, D], F32)
    FR = sb(f"FR{call}", [128, 2, D], BF16)
    OB = sb(f"OB{call}", [128, 2, D], F32)
    OPS = ph.enter_context(nc.psum_tensor(f"OPS{call}", [128, 2, D], F32))
    with ph, nc.Block() as blk:
        @blk.gpsimd
        def _(g):
            for kc in range(KC):
                S("p4_wol").inc(g.dma_start(
                    WOB[:, kc, :], E.wo[kc * 128:(kc + 1) * 128, :]), 16)

        @blk.sync
        def _(sync):
            for ti in range(TPH):
                i = call * TPH + ti
                tt = tt0 + ti
                if ti >= 2:
                    sync.wait_ge(S("p4_oev").h, 3 * (i - 1))
                S("p4_xrl").inc(sync.dma_start(
                    XR[:, ti % 2, :], E.x[tt * 128:(tt + 1) * 128, :]), 16)
                S("p4_frl").inc(sync.dma_start(
                    FR[:, ti % 2, :], E.ffn_s[tt * 128:(tt + 1) * 128, :]), 16)

        @blk.tensor
        def _(t):
            t.wait_ge(S("p4_wol").h, 16 * KC * (call + 1))
            t.wait_ge(S("p4_bo2").h, 1)
            for ti in range(TPH):
                i = call * TPH + ti
                if ti >= 2:
                    t.wait_ge(S("p4_oev").h, 3 * (i - 1))
                for kc in range(KC):
                    for oc4 in range(4):
                        mm = t.matmul(OPS[:, ti % 2, oc4 * 512:(oc4 + 1) * 512],
                                      CTh[:, kc, ti * 128:(ti + 1) * 128],
                                      WOB[:, kc, oc4 * 512:(oc4 + 1) * 512],
                                      start=(kc == 0), stop=False)
                for oc4 in range(4):
                    t.matmul(OPS[:, ti % 2, oc4 * 512:(oc4 + 1) * 512], P.ONES1[:, :],
                             BOH[:, oc4 * 512:(oc4 + 1) * 512], start=False, stop=False)
                    mm = t.matmul(OPS[:, ti % 2, oc4 * 512:(oc4 + 1) * 512], P.ONES1[:, :],
                                  BOL[:, oc4 * 512:(oc4 + 1) * 512], start=False, stop=True)
                S("p4_omm").inc(mm)

        @blk.vector
        def _(v):
            if call == 0:
                v.wait_ge(S("p3_bo").h, 16)
                v.memset(BOH[:, :], 0.0)
                v.memset(BOL[:, :], 0.0)
                v.drain()
                v.tensor_copy(BOH[0:1, :], BOF[0:1, :])
                v.drain()
                v.tensor_tensor(BOL[0:1, :], BOF[0:1, :], BOH[0:1, :], OP.subtract)
                S("p4_bo2").dinc(v)
            for ti in range(TPH):
                i = call * TPH + ti
                v.wait_ge(S("p4_omm").h, i + 1)
                v.wait_ge(S("p4_xrl").h, 16 * (i + 1))
                v.wait_ge(S("p4_frl").h, 16 * (i + 1))
                if i >= 2:
                    v.wait_ge(S("p4_ost").h, 16 * (i - 1))
                v.tensor_tensor(
                    OB[:, ti % 2, :], OPS[:, ti % 2, :], XR[:, ti % 2, :], OP.add)
                S("p4_oev").dinc(v)
                S("p4_oev").dinc(v)
                v.tensor_tensor(
                    OB[:, ti % 2, :], OB[:, ti % 2, :], FR[:, ti % 2, :], OP.add)
                S("p4_oev").dinc(v)

        @blk.scalar
        def _(s):
            for ti in range(TPH):
                i = call * TPH + ti
                tt = tt0 + ti
                s.wait_ge(S("p4_oev").h, 3 * (i + 1))
                S("p4_ost").inc(s.dma_start(
                    E.out[tt * 128:(tt + 1) * 128, :], OB[:, ti % 2, :]), 16)
            if call == 1:
                s.wait_ge(S("p4_ost").h, 16 * NT)


def _build_p34(nc, S, E, P):
    outer = contextlib.ExitStack()
    CTh = outer.enter_context(nc.sbuf_tensor("CTh", [128, KC, T // 2], BF16))
    BOH = outer.enter_context(nc.sbuf_tensor("BOH", [128, D], BF16))
    BOL = outer.enter_context(nc.sbuf_tensor("BOL", [128, D], BF16))
    BOF = outer.enter_context(nc.sbuf_tensor("BOF", [1, D], F32))
    with outer:
        ph = contextlib.ExitStack()
        sb = lambda n, sh, d: ph.enter_context(nc.sbuf_tensor(n, sh, d))
        XTh = sb("XTh", [128, NT // 2, KC, 128], BF16)
        QTh = sb("QTh", [128, KC, T // 2], BF16)
        KTh = sb("KTh", [128, KC, T // 2], BF16)
        VTh = sb("VTh", [128, KC, T // 2], BF16)
        WQP = sb("WQP", [128, 2, KC, 128], BF16)
        LQ = sb("LQ", [128, 4, 8, 16], BF16)
        LK = sb("LK", [128, 4, 8, 16], BF16)
        LG = sb("LG", [128, 4, 8, 16], BF16)
        ESb = sb("ESb", [128, 4, 128], F32)
        EXb = sb("EXb", [128, 4, 128], F32)
        RMX = sb("RMX", [128, 4], F32)
        RSM = sb("RSM", [128, 4], F32)
        WBt = sb("WBt", [128, 4, 128], BF16)
        WTs = sb("WTs", [128, 4, 128], BF16)
        VGs = sb("VGs", [128, 4, 128], BF16)
        with ph:
            for hf in range(2):
                _build_qkv(nc, S, E, P, hf, XTh, QTh, KTh, VTh, WQP, BOF)
                _build_mix(nc, S, P, hf, QTh, KTh, VTh, CTh, LQ, LK, LG,
                           ESb, EXb, RMX, RSM, WBt, WTs, VGs)
                if hf == 0:
                    with nc.Block() as blk:
                        @blk.gpsimd
                        def _(g):
                            S("ct_sp").inc(g.dma_start(
                                bass.AP(E.ct_s, 0, [[KC * (T // 2), 128], [T // 2, KC], [1, T // 2]]),
                                CTh[:, :, :]), 16)
                            g.wait_ge(S("ct_sp").h, 16)
        _build_wo(nc, S, E, P, 1, CTh, (BOF, BOH, BOL))
        with nc.Block() as blk:
            @blk.gpsimd
            def _(g):
                S("ct_sp").inc(g.dma_start(
                    CTh[:, :, :],
                    bass.AP(E.ct_s, 0, [[KC * (T // 2), 128], [T // 2, KC], [1, T // 2]])), 16)
                g.wait_ge(S("ct_sp").h, 32)
        _build_wo(nc, S, E, P, 0, CTh, (BOF, BOH, BOL))


class _NS:
    pass


def build_program():
    nc = bass.Bass("TRN2", target_bir_lowering=False, detect_race_conditions=False)
    E = _NS()
    E.x = nc.dram_tensor("x", [T, D], F32, kind="ExternalInput")
    E.wq = nc.dram_tensor("wq", [D, D], F32, kind="ExternalInput")
    E.wk = nc.dram_tensor("wk", [D, D], F32, kind="ExternalInput")
    E.wv = nc.dram_tensor("wv", [D, D], F32, kind="ExternalInput")
    E.wo = nc.dram_tensor("wo", [D, D], F32, kind="ExternalInput")
    E.bq = nc.dram_tensor("bq", [1, D], F32, kind="ExternalInput")
    E.bk = nc.dram_tensor("bk", [1, D], F32, kind="ExternalInput")
    E.bv = nc.dram_tensor("bv", [1, D], F32, kind="ExternalInput")
    E.bo = nc.dram_tensor("bo", [1, D], F32, kind="ExternalInput")
    E.g2 = nc.dram_tensor("g2", [1, D], F32, kind="ExternalInput")
    E.be2 = nc.dram_tensor("be2", [1, D], F32, kind="ExternalInput")
    E.lnw = nc.dram_tensor("lnw", [1, D], F32, kind="ExternalInput")
    E.lnb = nc.dram_tensor("lnb", [1, D], F32, kind="ExternalInput")
    E.w1 = nc.dram_tensor("w1", [FF, D], F32, kind="ExternalInput")
    E.b1 = nc.dram_tensor("b1", [1, FF], F32, kind="ExternalInput")
    E.w2 = nc.dram_tensor("w2", [D, FF], F32, kind="ExternalInput")
    E.b2 = nc.dram_tensor("b2", [1, D], F32, kind="ExternalInput")
    E.out = nc.dram_tensor("out", [T, D], F32, kind="ExternalOutput")
    kw = {"kind": "ExternalOutput"} if DEBUG_SCRATCH else {}
    E.xt_s = nc.dram_tensor("xt_s", [NT, 128, KC, 128], BF16, **kw)
    E.xffn_s = nc.dram_tensor("xffn_s", [T, D], BF16, **kw)
    E.qh2t_s = nc.dram_tensor("qh2t_s", [NT, 128, FC, 128], BF16, **kw)
    E.ffn_s = nc.dram_tensor("ffn_s", [T, D], BF16, **kw)
    E.ct_s = nc.dram_tensor("ct_s", [128, KC, T // 2], BF16, **kw)
    if DEBUG_SCRATCH:
        E.dbg_qht = nc.dram_tensor("dbg_qht", [128, KC, T], BF16, kind="ExternalOutput")
        E.dbg_qhi = nc.dram_tensor("dbg_qhi", [128, D], BF16, kind="ExternalOutput")
        E.dbg_yb = nc.dram_tensor("dbg_yb", [NT, 128, 512], BF16, kind="ExternalOutput")
        E.dbg_gi = nc.dram_tensor("dbg_gi", [NT, 128, 512], BF16, kind="ExternalOutput")
    else:
        E.dbg_qht = E.dbg_qhi = E.dbg_yb = E.dbg_gi = None

    persist = contextlib.ExitStack()
    sbp = lambda n, sh, d: persist.enter_context(nc.sbuf_tensor(n, sh, d))
    P = _NS()
    P.IdF = sbp("IdF", [128, 128], F32)
    P.IdB = sbp("IdB", [128, 128], BF16)
    P.MASK = sbp("MASK", [128, 128], F32)
    P.BQ = sbp("BQ", [128, KC], F32)
    P.BK = sbp("BK", [128, KC], F32)
    P.BV = sbp("BV", [128, KC], F32)
    P.G2C = sbp("G2C", [128, KC], F32)
    P.B2C = sbp("B2C", [128, KC], F32)
    P.ONES1 = sbp("ONES1", [128, 128], BF16)

    sems = {}

    def S(name):
        if name not in sems:
            sems[name] = Sem(nc, persist, name)
        return sems[name]

    _build_init(nc, S, E, P)
    with contextlib.ExitStack() as s01:
        QHT = s01.enter_context(nc.sbuf_tensor("QHT", [128, KC, T], I16))
        _build_p0(nc, S, E, P, QHT)
        if BUILD_PHASES == "all":
            _build_p1(nc, S, E, P, QHT)
    if BUILD_PHASES == "all":
        _build_p2(nc, S, E, P)
        _build_p34(nc, S, E, P)
    return nc, persist


_CACHE = {}


def prepare_maps(inputs):
    x = np.ascontiguousarray(np.asarray(inputs["L0_attn_x"], dtype=np.float32).reshape(16384, D))
    w = {
        "wq": inputs["L0_attn_Wq"], "wk": inputs["L0_attn_Wk"],
        "wv": inputs["L0_attn_Wv"], "wo": inputs["L0_attn_Wo"],
        "bq": np.asarray(inputs["L0_attn_bq"]).reshape(1, D),
        "bk": np.asarray(inputs["L0_attn_bk"]).reshape(1, D),
        "bv": np.asarray(inputs["L0_attn_bv"]).reshape(1, D),
        "bo": np.asarray(inputs["L0_attn_bo"]).reshape(1, D),
        "g2": np.asarray(inputs["L0_attn_ln_g"]).reshape(1, D),
        "be2": np.asarray(inputs["L0_attn_ln_b"]).reshape(1, D),
        "lnw": np.asarray(inputs["L0_ffn_ln_w"]).reshape(1, D),
        "lnb": np.asarray(inputs["L0_ffn_ln_b"]).reshape(1, D),
        "w1": inputs["L0_ffn_W1"], "b1": np.asarray(inputs["L0_ffn_b1"]).reshape(1, FF),
        "w2": inputs["L0_ffn_W2"], "b2": np.asarray(inputs["L0_ffn_b2"]).reshape(1, D),
    }
    w = {k: np.ascontiguousarray(np.asarray(v, dtype=np.float32)) for k, v in w.items()}
    in_maps = []
    for c in range(NCORES):
        m = dict(w)
        m["x"] = np.ascontiguousarray(x[c * T:(c + 1) * T, :])
        in_maps.append(m)
    return in_maps


def kernel(**inputs):
    in_maps = prepare_maps(inputs)
    if "nc" not in _CACHE:
        _CACHE["nc"], _CACHE["persist"] = build_program()
    nc = _CACHE["nc"]
    res = run_bass_kernel_spmd(nc, in_maps, core_ids=list(range(NCORES)))
    out = np.concatenate([np.asarray(r["out"]) for r in res.results], axis=0)
    return out.reshape(4, 4096, 2048)


# version 12
# speedup vs baseline: 1.4293x; 1.0301x over previous
import contextlib
import numpy as np
import concourse.bass as bass
import concourse.mybir as mybir
from concourse.bass_utils import run_bass_kernel_spmd

F32 = mybir.dt.float32
BF16 = mybir.dt.bfloat16
I16 = mybir.dt.int16
I32 = mybir.dt.int32
AF = mybir.ActivationFunctionType
OP = mybir.AluOpType
AX = mybir.AxisListType

DEBUG_SCRATCH = False
BUILD_PHASES = "all"
NCORES = 8
T = 2048
D = 2048
FF = 8192
NT = T // 128
KC = D // 128
FC = FF // 128
C2 = 196608.0
MASK_EXP = 0xFF800000 - (1 << 32)
K_RS = float(0x7F000000)
CLIP_HI = 1.984375
EPS = 1e-5
RSQD = float(1.0 / np.float32(np.sqrt(np.float32(2048.0))))
f32 = np.float32


class Sem:
    def __init__(self, nc, es, name):
        self.h = es.enter_context(nc.semaphore(name))
        self.n = 0

    def inc(self, instr, k=1):
        instr.then_inc(self.h, k)
        self.n += k
        return self.n

    def dinc(self, eng, k=1):
        eng.drain().then_inc(self.h, k)
        self.n += k
        return self.n


def quant_chain(v, X_ap, n_el, AM, PB, RS, PBB, RSB, U_ap, V_ap, Q_ap, fp32_input, npart=128):
    nb = n_el // 32
    nb2 = nb // 2
    c2 = n_el // 2
    hs = [(slice(0, nb2), slice(0, c2)), (slice(nb2, nb), slice(c2, n_el))]
    v.drain()
    for bs, cs in hs:
        v.tensor_reduce(AM[:, bs], X_ap[:, cs].rearrange("p (b j) -> p b j", j=32),
                        axis=AX.X, op=OP.max, apply_absolute_value=True)
    for bs, cs in hs:
        v.tensor_single_scalar(PB[:, bs], AM[:, bs].bitcast(I32), MASK_EXP, OP.bitwise_and)
    v.drain()
    for bs, cs in hs:
        v.tensor_scalar(RS[:, bs], PB[:, bs], -1.0, K_RS, OP.mult, OP.add)
    for bs, cs in hs:
        v.tensor_copy(PBB[:, bs], PB[:, bs].bitcast(F32))
    v.drain()
    if not fp32_input:
        for bs, cs in hs:
            v.tensor_copy(RSB[:, bs], RS[:, bs].bitcast(F32))
        v.drain()
    for bs, cs in hs:
        rs_b = (RS[:, bs].bitcast(F32) if fp32_input else RSB[:, bs])
        v.tensor_tensor(U_ap[:, cs].rearrange("p (b j) -> p b j", j=32),
                        X_ap[:, cs].rearrange("p (b j) -> p b j", j=32),
                        rs_b[:, :, None].broadcast_to([npart, nb2, 32]), OP.mult)
    for bs, cs in hs:
        v.tensor_scalar(V_ap[:, cs], U_ap[:, cs], C2, -C2, OP.add, OP.add)
    for bs, cs in hs:
        q = v.scalar_tensor_tensor(Q_ap[:, cs].rearrange("p (b j) -> p b j", j=32),
                                   V_ap[:, cs].rearrange("p (b j) -> p b j", j=32),
                                   CLIP_HI,
                                   PBB[:, bs][:, :, None].broadcast_to([npart, nb2, 32]),
                                   OP.min, OP.mult)
    return q


def _build_init(nc, S, E, P):
    ph = contextlib.ExitStack()
    sb = lambda n, sh, d: ph.enter_context(nc.sbuf_tensor(n, sh, d))
    Pidx = sb("Pidx", [128, 1], I32)
    Cidx = sb("Cidx", [128, 128], I32)
    CBl = sb("CBl", [128, 128], I32)
    TDm = sb("TDm", [128, 128], F32)
    GEm = sb("GEm", [128, 128], F32)
    with ph:
        with nc.Block() as blk:
            @blk.gpsimd
            def _(g):
                g.iota(Pidx[:, :], [[1, 1]], channel_multiplier=1)
                g.iota(Cidx[:, :], [[1, 128]], channel_multiplier=0)
                g.iota(CBl[:, :].rearrange("p (a b) -> p a b", b=16),
                       [[1, 8], [0, 16]], channel_multiplier=0)

            @blk.sync
            def _(sync):
                ldi = S("ldi")
                for dst, src in ((P.BQ, E.bq), (P.BK, E.bk), (P.BV, E.bv),
                                 (P.G2C, E.g2), (P.B2C, E.be2)):
                    ldi.inc(sync.dma_start(
                        dst[:, :], src[0:1, :].rearrange("o (c p) -> p (o c)", p=128),
                        allow_slow_non_contiguous=True), 16)
                sync.wait_ge(ldi.h, ldi.n)

        with nc.Block() as blk:
            @blk.vector
            def _(v):
                v.tensor_tensor(P.IdF[:, :], Cidx[:, :],
                                Pidx[:, :].broadcast_to([128, 128]), OP.is_equal)
                v.drain()
                v.tensor_copy(P.IdB[:, :], P.IdF[:, :])
                v.tensor_single_scalar(TDm[:, 0:1], Pidx[:, :], 0.0, OP.is_le)
                v.drain()
                v.tensor_copy(P.ONES1[:, :], TDm[:, 0:1].broadcast_to([128, 128]))
                v.drain()
                v.tensor_scalar(TDm[:, :], CBl[:, :], -16.0, None, OP.mult)
                v.drain()
                v.tensor_tensor(TDm[:, :], Pidx[:, :].broadcast_to([128, 128]), TDm[:, :], OP.add)
                v.drain()
                v.tensor_single_scalar(GEm[:, :], TDm[:, :], 0.0, OP.is_ge)
                v.drain()
                v.tensor_single_scalar(TDm[:, :], TDm[:, :], 15.0, OP.is_le)
                v.drain()
                v.tensor_tensor(GEm[:, :], GEm[:, :], TDm[:, :], OP.logical_and)
                v.drain()
                v.tensor_scalar(P.MASK[:, :], GEm[:, :], 1e30, -1e30, OP.mult, OP.add)


def _build_p0(nc, S, E, P, QHT):
    ph = contextlib.ExitStack()
    sb = lambda n, sh, d: ph.enter_context(nc.sbuf_tensor(n, sh, d))
    ps = lambda n, sh, d: ph.enter_context(nc.psum_tensor(n, sh, d))
    XIN = sb("XIN", [128, 2, D], F32)
    XN = sb("XN", [128, D], F32)
    T2 = sb("T2", [128, D], F32)
    T3 = sb("T3", [128, D], F32)
    H3T = sb("H3T", [128, D], F32)
    XFN = sb("XFN", [128, D], I16)
    XFB = sb("XFB", [128, 2, D], BF16)
    LNW = sb("LNW", [128, D], F32)
    LNB = sb("LNB", [128, D], F32)
    HQ = sb("HQ", [128, D], I16)
    UQ = sb("UQ", [128, D], I16)
    VQ = sb("VQ", [128, D], I16)
    QHI = sb("QHI", [128, D], I16)
    AM0 = sb("AM0", [128, 64], F32)
    PB0 = sb("PB0", [128, 64], I32)
    RS0 = sb("RS0", [128, 64], I32)
    PBB0 = sb("PBB0", [128, 64], BF16)
    RSB0 = sb("RSB0", [128, 64], BF16)
    STt = sb("STt", [128, 24], F32)
    MVt = sb("MVt", [128, 2], F32)
    VARt = sb("VARt", [128, 1], F32)
    SQt = sb("SQt", [128, 1], F32)
    R0t = sb("R0t", [128, 1], F32)
    A1t = sb("A1t", [128, 1], F32)
    RRt = sb("RRt", [128, 3], F32)
    NBt = sb("NBt", [128, 3], F32)
    XTB = sb("XTB", [128, 2, KC, 128], BF16)
    TRP = ps("TRP", [128, KC, 128], F32)
    TRQ = ps("TRQ", [128, KC, 128], BF16)

    def stats(v, src_ap):
        v.drain()
        for c in range(4):
            v.bn_stats(STt[:, 6 * c:6 * c + 6], src_ap[:, 512 * c:512 * (c + 1)])
        v.drain()
        v.bn_aggr(MVt[:, :].rearrange("p (a t) -> p a t", t=2),
                  STt[:, :].rearrange("p (g t) -> p g t", t=3))
        v.drain()
        return v.tensor_single_scalar(VARt[:, :], MVt[:, 1:2], EPS, OP.add)

    def rstd(v, k):
        v.reciprocal(RRt[:, k:k + 1], SQt[:, :])
        v.drain()
        v.tensor_tensor(NBt[:, k:k + 1], MVt[:, 0:1], RRt[:, k:k + 1], OP.mult)
        v.drain()
        return v.tensor_single_scalar(NBt[:, k:k + 1], NBt[:, k:k + 1], -1.0, OP.mult)

    with ph, nc.Block() as blk:
        @blk.sync
        def _(sync):
            ldw = S("p0_lnwb")
            ldw.inc(sync.dma_start(LNW[:, :], E.lnw[0:1, :].partition_broadcast(128)), 16)
            ldw.inc(sync.dma_start(LNB[:, :], E.lnb[0:1, :].partition_broadcast(128)), 16)
            for tt in range(NT):
                if tt >= 2:
                    sync.wait_ge(S("p0_xn").h, tt - 1)
                S("p0_ldx").inc(sync.dma_start(
                    XIN[:, tt % 2, :], E.x[tt * 128:(tt + 1) * 128, :]), 16)

        @blk.vector
        def _(v):
            for tt in range(NT):
                b = tt % 2
                v.wait_ge(S("p0_ldx").h, 16 * (tt + 1))
                if tt == 0:
                    v.wait_ge(S("p0_lnwb").h, 32)
                stats(v, XIN[:, b, :])
                S("p0_var").dinc(v)
                v.wait_ge(S("p0_sqd").h, 3 * tt + 1)
                rstd(v, 0)
                S("p0_r").dinc(v)
                v.wait_ge(S("p0_xn").h, tt + 1)
                stats(v, XN[:, :])
                S("p0_var").dinc(v)
                v.wait_ge(S("p0_sqd").h, 3 * tt + 2)
                rstd(v, 1)
                if tt >= 2:
                    v.wait_ge(S("p0_t3").h, tt - 1)
                    v.wait_ge(S("p0_xfs").h, 16 * (tt - 1))
                v.drain()
                v.tensor_copy(XFN[:, :].bitcast(BF16), XN[:, :])
                S("p0_r").dinc(v)
                v.tensor_copy(XFB[:, b, :], XFN[:, :].bitcast(BF16))
                S("p0_xfb").dinc(v)
                stats(v, XFB[:, b, :])
                S("p0_var").dinc(v)
                v.wait_ge(S("p0_sqd").h, 3 * tt + 3)
                rstd(v, 2)
                S("p0_r").dinc(v)
                v.wait_ge(S("p0_tr1").h, tt + 1)
                if tt >= 2:
                    v.wait_ge(S("p0_xts").h, 16 * (tt - 1))
                for c in range(KC):
                    v.tensor_scalar(XTB[:, b, c, :], TRP[:, c, :],
                                    P.G2C[:, c:c + 1], P.B2C[:, c:c + 1], OP.mult, OP.add)
                S("p0_xtb").dinc(v)
                v.wait_ge(S("p0_t3").h, tt + 1)
                v.scalar_tensor_tensor(H3T[:, :], T3[:, :], 1.0, LNW[:, :], OP.mult, OP.mult)
                v.drain()
                v.scalar_tensor_tensor(HQ[:, :].bitcast(BF16), H3T[:, :], 1.0, LNB[:, :],
                                       OP.mult, OP.add)
                if tt >= 1:
                    v.wait_ge(S("p0_trq").h, tt)
                if DEBUG_SCRATCH and tt == 1:
                    v.wait_ge(S("p0_dbg").h, 16)
                quant_chain(v, HQ[:, :].bitcast(BF16), D, AM0, PB0, RS0, PBB0, RSB0,
                            UQ[:, :].bitcast(BF16), VQ[:, :].bitcast(BF16),
                            QHI[:, :].bitcast(BF16), fp32_input=False)
                S("p0_qh").dinc(v)

        @blk.scalar
        def _(s):
            for tt in range(NT):
                b = tt % 2
                for k in range(3):
                    s.wait_ge(S("p0_var").h, 3 * tt + k + 1)
                    s.activation(SQt[:, :], VARt[:, :], AF.Sqrt,
                                 bias=f32(0.0), scale=f32(1.0))
                    S("p0_sqd").dinc(s)
                    if k == 0:
                        s.wait_ge(S("p0_r").h, 3 * tt + 1)
                        s.activation(XN[:, :], XIN[:, b, :], AF.Identity,
                                     bias=NBt[:, 0:1], scale=RRt[:, 0:1])
                        S("p0_xn").dinc(s)
                    elif k == 1:
                        s.wait_ge(S("p0_r").h, 3 * tt + 2)
                        s.activation(T2[:, :], XN[:, :], AF.Identity,
                                     bias=NBt[:, 1:2], scale=RRt[:, 1:2])
                        S("p0_t2").dinc(s)
                    else:
                        s.wait_ge(S("p0_r").h, 3 * tt + 3)
                        s.activation(T3[:, :], XFB[:, b, :], AF.Identity,
                                     bias=NBt[:, 2:3], scale=RRt[:, 2:3])
                        S("p0_t3").dinc(s)
                s.wait_ge(S("p0_trq").h, tt + 1)
                s.activation(
                    QHT[:, :, tt * 128:(tt + 1) * 128].bitcast(BF16),
                    TRQ[:, :, :], AF.Identity, bias=f32(0.0), scale=f32(1.0))
                S("p0_qht").dinc(s)

        @blk.tensor
        def _(t):
            for tt in range(NT):
                t.wait_ge(S("p0_t2").h, tt + 1)
                if tt >= 1:
                    t.wait_ge(S("p0_xtb").h, tt)
                for c in range(KC):
                    tr = t.transpose(TRP[:, c, :], T2[:, c * 128:(c + 1) * 128], P.IdF[:, :])
                S("p0_tr1").inc(tr)
                t.wait_ge(S("p0_qh").h, tt + 1)
                if tt >= 1:
                    t.wait_ge(S("p0_qht").h, tt)
                for c in range(KC):
                    tr = t.transpose(TRQ[:, c, :],
                                     QHI[:, c * 128:(c + 1) * 128].bitcast(BF16), P.IdB[:, :])
                S("p0_trq").inc(tr)

        @blk.gpsimd
        def _(g):
            if DEBUG_SCRATCH:
                g.wait_ge(S("p0_qh").h, 1)
                S("p0_dbg").inc(g.dma_start(E.dbg_qhi[:, :], QHI[:, :].bitcast(BF16)), 16)
            for tt in range(NT):
                b = tt % 2
                g.wait_ge(S("p0_xtb").h, tt + 1)
                S("p0_xts").inc(g.dma_start(
                    bass.AP(E.xt_s, tt * 128 * KC * 128,
                            [[KC * 128, 128], [128, KC], [1, 128]]),
                    XTB[:, b, :, :]), 16)
                g.wait_ge(S("p0_xfb").h, tt + 1)
                S("p0_xfs").inc(g.dma_start(
                    E.xffn_s[tt * 128:(tt + 1) * 128, :], XFB[:, b, :]), 16)
            if DEBUG_SCRATCH:
                g.wait_ge(S("p0_qht").h, NT)
                S("p0_dbg").inc(g.dma_start(
                    E.dbg_qht[:, :, :].rearrange("p c t -> p (c t)"),
                    QHT[:, :, :].bitcast(BF16).rearrange("p c t -> p (c t)")), 16)
                g.wait_ge(S("p0_dbg").h, 32)
            g.wait_ge(S("p0_xts").h, 16 * NT)
            g.wait_ge(S("p0_xfs").h, 16 * NT)


def _build_p1(nc, S, E, P, QHT):
    NPAN = FF // 512
    ph = contextlib.ExitStack()
    sb = lambda n, sh, d: ph.enter_context(nc.sbuf_tensor(n, sh, d))
    ps = lambda n, sh, d: ph.enter_context(nc.psum_tensor(n, sh, d))
    W1P = sb("W1P", [128, 2, 2, D], F32)
    U1 = sb("U1", [128, 1024], F32)
    V1 = sb("V1", [128, 1024], I16)
    Q1P = sb("Q1P", [128, 2, 2, D], I16)
    AM1 = sb("AM1", [128, 128], F32)
    PB1 = sb("PB1", [128, 128], I32)
    RS1 = sb("RS1", [128, 128], I32)
    PBB1 = sb("PBB1", [128, 128], BF16)
    RSB1 = sb("RSB1", [128, 128], BF16)
    QW1 = sb("QW1", [128, 2, KC, 512], I16)
    BH1 = sb("BH1", [128, 2, 512], BF16)
    BL1 = sb("BL1", [128, 2, 512], BF16)
    B1F = sb("B1F", [1, 2, 512], F32)
    YB = sb("YB", [128, 512], I16)
    GI = sb("GI", [128, 4, 512], I16)
    UG = sb("UG", [128, 512], I16)
    VG = sb("VG", [128, 512], I16)
    QG = sb("QG", [128, NT, 512], I16)
    AMG = sb("AMG", [128, 16], F32)
    PBG = sb("PBG", [128, 16], I32)
    RSG = sb("RSG", [128, 16], I32)
    PBBG = sb("PBBG", [128, 16], BF16)
    RSBG = sb("RSBG", [128, 16], BF16)
    QH2B = sb("QH2B", [128, 4, 4, 128], I16)
    Y1PS = ps("Y1PS", [128, 4, 512], F32)
    W1TR = ps("W1TR", [128, KC, 128], BF16)
    H2TR = ps("H2TR", [128, 2, 4, 256], BF16)

    with ph, nc.Block() as blk:
        @blk.sync
        def _(sync):
            for pn in range(NPAN):
                for r in range(2):
                    idx = pn * 2 + r
                    if idx >= 2:
                        sync.wait_ge(S("p1_w1q").h, idx - 1)
                    S("p1_w1l").inc(sync.dma_start(
                        W1P[:, idx % 2, :, :],
                        E.w1[pn * 512 + r * 256: pn * 512 + (r + 1) * 256, :]
                        .rearrange("(rr p) k -> p rr k", p=128)), 16)
                if pn >= 2:
                    sync.wait_ge(S("p1_mm").h, (pn - 1) * NT)
                S("p1_b1l").inc(sync.dma_start(
                    B1F[0:1, pn % 2, :], E.b1[0:1, pn * 512:(pn + 1) * 512]), 16)

        def w1_sub(v, idx, s):
            if s == 0:
                v.wait_ge(S("p1_w1l").h, 16 * (idx + 1))
                if idx >= 2:
                    v.wait_ge(S("p1_w1t").h, 2 * (idx - 1))
            sl = slice(s * 1024, (s + 1) * 1024)
            quant_chain(
                v, W1P[:, idx % 2, :, :].rearrange("p r k -> p (r k)")[:, sl], 1024,
                AM1, PB1, RS1, PBB1, RSB1,
                U1[:, :],
                V1[:, :].bitcast(BF16),
                Q1P[:, idx % 2, :, :].rearrange("p r k -> p (r k)")[:, sl].bitcast(BF16),
                fp32_input=True)
            if s == 3:
                S("p1_w1q").dinc(v)

        def _gi_tile(v, pn, tt):
            i = pn * NT + tt
            if i >= 1:
                v.wait_ge(S("p1_gi").h, i)
            if i >= NT + 1:
                v.wait_ge(S("p1_h2t").h, i - NT)
            if DEBUG_SCRATCH and pn == 0 and tt == 1:
                v.wait_ge(S("p1_dbg").h, 16)
            if i >= 1:
                quant_chain(v, GI[:, (i - 1) % 4, :].bitcast(BF16), 512,
                            AMG, PBG, RSG, PBBG, RSBG,
                            UG[:, :].bitcast(BF16), VG[:, :].bitcast(BF16),
                            QG[:, (i - 1) % NT, :].bitcast(BF16), fp32_input=False)
                S("p1_qg").dinc(v)
            if i == NPAN * NT - 1:
                v.wait_ge(S("p1_gi").h, i + 1)
                quant_chain(v, GI[:, i % 4, :].bitcast(BF16), 512,
                            AMG, PBG, RSG, PBBG, RSBG,
                            UG[:, :].bitcast(BF16), VG[:, :].bitcast(BF16),
                            QG[:, i % NT, :].bitcast(BF16), fp32_input=False)
                S("p1_qg").dinc(v)

        @blk.vector
        def _(v):
            v.memset(BH1[:, :, :], 0.0)
            v.memset(BL1[:, :, :], 0.0)
            S("p1_bz").dinc(v)
            subsB = [(r, s) for r in range(2) for s in range(4)]
            for pn in range(NPAN):
                v.wait_ge(S("p1_b1l").h, 16 * (pn + 1))
                if pn >= 2:
                    v.wait_ge(S("p1_mm").h, (pn - 1) * NT)
                v.tensor_copy(BH1[0:1, pn % 2, :], B1F[0:1, pn % 2, :])
                v.drain()
                v.tensor_tensor(BL1[0:1, pn % 2, :], B1F[0:1, pn % 2, :],
                                BH1[0:1, pn % 2, :], OP.subtract)
                S("p1_blo").dinc(v)
                if pn == 0:
                    for r in range(2):
                        for s in range(4):
                            w1_sub(v, r, s)
                for tt in range(NT):
                    if pn < NPAN - 1:
                        for (r, s) in subsB[tt * 8 // NT:(tt + 1) * 8 // NT]:
                            w1_sub(v, 2 * (pn + 1) + r, s)
                    _gi_tile(v, pn, tt)

        @blk.tensor
        def _(t):
            for pn in range(NPAN):
                for r in range(2):
                    idx = pn * 2 + r
                    t.wait_ge(S("p1_w1q").h, idx + 1)
                    for pr in range(2):
                        gidx = idx * 2 + pr
                        if gidx >= 1:
                            t.wait_ge(S("p1_w1e").h, gidx)
                        for c in range(KC):
                            tr = t.transpose(W1TR[:, c, :],
                                             Q1P[:, idx % 2, pr, c * 128:(c + 1) * 128].bitcast(BF16),
                                             P.IdB[:, :])
                        S("p1_w1t").inc(tr)
                def h2_tr(t, j):
                    t.wait_ge(S("p1_qg").h, j + 1)
                    if j >= 2:
                        t.wait_ge(S("p1_h2e").h, j - 1)
                    for c in range(4):
                        tr = t.transpose(H2TR[:, j % 2, c, 0:128],
                                         QG[:, j % NT, c * 128:(c + 1) * 128].bitcast(BF16),
                                         P.IdB[:, :])
                    S("p1_h2t").inc(tr)

                for tt in range(NT):
                    i = pn * NT + tt
                    t.wait_ge(S("p1_w1e").h, 4 * (pn + 1))
                    if tt == 0:
                        t.wait_ge(S("p1_bz").h, 1)
                        t.wait_ge(S("p1_blo").h, pn + 1)
                    if i >= 4:
                        t.wait_ge(S("p1_yb").h, i - 3)
                    for kc in range(KC):
                        mm = t.matmul(Y1PS[:, tt % 4, :],
                                      QHT[:, kc, tt * 128:(tt + 1) * 128].bitcast(BF16),
                                      QW1[:, pn % 2, kc, :].bitcast(BF16),
                                      start=(kc == 0), stop=False)
                    t.matmul(Y1PS[:, tt % 4, :], P.ONES1[:, :], BH1[:, pn % 2, :],
                             start=False, stop=False)
                    mm = t.matmul(Y1PS[:, tt % 4, :], P.ONES1[:, :], BL1[:, pn % 2, :],
                                  start=False, stop=True)
                    S("p1_mm").inc(mm)
                    if i >= 2:
                        h2_tr(t, i - 2)
                    if i == NPAN * NT - 1:
                        h2_tr(t, i - 1)
                        h2_tr(t, i)

        @blk.scalar
        def _(s):
            for pn in range(NPAN):
                for r in range(2):
                    idx = pn * 2 + r
                    for pr in range(2):
                        gidx = idx * 2 + pr
                        s.wait_ge(S("p1_w1t").h, gidx + 1)
                        if pn >= 2 and r == 0 and pr == 0:
                            s.wait_ge(S("p1_mm").h, (pn - 1) * NT)
                        s.activation(
                            QW1[:, pn % 2, :,
                                (r * 2 + pr) * 128:(r * 2 + pr + 1) * 128].bitcast(BF16),
                            W1TR[:, :, :], AF.Identity, bias=f32(0.0), scale=f32(1.0))
                        S("p1_w1e").dinc(s)
                for tt in range(NT):
                    i = pn * NT + tt
                    s.wait_ge(S("p1_mm").h, i + 1)
                    if i >= 4:
                        s.wait_ge(S("p1_qg").h, i - 3)
                    if DEBUG_SCRATCH and pn == 0 and tt == 1:
                        s.wait_ge(S("p1_dbg").h, 32)
                    s.activation(YB[:, :].bitcast(BF16), Y1PS[:, tt % 4, :],
                                 AF.Identity, bias=f32(0.0), scale=f32(1.0))
                    S("p1_yb").dinc(s)
                    s.activation(GI[:, i % 4, :].bitcast(BF16), YB[:, :].bitcast(BF16),
                                 AF.Gelu_apprx_tanh)
                    S("p1_gi").dinc(s)

                    def h2_ev(s, j):
                        s.wait_ge(S("p1_h2t").h, j + 1)
                        if j >= 4:
                            s.wait_ge(S("p1_h2s").h, 16 * (j - 3))
                        s.activation(QH2B[:, j % 4, :, :].bitcast(BF16),
                                     H2TR[:, j % 2, :, 0:128], AF.Identity,
                                     bias=f32(0.0), scale=f32(1.0))
                        S("p1_h2e").dinc(s)

                    if i >= 2:
                        h2_ev(s, i - 2)
                    if i == NPAN * NT - 1:
                        h2_ev(s, i - 1)
                        h2_ev(s, i)

        @blk.gpsimd
        def _(g):
            if DEBUG_SCRATCH:
                g.wait_ge(S("p1_yb").h, 1)
                S("p1_dbg").inc(g.dma_start(E.dbg_yb[0, :, :], YB[:, :].bitcast(BF16)), 16)
                g.wait_ge(S("p1_gi").h, 1)
                S("p1_dbg").inc(g.dma_start(E.dbg_gi[0, :, :], GI[:, 0, :].bitcast(BF16)), 16)
            for pn in range(NPAN):
                for tt in range(NT):
                    i = pn * NT + tt
                    g.wait_ge(S("p1_h2e").h, i + 1)
                    S("p1_h2s").inc(g.dma_start(
                        bass.AP(E.qh2t_s, tt * 128 * FC * 128 + pn * 4 * 128,
                                [[FC * 128, 128], [128, 4], [1, 128]]),
                        QH2B[:, i % 4, :, :].bitcast(BF16)), 16)
            g.wait_ge(S("p1_h2s").h, 16 * NPAN * NT)


def _build_p2(nc, S, E, P):
    NQ = 4
    NW2 = 4
    ph = contextlib.ExitStack()
    sb = lambda n, sh, d: ph.enter_context(nc.sbuf_tensor(n, sh, d))
    ps = lambda n, sh, d: ph.enter_context(nc.psum_tensor(n, sh, d))
    W2P = sb("W2P", [128, FF], F32)
    U2 = sb("U2", [128, FF // 4], F32)
    V2 = sb("V2", [128, FF // 4], I16)
    Q2P = sb("Q2P", [128, 2, FF], I16)
    AM2 = sb("AM2", [128, 128], F32)
    PB2 = sb("PB2", [128, 128], I32)
    RS2 = sb("RS2", [128, 128], I32)
    PBB2 = sb("PBB2", [128, 128], BF16)
    RSB2 = sb("RSB2", [128, 128], BF16)
    QW2 = sb("QW2", [128, FC, 512], I16)
    BH2 = sb("BH2", [128, 2, 512], BF16)
    BL2 = sb("BL2", [128, 2, 512], BF16)
    B2F = sb("B2F", [1, 2, 512], F32)
    QH2L = sb("QH2L", [128, 2, FC, 128], I16)
    HB2 = sb("HB2", [128, 2, 512], I16)
    XFQ = sb("XFQ", [128, 4, 512], BF16)
    FFB = sb("FFB", [128, 4, 512], I16)
    Y2PS = ps("Y2PS", [128, 4, 512], F32)
    W2TR = ps("W2TR", [128, 16, 128], BF16)

    with ph, nc.Block() as blk:
        def w2_load(sync, idx):
            if idx >= 1:
                sync.wait_ge(S("p2_w2q").h, idx)
            S("p2_w2l").inc(sync.dma_start(
                W2P[:, :],
                E.w2[idx * 128:(idx + 1) * 128, :]), 16)

        chunksA = [(pw, qt) for pw in range(2) for qt in range(4)]
        cptA = (len(chunksA) + NT - 1) // NT
        loads_at = {0: 0, 4 // cptA: 1}

        @blk.sync
        def _(sync):
            for q in range(NQ):
                if q >= 2:
                    sync.wait_ge(S("p2_blo").h, q - 1)
                S("p2_b2l").inc(sync.dma_start(
                    B2F[0:1, q % 2, :], E.b2[0:1, q * 512:(q + 1) * 512]), 16)
                if q == 0:
                    for pw in range(NW2):
                        w2_load(sync, pw)
                else:
                    w2_load(sync, q * NW2 + 2)
                    w2_load(sync, q * NW2 + 3)
                for tt in range(NT):
                    i = q * NT + tt
                    if tt in loads_at and q < NQ - 1:
                        w2_load(sync, (q + 1) * NW2 + loads_at[tt])
                    if i >= 2:
                        sync.wait_ge(S("p2_mm").h, i - 1)
                    S("p2_sl").inc(sync.dma_start(
                        QH2L[:, tt % 2, :, :].bitcast(BF16),
                        bass.AP(E.qh2t_s, tt * 128 * FC * 128,
                                [[FC * 128, 128], [128, FC], [1, 128]])), 16)
                    if i >= 4:
                        sync.wait_ge(S("p2_ff").h, i - 3)
                    S("p2_xfl").inc(sync.dma_start(
                        XFQ[:, i % 4, :],
                        E.xffn_s[tt * 128:(tt + 1) * 128, q * 512:(q + 1) * 512]), 16)

        def w2_quarter(v, idx, qt):
            qw = FF // 4
            if qt == 0:
                v.wait_ge(S("p2_w2l").h, 16 * (idx + 1))
                if idx >= 2:
                    v.wait_ge(S("p2_w2t").h, (FC // 16) * (idx - 1))
            quant_chain(v, W2P[:, qt * qw:(qt + 1) * qw], qw,
                        AM2, PB2, RS2, PBB2, RSB2,
                        U2[:, 0:qw], V2[:, 0:qw].bitcast(BF16),
                        Q2P[:, idx % 2, qt * qw:(qt + 1) * qw].bitcast(BF16),
                        fp32_input=True)
            if qt == 3:
                S("p2_w2q").dinc(v)

        @blk.vector
        def _(v):
            v.memset(BH2[:, :, :], 0.0)
            v.memset(BL2[:, :, :], 0.0)
            S("p2_bz").dinc(v)
            chunksA = [(pw, qt) for pw in range(2) for qt in range(4)]
            cptA = (len(chunksA) + NT - 1) // NT
            for q in range(NQ):
                v.wait_ge(S("p2_b2l").h, 16 * (q + 1))
                if q >= 2:
                    v.wait_ge(S("p2_mm").h, (q - 1) * NT)
                v.tensor_copy(BH2[0:1, q % 2, :], B2F[0:1, q % 2, :])
                v.drain()
                v.tensor_tensor(BL2[0:1, q % 2, :], B2F[0:1, q % 2, :],
                                BH2[0:1, q % 2, :], OP.subtract)
                S("p2_blo").dinc(v)
                if q == 0:
                    for pw in range(NW2):
                        for qt in range(4):
                            w2_quarter(v, pw, qt)
                for tt in range(NT):
                    i = q * NT + tt
                    if q < NQ - 1:
                        for (pw, qt) in chunksA[tt * cptA:(tt + 1) * cptA]:
                            w2_quarter(v, (q + 1) * NW2 + pw, qt)
                    v.wait_ge(S("p2_hb").h, i + 1)
                    v.wait_ge(S("p2_xfl").h, 16 * (i + 1))
                    if i >= 4:
                        v.wait_ge(S("p2_ffs").h, 16 * (i - 3))
                    v.tensor_tensor(FFB[:, i % 4, :].bitcast(BF16),
                                    HB2[:, i % 2, :].bitcast(BF16), XFQ[:, i % 4, :], OP.add)
                    S("p2_ff").dinc(v)
                if q < NQ - 1:
                    for pw in (2, 3):
                        for qt in range(4):
                            w2_quarter(v, (q + 1) * NW2 + pw, qt)

        @blk.tensor
        def _(t):
            for q in range(NQ):
                for pw in range(NW2):
                    idx = q * NW2 + pw
                    t.wait_ge(S("p2_w2q").h, idx + 1)
                    for oc8 in range(FC // 16):
                        gg = (FC // 16) * idx + oc8
                        if gg >= 1:
                            t.wait_ge(S("p2_w2e").h, gg)
                        for c in range(16):
                            oc = oc8 * 16 + c
                            tr = t.transpose(W2TR[:, c, :],
                                             Q2P[:, idx % 2, oc * 128:(oc + 1) * 128].bitcast(BF16),
                                             P.IdB[:, :])
                        S("p2_w2t").inc(tr)
                for tt in range(NT):
                    i = q * NT + tt
                    t.wait_ge(S("p2_w2e").h, (FC // 16) * NW2 * (q + 1))
                    t.wait_ge(S("p2_sl").h, 16 * (i + 1))
                    if tt == 0:
                        t.wait_ge(S("p2_bz").h, 1)
                        t.wait_ge(S("p2_blo").h, q + 1)
                    if i >= 4:
                        t.wait_ge(S("p2_hb").h, i - 3)
                    for oc in range(FC):
                        mm = t.matmul(Y2PS[:, i % 4, :],
                                      QH2L[:, tt % 2, oc, :].bitcast(BF16),
                                      QW2[:, oc, :].bitcast(BF16),
                                      start=(oc == 0), stop=False)
                    t.matmul(Y2PS[:, i % 4, :], P.ONES1[:, :], BH2[:, q % 2, :],
                             start=False, stop=False)
                    mm = t.matmul(Y2PS[:, i % 4, :], P.ONES1[:, :], BL2[:, q % 2, :],
                                  start=False, stop=True)
                    S("p2_mm").inc(mm)

        @blk.scalar
        def _(s):
            for q in range(NQ):
                for pw in range(NW2):
                    idx = q * NW2 + pw
                    for oc8 in range(FC // 16):
                        gg = (FC // 16) * idx + oc8
                        s.wait_ge(S("p2_w2t").h, gg + 1)
                        if q >= 1 and pw == 0 and oc8 == 0:
                            s.wait_ge(S("p2_mm").h, q * NT)
                        s.activation(
                            QW2[:, oc8 * 16:(oc8 + 1) * 16, pw * 128:(pw + 1) * 128].bitcast(BF16),
                            W2TR[:, :, :], AF.Identity, bias=f32(0.0), scale=f32(1.0))
                        S("p2_w2e").dinc(s)
                for tt in range(NT):
                    i = q * NT + tt
                    s.wait_ge(S("p2_mm").h, i + 1)
                    if i >= 2:
                        s.wait_ge(S("p2_ff").h, i - 1)
                    s.activation(HB2[:, i % 2, :].bitcast(BF16), Y2PS[:, i % 4, :],
                                 AF.Identity, bias=f32(0.0), scale=f32(1.0))
                    S("p2_hb").dinc(s)

        @blk.gpsimd
        def _(g):
            for q in range(NQ):
                for tt in range(NT):
                    i = q * NT + tt
                    g.wait_ge(S("p2_ff").h, i + 1)
                    S("p2_ffs").inc(g.dma_start(
                        E.ffn_s[tt * 128:(tt + 1) * 128, q * 512:(q + 1) * 512],
                        FFB[:, i % 4, :].bitcast(BF16)), 16)
            g.wait_ge(S("p2_ffs").h, 16 * NQ * NT)


def _build_qkv(nc, S, E, P, hf, XTh, QTh, KTh, VTh, WQP, BOF):
    TPH = NT // 2
    tt0 = hf * TPH
    PW = max(512, T // 2)
    with nc.psum_tensor(f"QKVPS{hf}", [128, 2, PW], F32) as QKVPS, nc.Block() as blk:
        @blk.sync
        def _(sync):
            if hf == 0:
                S("p3_bo").inc(sync.dma_start(BOF[0:1, :], E.bo[0:1, :]), 16)
            for ti in range(TPH):
                S("p3_xtl").inc(sync.dma_start(
                    XTh[:, ti, :, :],
                    bass.AP(E.xt_s, (tt0 + ti) * 128 * KC * 128,
                            [[KC * 128, 128], [128, KC], [1, 128]])), 16)
        @blk.gpsimd
        def _(g):
            for wi in range(3):
                wd = (E.wq, E.wk, E.wv)[wi]
                for oc in range(KC):
                    li = wi * KC + oc
                    gi = hf * 48 + li
                    if li >= 2:
                        g.wait_ge(S("p3_qkv").h, gi - 1)
                    S("p3_wl").inc(g.dma_start(
                        WQP[:, li % 2, :, :],
                        wd[:, oc * 128:(oc + 1) * 128]
                        .rearrange("(c p) o -> p c o", p=128)), 16)

        @blk.tensor
        def _(t):
            t.wait_ge(S("p3_xtl").h, S("p3_xtl").n)
            for wi in range(3):
                for oc in range(KC):
                    li = wi * KC + oc
                    gi = hf * 48 + li
                    t.wait_ge(S("p3_wl").h, 16 * (gi + 1))
                    if li >= 2:
                        t.wait_ge(S("p3_ev").h, gi - 1)
                    hw = TPH * 64
                    for half in range(2):
                        for kc in range(KC):
                            mm = t.matmul(
                                QKVPS[:, li % 2, half * hw:(half + 1) * hw],
                                WQP[:, li % 2, kc, :],
                                XTh[:, (TPH // 2) * half:(TPH // 2) * (half + 1), kc, :],
                                start=(kc == 0), stop=(kc == KC - 1))
                    S("p3_qkv").inc(mm)

        @blk.scalar
        def _(s):
            for wi in range(3):
                dst = (QTh, KTh, VTh)[wi]
                bias = (P.BQ, P.BK, P.BV)[wi]
                for oc in range(KC):
                    li = wi * KC + oc
                    gi = hf * 48 + li
                    s.wait_ge(S("p3_qkv").h, gi + 1)
                    s.activation(
                        dst[:, oc, :], QKVPS[:, li % 2, 0:T // 2],
                        AF.Identity, bias=bias[:, oc:oc + 1], scale=f32(1.0))
                    S("p3_ev").dinc(s)


def _build_mix(nc, S, P, hf, QTh, KTh, VTh, CTh, LQ, LK, LG,
               ESb, EXb, RMX, RSM, WBt, WTs, VGs):
    NBT = (T // 2) // 32
    ph = contextlib.ExitStack()
    ps = lambda n, sh, d: ph.enter_context(nc.psum_tensor(n, sh, d))
    SCPS = ps(f"SCPS{hf}", [128, 4, 128], F32)
    WTPS = ps(f"WTPS{hf}", [128, 4, 128], BF16)
    VGPS = ps(f"VGPS{hf}", [128, 4, 128], BF16)
    CXPS = ps(f"CXPS{hf}", [128, 4, 128], F32)
    with ph, nc.Block() as blk:
        @blk.vector
        def _(v):
            for bt in range(NBT):
                gb = hf * NBT + bt
                t0 = bt * 32
                if gb >= 1:
                    v.wait_ge(S("mx_sc").h, gb)
                    v.wait_ge(S("mx_vg").h, gb)
                for (Lx, Src) in ((LQ, QTh), (LK, KTh), (LG, VTh)):
                    v.tensor_copy(
                        Lx[:, :, :, :],
                        Src[:, :, t0:t0 + 32].rearrange("p h (j ti) -> p j ti h", ti=8))
                S("mx_g").dinc(v)
                v.wait_ge(S("mx_sc").h, gb + 1)
                v.scalar_tensor_tensor(
                    ESb[:, :, :], SCPS[:, :, :], 1.0,
                    P.MASK[:, None, :].broadcast_to([128, 4, 128]), OP.mult, OP.add)
                S("mx_e0").dinc(v)
                v.tensor_reduce(RMX[:, :], ESb[:, :, :], axis=AX.X, op=OP.max)
                v.drain()
                v.tensor_tensor(ESb[:, :, :], ESb[:, :, :],
                                RMX[:, :, None].broadcast_to([128, 4, 128]), OP.subtract)
                S("mx_e0").dinc(v)
                v.wait_ge(S("mx_ex").h, gb + 1)
                v.tensor_reduce(RSM[:, :], EXb[:, :, :], axis=AX.X, op=OP.add)
                v.drain()
                v.reciprocal(RSM[:, :], RSM[:, :])
                if gb >= 1:
                    v.wait_ge(S("mx_wt").h, gb)
                v.drain()
                v.tensor_tensor(WBt[:, :, :], EXb[:, :, :],
                                RSM[:, :, None].broadcast_to([128, 4, 128]), OP.mult)
                S("mx_wb").dinc(v)

        @blk.tensor
        def _(t):
            for bt in range(NBT):
                gb = hf * NBT + bt
                t.wait_ge(S("mx_g").h, gb + 1)
                if gb >= 1:
                    t.wait_ge(S("mx_e0").h, 2 * gb)
                for j in range(4):
                    mm = t.matmul(SCPS[:, j, :],
                                  LQ[:, j, :, :].rearrange("p a b -> p (a b)"),
                                  LK[:, j, :, :].rearrange("p a b -> p (a b)"),
                                  start=True, stop=True)
                S("mx_sc").inc(mm)
                t.wait_ge(S("mx_wb").h, gb + 1)
                if gb >= 1:
                    t.wait_ge(S("mx_wte").h, gb)
                for j in range(4):
                    tr = t.transpose(WTPS[:, j, :], WBt[:, j, :], P.IdB[:, :])
                S("mx_wt").inc(tr)
                if gb >= 1:
                    t.wait_ge(S("mx_vge").h, gb)
                for j in range(4):
                    tr = t.transpose(VGPS[:, j, :],
                                     LG[:, j, :, :].rearrange("p a b -> p (a b)"), P.IdB[:, :])
                S("mx_vg").inc(tr)
                t.wait_ge(S("mx_wte").h, gb + 1)
                t.wait_ge(S("mx_vge").h, gb + 1)
                if gb >= 1:
                    t.wait_ge(S("mx_ct").h, gb)
                for j in range(4):
                    mm = t.matmul(CXPS[:, j, :], VGs[:, j, :], WTs[:, j, :],
                                  start=True, stop=True)
                S("mx_cx").inc(mm)

        @blk.scalar
        def _(s):
            for bt in range(NBT):
                gb = hf * NBT + bt
                t0 = bt * 32
                s.wait_ge(S("mx_e0").h, 2 * gb + 2)
                s.activation(EXb[:, :, :], ESb[:, :, :], AF.Exp, scale=f32(RSQD))
                S("mx_ex").dinc(s)
                s.wait_ge(S("mx_wt").h, gb + 1)
                s.activation(WTs[:, :, :], WTPS[:, :, :], AF.Identity,
                             bias=f32(0.0), scale=f32(1.0))
                S("mx_wte").dinc(s)
                s.wait_ge(S("mx_vg").h, gb + 1)
                s.activation(VGs[:, :, :], VGPS[:, :, :], AF.Identity,
                             bias=f32(0.0), scale=f32(1.0))
                S("mx_vge").dinc(s)
                s.wait_ge(S("mx_cx").h, gb + 1)
                s.activation(
                    CTh[:, :, t0:t0 + 32].rearrange("p h (j ti) -> p j ti h", ti=8),
                    CXPS[:, :, :].rearrange("p j (ti h) -> p j ti h", h=16),
                    AF.Identity, bias=f32(0.0), scale=f32(1.0))
                S("mx_ct").dinc(s)


_WO_CALL = [0]


def _build_wo(nc, S, E, P, hf, CTh, BOX):
    BOF, BOH, BOL = BOX
    TPH = NT // 2
    tt0 = hf * TPH
    call = _WO_CALL[0]
    _WO_CALL[0] += 1
    ph = contextlib.ExitStack()
    sb = lambda n, sh, d: ph.enter_context(nc.sbuf_tensor(n, sh, d))
    WOB = sb(f"WOB{call}", [128, KC, D], BF16)
    XR = sb(f"XR{call}", [128, 2, D], F32)
    FR = sb(f"FR{call}", [128, 2, D], BF16)
    OB = sb(f"OB{call}", [128, 2, D], F32)
    OPS = ph.enter_context(nc.psum_tensor(f"OPS{call}", [128, 2, D], F32))
    with ph, nc.Block() as blk:
        @blk.gpsimd
        def _(g):
            for kc in range(KC):
                S("p4_wol").inc(g.dma_start(
                    WOB[:, kc, :], E.wo[kc * 128:(kc + 1) * 128, :]), 16)

        @blk.sync
        def _(sync):
            for ti in range(TPH):
                i = call * TPH + ti
                tt = tt0 + ti
                if ti >= 2:
                    sync.wait_ge(S("p4_oev").h, 3 * (i - 1))
                S("p4_xrl").inc(sync.dma_start(
                    XR[:, ti % 2, :], E.x[tt * 128:(tt + 1) * 128, :]), 16)
                S("p4_frl").inc(sync.dma_start(
                    FR[:, ti % 2, :], E.ffn_s[tt * 128:(tt + 1) * 128, :]), 16)

        @blk.tensor
        def _(t):
            t.wait_ge(S("p4_wol").h, 16 * KC * (call + 1))
            t.wait_ge(S("p4_bo2").h, 1)
            for ti in range(TPH):
                i = call * TPH + ti
                if ti >= 2:
                    t.wait_ge(S("p4_oev").h, 3 * (i - 1))
                for kc in range(KC):
                    for oc4 in range(4):
                        mm = t.matmul(OPS[:, ti % 2, oc4 * 512:(oc4 + 1) * 512],
                                      CTh[:, kc, ti * 128:(ti + 1) * 128],
                                      WOB[:, kc, oc4 * 512:(oc4 + 1) * 512],
                                      start=(kc == 0), stop=False)
                for oc4 in range(4):
                    t.matmul(OPS[:, ti % 2, oc4 * 512:(oc4 + 1) * 512], P.ONES1[:, :],
                             BOH[:, oc4 * 512:(oc4 + 1) * 512], start=False, stop=False)
                    mm = t.matmul(OPS[:, ti % 2, oc4 * 512:(oc4 + 1) * 512], P.ONES1[:, :],
                                  BOL[:, oc4 * 512:(oc4 + 1) * 512], start=False, stop=True)
                S("p4_omm").inc(mm)

        @blk.vector
        def _(v):
            if call == 0:
                v.wait_ge(S("p3_bo").h, 16)
                v.memset(BOH[:, :], 0.0)
                v.memset(BOL[:, :], 0.0)
                v.drain()
                v.tensor_copy(BOH[0:1, :], BOF[0:1, :])
                v.drain()
                v.tensor_tensor(BOL[0:1, :], BOF[0:1, :], BOH[0:1, :], OP.subtract)
                S("p4_bo2").dinc(v)
            for ti in range(TPH):
                i = call * TPH + ti
                v.wait_ge(S("p4_omm").h, i + 1)
                v.wait_ge(S("p4_xrl").h, 16 * (i + 1))
                v.wait_ge(S("p4_frl").h, 16 * (i + 1))
                if i >= 2:
                    v.wait_ge(S("p4_ost").h, 16 * (i - 1))
                v.tensor_tensor(
                    OB[:, ti % 2, :], OPS[:, ti % 2, :], XR[:, ti % 2, :], OP.add)
                S("p4_oev").dinc(v)
                S("p4_oev").dinc(v)
                v.tensor_tensor(
                    OB[:, ti % 2, :], OB[:, ti % 2, :], FR[:, ti % 2, :], OP.add)
                S("p4_oev").dinc(v)

        @blk.scalar
        def _(s):
            for ti in range(TPH):
                i = call * TPH + ti
                tt = tt0 + ti
                s.wait_ge(S("p4_oev").h, 3 * (i + 1))
                S("p4_ost").inc(s.dma_start(
                    E.out[tt * 128:(tt + 1) * 128, :], OB[:, ti % 2, :]), 16)
            if call == 1:
                s.wait_ge(S("p4_ost").h, 16 * NT)


def _build_p34(nc, S, E, P):
    outer = contextlib.ExitStack()
    CTh = outer.enter_context(nc.sbuf_tensor("CTh", [128, KC, T // 2], BF16))
    BOH = outer.enter_context(nc.sbuf_tensor("BOH", [128, D], BF16))
    BOL = outer.enter_context(nc.sbuf_tensor("BOL", [128, D], BF16))
    BOF = outer.enter_context(nc.sbuf_tensor("BOF", [1, D], F32))
    with outer:
        ph = contextlib.ExitStack()
        sb = lambda n, sh, d: ph.enter_context(nc.sbuf_tensor(n, sh, d))
        XTh = sb("XTh", [128, NT // 2, KC, 128], BF16)
        QTh = sb("QTh", [128, KC, T // 2], BF16)
        KTh = sb("KTh", [128, KC, T // 2], BF16)
        VTh = sb("VTh", [128, KC, T // 2], BF16)
        WQP = sb("WQP", [128, 2, KC, 128], BF16)
        LQ = sb("LQ", [128, 4, 8, 16], BF16)
        LK = sb("LK", [128, 4, 8, 16], BF16)
        LG = sb("LG", [128, 4, 8, 16], BF16)
        ESb = sb("ESb", [128, 4, 128], F32)
        EXb = sb("EXb", [128, 4, 128], F32)
        RMX = sb("RMX", [128, 4], F32)
        RSM = sb("RSM", [128, 4], F32)
        WBt = sb("WBt", [128, 4, 128], BF16)
        WTs = sb("WTs", [128, 4, 128], BF16)
        VGs = sb("VGs", [128, 4, 128], BF16)
        with ph:
            for hf in range(2):
                _build_qkv(nc, S, E, P, hf, XTh, QTh, KTh, VTh, WQP, BOF)
                _build_mix(nc, S, P, hf, QTh, KTh, VTh, CTh, LQ, LK, LG,
                           ESb, EXb, RMX, RSM, WBt, WTs, VGs)
                if hf == 0:
                    with nc.Block() as blk:
                        @blk.gpsimd
                        def _(g):
                            S("ct_sp").inc(g.dma_start(
                                bass.AP(E.ct_s, 0, [[KC * (T // 2), 128], [T // 2, KC], [1, T // 2]]),
                                CTh[:, :, :]), 16)
                            g.wait_ge(S("ct_sp").h, 16)
        _build_wo(nc, S, E, P, 1, CTh, (BOF, BOH, BOL))
        with nc.Block() as blk:
            @blk.gpsimd
            def _(g):
                S("ct_sp").inc(g.dma_start(
                    CTh[:, :, :],
                    bass.AP(E.ct_s, 0, [[KC * (T // 2), 128], [T // 2, KC], [1, T // 2]])), 16)
                g.wait_ge(S("ct_sp").h, 32)
        _build_wo(nc, S, E, P, 0, CTh, (BOF, BOH, BOL))


class _NS:
    pass


def build_program():
    nc = bass.Bass("TRN2", target_bir_lowering=False, detect_race_conditions=False)
    E = _NS()
    E.x = nc.dram_tensor("x", [T, D], F32, kind="ExternalInput")
    E.wq = nc.dram_tensor("wq", [D, D], F32, kind="ExternalInput")
    E.wk = nc.dram_tensor("wk", [D, D], F32, kind="ExternalInput")
    E.wv = nc.dram_tensor("wv", [D, D], F32, kind="ExternalInput")
    E.wo = nc.dram_tensor("wo", [D, D], F32, kind="ExternalInput")
    E.bq = nc.dram_tensor("bq", [1, D], F32, kind="ExternalInput")
    E.bk = nc.dram_tensor("bk", [1, D], F32, kind="ExternalInput")
    E.bv = nc.dram_tensor("bv", [1, D], F32, kind="ExternalInput")
    E.bo = nc.dram_tensor("bo", [1, D], F32, kind="ExternalInput")
    E.g2 = nc.dram_tensor("g2", [1, D], F32, kind="ExternalInput")
    E.be2 = nc.dram_tensor("be2", [1, D], F32, kind="ExternalInput")
    E.lnw = nc.dram_tensor("lnw", [1, D], F32, kind="ExternalInput")
    E.lnb = nc.dram_tensor("lnb", [1, D], F32, kind="ExternalInput")
    E.w1 = nc.dram_tensor("w1", [FF, D], F32, kind="ExternalInput")
    E.b1 = nc.dram_tensor("b1", [1, FF], F32, kind="ExternalInput")
    E.w2 = nc.dram_tensor("w2", [D, FF], F32, kind="ExternalInput")
    E.b2 = nc.dram_tensor("b2", [1, D], F32, kind="ExternalInput")
    E.out = nc.dram_tensor("out", [T, D], F32, kind="ExternalOutput")
    kw = {"kind": "ExternalOutput"} if DEBUG_SCRATCH else {}
    E.xt_s = nc.dram_tensor("xt_s", [NT, 128, KC, 128], BF16, **kw)
    E.xffn_s = nc.dram_tensor("xffn_s", [T, D], BF16, **kw)
    E.qh2t_s = nc.dram_tensor("qh2t_s", [NT, 128, FC, 128], BF16, **kw)
    E.ffn_s = nc.dram_tensor("ffn_s", [T, D], BF16, **kw)
    E.ct_s = nc.dram_tensor("ct_s", [128, KC, T // 2], BF16, **kw)
    if DEBUG_SCRATCH:
        E.dbg_qht = nc.dram_tensor("dbg_qht", [128, KC, T], BF16, kind="ExternalOutput")
        E.dbg_qhi = nc.dram_tensor("dbg_qhi", [128, D], BF16, kind="ExternalOutput")
        E.dbg_yb = nc.dram_tensor("dbg_yb", [NT, 128, 512], BF16, kind="ExternalOutput")
        E.dbg_gi = nc.dram_tensor("dbg_gi", [NT, 128, 512], BF16, kind="ExternalOutput")
    else:
        E.dbg_qht = E.dbg_qhi = E.dbg_yb = E.dbg_gi = None

    persist = contextlib.ExitStack()
    sbp = lambda n, sh, d: persist.enter_context(nc.sbuf_tensor(n, sh, d))
    P = _NS()
    P.IdF = sbp("IdF", [128, 128], F32)
    P.IdB = sbp("IdB", [128, 128], BF16)
    P.MASK = sbp("MASK", [128, 128], F32)
    P.BQ = sbp("BQ", [128, KC], F32)
    P.BK = sbp("BK", [128, KC], F32)
    P.BV = sbp("BV", [128, KC], F32)
    P.G2C = sbp("G2C", [128, KC], F32)
    P.B2C = sbp("B2C", [128, KC], F32)
    P.ONES1 = sbp("ONES1", [128, 128], BF16)

    sems = {}

    def S(name):
        if name not in sems:
            sems[name] = Sem(nc, persist, name)
        return sems[name]

    _build_init(nc, S, E, P)
    with contextlib.ExitStack() as s01:
        QHT = s01.enter_context(nc.sbuf_tensor("QHT", [128, KC, T], I16))
        _build_p0(nc, S, E, P, QHT)
        if BUILD_PHASES == "all":
            _build_p1(nc, S, E, P, QHT)
    if BUILD_PHASES == "all":
        _build_p2(nc, S, E, P)
        _build_p34(nc, S, E, P)
    return nc, persist


_CACHE = {}


def prepare_maps(inputs):
    x = np.ascontiguousarray(np.asarray(inputs["L0_attn_x"], dtype=np.float32).reshape(16384, D))
    w = {
        "wq": inputs["L0_attn_Wq"], "wk": inputs["L0_attn_Wk"],
        "wv": inputs["L0_attn_Wv"], "wo": inputs["L0_attn_Wo"],
        "bq": np.asarray(inputs["L0_attn_bq"]).reshape(1, D),
        "bk": np.asarray(inputs["L0_attn_bk"]).reshape(1, D),
        "bv": np.asarray(inputs["L0_attn_bv"]).reshape(1, D),
        "bo": np.asarray(inputs["L0_attn_bo"]).reshape(1, D),
        "g2": np.asarray(inputs["L0_attn_ln_g"]).reshape(1, D),
        "be2": np.asarray(inputs["L0_attn_ln_b"]).reshape(1, D),
        "lnw": np.asarray(inputs["L0_ffn_ln_w"]).reshape(1, D),
        "lnb": np.asarray(inputs["L0_ffn_ln_b"]).reshape(1, D),
        "w1": inputs["L0_ffn_W1"], "b1": np.asarray(inputs["L0_ffn_b1"]).reshape(1, FF),
        "w2": inputs["L0_ffn_W2"], "b2": np.asarray(inputs["L0_ffn_b2"]).reshape(1, D),
    }
    w = {k: np.ascontiguousarray(np.asarray(v, dtype=np.float32)) for k, v in w.items()}
    in_maps = []
    for c in range(NCORES):
        m = dict(w)
        m["x"] = np.ascontiguousarray(x[c * T:(c + 1) * T, :])
        in_maps.append(m)
    return in_maps


def kernel(**inputs):
    in_maps = prepare_maps(inputs)
    if "nc" not in _CACHE:
        _CACHE["nc"], _CACHE["persist"] = build_program()
    nc = _CACHE["nc"]
    res = run_bass_kernel_spmd(nc, in_maps, core_ids=list(range(NCORES)))
    out = np.concatenate([np.asarray(r["out"]) for r in res.results], axis=0)
    return out.reshape(4, 4096, 2048)
